# version 03
# baseline (speedup 1.0000x reference)
import numpy as np
import ml_dtypes
import concourse.bass as bass
import concourse.mybir as mybir
from concourse.bass_utils import run_bass_kernel_spmd
from contextlib import ExitStack

F32 = mybir.dt.float32
BF16 = mybir.dt.bfloat16
I32 = mybir.dt.int32
AF = mybir.ActivationFunctionType
ALU = mybir.AluOpType
AX = mybir.AxisListType
BF = ml_dtypes.bfloat16

C = 1024
H = 16
NT = 16
SCALE = 1.0 / 64.0
BOOT = 64


def build_program(nb=4, nt=NT, stop_stage=None, detect_races=False,
                  debug_stats=False, debug_tail=False):
    nc = bass.Bass("TRN2", target_bir_lowering=False, detect_race_conditions=detect_races)
    seq = nt * 128

    x_h = nc.dram_tensor("x", [nb, seq, C], F32, kind="ExternalInput").ap()
    w_h = {nm: nc.dram_tensor(nm, [C, C], F32, kind="ExternalInput").ap()
           for nm in ("wq", "wk0", "wk1", "wv0", "wv1", "wo")}
    wffn_h = nc.dram_tensor("wffn", [C, C], F32, kind="ExternalInput").ap()
    rows_h = nc.dram_tensor("brows", [8, C], F32, kind="ExternalInput").ap()
    bq_fm_h = nc.dram_tensor("bq_fm", [128, 8], F32, kind="ExternalInput").ap()
    g_fm_h = nc.dram_tensor("g_fm", [128, 8], F32, kind="ExternalInput").ap()
    bcol_h = nc.dram_tensor("b_colb", [128, 8], BF16, kind="ExternalInput").ap()
    eyef_h = nc.dram_tensor("eyef", [128, 128], F32, kind="ExternalInput").ap()
    eyeb_h = nc.dram_tensor("eyeb", [128, 128], BF16, kind="ExternalInput").ap()
    onesb_h = nc.dram_tensor("onesb", [3, 128], BF16, kind="ExternalInput").ap()
    onesf_h = nc.dram_tensor("onesf", [1, 128], F32, kind="ExternalInput").ap()
    onescol_h = nc.dram_tensor("onescol", [128, 1], BF16, kind="ExternalInput").ap()
    csts_h = nc.dram_tensor("csts", [128, 8], F32, kind="ExternalInput").ap()
    y_h = nc.dram_tensor("y", [nb, seq, C], F32, kind="ExternalOutput").ap()
    qscr = nc.dram_tensor("qscr", [nt, 128, 8 * 128], BF16).ap()
    dbg_h = (nc.dram_tensor("dbg", [nb, nt, 128, 256], F32, kind="ExternalOutput").ap()
             if debug_stats else None)
    dbgq_h = (nc.dram_tensor("dbgq", [nb, nt, 128, 1024], BF16, kind="ExternalOutput").ap()
              if debug_stats else None)
    dbgs_h = (nc.dram_tensor("dbgs", [nb, nt, 128, 1024], BF16, kind="ExternalOutput").ap()
              if debug_stats else None)
    dbgv_h = (nc.dram_tensor("dbgv", [nb * (nt // 4 + 1), 2, 128, 2048], BF16,
                             kind="ExternalOutput").ap() if debug_stats else None)
    dbgx_h = (nc.dram_tensor("dbgx", [nb * (nt // 4 + 1), 128, 4096], BF16,
                             kind="ExternalOutput").ap() if debug_stats else None)
    dbgl_h = (nc.dram_tensor("dbgl", [nb, nt, 128, 2], F32,
                             kind="ExternalOutput").ap() if debug_stats else None)
    xdscr = nc.dram_tensor("xdscr", [nt, 128, 8 * 128], BF16).ap()
    tailf_h = (nc.dram_tensor("tailf", [8, 128, 1024], F32, kind="ExternalOutput").ap()
               if debug_tail else None)
    tailb_h = (nc.dram_tensor("tailb", [15, 128, 1024], BF16, kind="ExternalOutput").ap()
               if debug_tail else None)
    tailp_h = (nc.dram_tensor("tailp", [3, 128, 1024], F32, kind="ExternalOutput").ap()
               if debug_tail else None)
    tailq_h = (nc.dram_tensor("tailq", [2, nt, 128, 1024], BF16, kind="ExternalOutput").ap()
               if debug_tail else None)

    stk = ExitStack()
    sbt = lambda n, s, d: stk.enter_context(nc.sbuf_tensor(n, s, d))
    psa = lambda n, s: stk.enter_context(nc.psum_tensor(n, s, F32))
    mksem = lambda n: stk.enter_context(nc.semaphore(n))

    slab = {nm: sbt(nm + "_s", [128, 8 * C], BF16) for nm in
            ("wq", "wk0", "wk1", "wv0", "wv1", "wo", "wdt")}
    rows3 = sbt("rows3", [3, C], BF16)
    bq_fm = sbt("bqfm", [128, 8], F32)
    g_fm = sbt("gfm", [128, 8], F32)
    bcol_bf = sbt("bcolb", [128, 8], BF16)
    eyef = sbt("eyefs", [128, 128], F32)
    eyeb = sbt("eyebs", [128, 128], BF16)
    onesb = sbt("onesbs", [3, 128], BF16)
    onesf = sbt("onesfs", [1, 128], F32)
    onescol = sbt("onescols", [128, 1], BF16)
    csts = sbt("cstss", [128, 8], F32)
    krow = sbt("krow", [1, 4 * C], BF16)

    xtile = [sbt(f"xt{i}", [128, C], F32) for i in range(2)]
    xn = sbt("xn", [128, C], F32)
    xnT = sbt("xnT", [128, 4 * 8 * 128], BF16)
    ks_sb = {k: sbt("ks_" + k, [128, C], BF16) for k in ("k0", "k1", "v0", "v1")}
    qt = sbt("q_t", [128, C], F32)
    qa = sbt("q_a", [128, C], F32)
    qb = sbt("q_b", [128, C], F32)
    qw = sbt("q_w", [128, C], F32)
    qit = sbt("q_it", [128, C], I32)
    gmax = sbt("gmaxs", [128, 32], F32)
    s0 = sbt("s0s", [128, 32], F32)
    rcp = sbt("rcps", [128, 32], F32)
    rs_t = sbt("rss", [128, 32], F32)
    ss_t = sbt("sss", [128, 32], F32)
    sq_dump = sbt("sqdump", [128, C], BF16)
    ln1 = {n: sbt("l_" + n, [128, 1], F32) for n in
           ("sx", "sxx", "mneg", "m2", "u", "sig", "rinv", "rstd", "mnegA")}
    state_sb = sbt("state", [128, 8 * 128], BF16)
    srs_sb = sbt("srs", [128, 8], F32)
    srs_bf = sbt("srsbf", [128, 8], BF16)
    oddst = sbt("oddst", [64, 8 * 64], BF16)
    qstage = [sbt(f"qstage{i}", [128, 8 * 128], BF16) for i in range(2)]
    xdstage = [sbt(f"xdstage{i}", [128, 8 * 128], BF16) for i in range(2)]
    xdTout = sbt("xdTout", [128, 8 * 128], BF16)
    sq_sb = sbt("sqsb", [128, 8 * 128], BF16)
    ln_sb = sbt("lnsb", [128, 8 * 128], BF16)
    mrow = sbt("mrow", [1, 128], F32)
    mrow_bf = sbt("mrowbf", [1, 128], BF16)
    rrow = sbt("rrow", [1, 128], BF16)
    vrow = sbt("vrow", [1, 128], F32)
    m2row = sbt("m2row", [1, 128], F32)
    rqrow = sbt("rqrow", [1, 128], F32)
    mtile2 = sbt("mtile2", [128, 256], F32)
    out_sb = [sbt(f"outsb{i}", [128, C], F32) for i in range(2)]
    qevac = sbt("qevac", [128, 4 * 512], BF16)
    wfstage = out_sb
    kst = out_sb[0]

    pA = psa("pA", [128, C])
    pB = psa("pB", [128, C])
    pK0 = psa("pK0", [128, C])
    pK1 = psa("pK1", [128, C])

    semnames = ("x0", "x1", "p0", "p1", "wf0", "wf1", "wld", "sodd",
                "sxd", "sq", "so0", "so1",
                "dv", "ac", "dvt", "act_t", "mm", "mme", "free", "gp", "done")
    S = {n: mksem("s_" + n) for n in semnames}
    cnt = {n: 0 for n in semnames}
    events = []
    stages = {}

    def ev(engine, fn):
        events.append((engine, fn))

    def bump(name, by=1):
        cnt[name] += by
        return cnt[name]

    def val(name):
        return cnt[name]

    def W(e, gate):
        if gate is not None and gate[1] > 0:
            e.wait_ge(S[gate[0]], gate[1])

    def f_sync0(e):
        for s in S.values():
            e.sem_clear(s)
        e.sem_inc(S["done"], BOOT)
        for dst, src in ((bq_fm, bq_fm_h), (g_fm, g_fm_h),
                         (bcol_bf, bcol_h), (eyef, eyef_h), (eyeb, eyeb_h),
                         (onesb, onesb_h), (onesf, onesf_h),
                         (onescol, onescol_h), (csts, csts_h)):
            e.dma_start(out=dst[:], in_=src[:]).then_inc(S["wld"], 16)
    ev("sync", f_sync0)
    bump("done", BOOT)
    n_consts = bump("wld", 144)

    def f_gp0(e):
        e.wait_ge(S["done"], BOOT)
        for nm in ("wk0", "wv0", "wk1", "wv1", "wq", "wo"):
            src = bass.AP(w_h[nm].tensor, 0, [[C, 128], [128 * C, 8], [1, C]])
            e.dma_start(out=slab[nm][:], in_=src).then_inc(S["gp"], 16)
        e.dma_start(out=rows3[1:2, :],
                    in_=bass.AP(rows_h.tensor, 5 * C, [[C, 1], [1, C]])).then_inc(S["gp"], 16)
        e.dma_start(out=rows3[2:3, :],
                    in_=bass.AP(rows_h.tensor, 6 * C, [[C, 1], [1, C]])).then_inc(S["gp"], 16)
        for i, r in enumerate((1, 3, 2, 4)):
            e.dma_start(out=krow[0:1, i * C:(i + 1) * C],
                        in_=bass.AP(rows_h.tensor, r * C,
                                    [[C, 1], [1, C]])).then_inc(S["gp"], 16)
    ev("gpsimd", f_gp0)
    n_wslabs = bump("gp", 96)
    n_rowsbf = bump("gp", 96)

    def f_dve0(e):
        e.wait_ge(S["done"], BOOT)
        e.wait_ge(S["wld"], n_consts)
        e.wait_ge(S["gp"], n_rowsbf)
        for nm in ("wq", "wk0", "wk1", "wv0", "wv1"):
            for ct in range(8):
                d0 = ct * C + ct * 128
                e.tensor_tensor(slab[nm][:, d0:d0 + 128],
                                slab[nm][:, d0:d0 + 128], eyeb[:], ALU.add)
        e.drain().then_inc(S["dvt"], 1)
    ev("vector", f_dve0)
    n_wplusI = bump("dvt")

    def f_t0(e):
        e.wait_ge(S["done"], BOOT)
        e.wait_ge(S["gp"], n_rowsbf)
        e.wait_ge(S["dvt"], n_wplusI)
        for j in range(2):
            for ct in range(8):
                e.matmul(pA[0:1, j * 512:(j + 1) * 512], bcol_bf[:, ct:ct + 1],
                         slab["wo"][:, ct * C + j * 512: ct * C + (j + 1) * 512],
                         start=(ct == 0), stop=(ct == 7))
        e.drain().then_inc(S["mm"], 1)
    ev("tensor", f_t0)
    n_bwo_mm = bump("mm")

    def f_a0(e):
        e.wait_ge(S["done"], BOOT)
        e.wait_ge(S["wld"], n_consts)
        e.wait_ge(S["gp"], n_rowsbf)
        e.wait_ge(S["mm"], n_bwo_mm)
        e.activation(rows3[0:1, :], pA[0:1, :], AF.Identity)
        e.drain().then_inc(S["act_t"], 1)
    ev("scalar", f_a0)
    gA = ("act_t", bump("act_t"))

    def f_awo(e, g=gA):
        e.wait_ge(S[g[0]], g[1])
        for ct in range(8):
            e.activation(slab["wo"][:, ct * C:(ct + 1) * C],
                         slab["wo"][:, ct * C:(ct + 1) * C],
                         AF.Identity, bias=0.0, scale=g_fm[:, ct:ct + 1])
        e.drain().then_inc(S["ac"], 1)
    ev("scalar", f_awo)
    n_wo_ready = bump("ac")

    gB = ("free", 0)

    def emit_scale_chain():
        n_gm = bump("dv")

        def f_ac(e, n=n_gm):
            e.wait_ge(S["dv"], n)
            e.activation(s0[:], gmax[:], AF.Identity,
                         bias=csts[:, 1:2], scale=csts[:, 0:1])
            e.drain().then_inc(S["ac"], 1)
        ev("scalar", f_ac)
        n_s0 = bump("ac")

        def f_dr(e, n=n_s0):
            e.wait_ge(S["ac"], n)
            e.reciprocal(rcp[:], s0[:])
            e.drain().then_inc(S["dv"], 1)
        ev("vector", f_dr)
        n_rc = bump("dv")

        def f_ac2(e, n=n_rc):
            e.wait_ge(S["dv"], n)
            e.activation(rs_t[:], rcp[:], AF.Identity, bias=0.0, scale=csts[:, 2:3])
            e.activation(ss_t[:], s0[:], AF.Identity, bias=0.0, scale=csts[:, 3:4])
            e.drain().then_inc(S["ac"], 1)
        ev("scalar", f_ac2)
        return bump("ac")

    def emit_quant_dve(e, src):
        g3 = lambda ap: ap.rearrange("p (g j) -> p g j", j=32)
        rs_b = rs_t[:].unsqueeze(2).broadcast_to([128, 32, 32])
        ss_b = ss_t[:].unsqueeze(2).broadcast_to([128, 32, 32])
        e.tensor_tensor(g3(qt[:]), g3(src[:]), rs_b, ALU.mult)
        e.tensor_scalar(qa[:], qt[:], 16.0, -16.0, ALU.min, ALU.max)
        e.tensor_scalar(qb[:], qt[:], 32.0, -32.0, ALU.min, ALU.max)
        e.tensor_scalar(qw[:], qt[:], 0.25, 64.5, ALU.mult, ALU.add)
        e.scalar_tensor_tensor(qw[:], qb[:], 0.25, qw[:], ALU.mult, ALU.add)
        e.scalar_tensor_tensor(qw[:], qa[:], 0.5, qw[:], ALU.mult, ALU.add)
        e.tensor_copy(qit[:], qw[:])
        e.tensor_tensor(qa[:], qw[:], qit[:], ALU.is_ge)
        e.tensor_scalar(qb[:], qit[:], 1.0, None, ALU.add)
        e.tensor_tensor(qb[:], qw[:], qb[:], ALU.is_ge)
        e.tensor_tensor(qa[:], qa[:], qb[:], ALU.add)
        e.scalar_tensor_tensor(qt[:], qit[:], -65.0, qa[:], ALU.add, ALU.add)
        e.tensor_scalar(qa[:], qt[:], 24.0, -24.0, ALU.min, ALU.max)
        e.tensor_scalar(qb[:], qt[:], 16.0, -16.0, ALU.min, ALU.max)
        e.tensor_scalar(qw[:], qt[:], 4.0, None, ALU.mult)
        e.scalar_tensor_tensor(qw[:], qa[:], -2.0, qw[:], ALU.mult, ALU.add)
        e.scalar_tensor_tensor(qw[:], qb[:], -1.0, qw[:], ALU.mult, ALU.add)
        e.tensor_tensor(g3(qa[:]), g3(qw[:]), ss_b, ALU.mult)

    def f_swf(e):
        for j in range(2):
            src = bass.AP(wffn_h.tensor, j * 128 * C, [[C, 128], [1, C]])
            e.dma_start(out=wfstage[j][:], in_=src).then_inc(S["wf" + str(j)], 16)
    ev("sync", f_swf)
    bump("wf0", 16)
    bump("wf1", 16)

    prev_txd = 0
    for jt in range(8):
        wsem = "wf" + str(jt % 2)
        n_have = val(wsem)

        def f_dg(e, jt=jt, n=n_have, wsem=wsem):
            e.wait_ge(S[wsem], n)
            xg = wfstage[jt % 2][:].rearrange("p (g j) -> p g j", j=32)
            e.tensor_reduce(gmax[:].unsqueeze(2), xg, AX.X, ALU.max,
                            apply_absolute_value=True)
            e.drain().then_inc(S["dv"], 1)
        ev("vector", f_dg)
        n_rs = emit_scale_chain()

        def f_dq(e, jt=jt, n=n_rs, g=prev_txd):
            e.wait_ge(S["ac"], n)
            if g:
                e.wait_ge(S["mm"], g)
            emit_quant_dve(e, wfstage[jt % 2])
            e.drain().then_inc(S["dvt"], 1)
        ev("vector", f_dq)
        n_xd = bump("dvt")

        if jt + 2 < 8:
            def f_sn(e, jt=jt, n=n_xd):
                e.wait_ge(S["dvt"], n)
                src = bass.AP(wffn_h.tensor, (jt + 2) * 128 * C, [[C, 128], [1, C]])
                e.dma_start(out=wfstage[jt % 2][:], in_=src).then_inc(
                    S["wf" + str(jt % 2)], 16)
            ev("sync", f_sn)
            bump("wf" + str(jt % 2), 16)

        def f_tt(e, n=n_xd, g=gA):
            e.wait_ge(S["dvt"], n)
            W(e, g)
            for ct in range(8):
                e.transpose(pA[:, ct * 128:(ct + 1) * 128],
                            qa[:, ct * 128:(ct + 1) * 128], eyef[:])
            e.drain().then_inc(S["mm"], 1)
        ev("tensor", f_tt)
        prev_txd = bump("mm")

        def f_ae(e, jt=jt, n=prev_txd):
            e.wait_ge(S["mm"], n)
            for ct in range(8):
                e.activation(slab["wdt"][:, ct * C + jt * 128: ct * C + jt * 128 + 128],
                             pA[:, ct * 128:(ct + 1) * 128], AF.Identity)
            e.drain().then_inc(S["act_t"], 1)
        ev("scalar", f_ae)
        gA = ("act_t", bump("act_t"))

    stages["startup_ffn"] = len(events)
    x_gate = [[("dv", val("dv"))], []]
    stage_gate = [None, None]
    out_gate = [None, None]
    xnT_gate = None
    xd_store_gate = None
    q_store_gate = None
    state_ready = None
    prev_txn = 0
    prev_txd_x = prev_txd

    for b in range(nb):
        for tau in range(nt):
            buf = tau % 2
            xsem = "x0" if buf == 0 else "x1"

            def f_sx(e, b=b, tau=tau, buf=buf, gs=tuple(x_gate[buf]), xsem=xsem):
                for g in gs:
                    W(e, g)
                src = bass.AP(x_h.tensor, (b * seq + tau * 128) * C,
                              [[C, 128], [1, C]])
                e.dma_start(out=xtile[buf][:], in_=src).then_inc(S[xsem], 16)
            ev("sync", f_sx)
            n_x = bump(xsem, 16)

            def f_a1(e, buf=buf, n=n_x, xsem=xsem):
                e.wait_ge(S[xsem], n)
                e.activation(sq_dump[:], xtile[buf][:], AF.Square,
                             accum_out=ln1["sxx"][:])
                e.activation(sq_dump[:], xtile[buf][:], AF.Identity,
                             accum_out=ln1["sx"][:])
                e.drain().then_inc(S["ac"], 1)
            ev("scalar", f_a1)
            n_sxx = bump("ac")
            act_read_x = ("ac", n_sxx)

            def f_d2(e, n=n_sxx):
                e.wait_ge(S["ac"], n)
                e.tensor_scalar(ln1["mneg"][:], ln1["sx"][:], -1.0 / C, None,
                                ALU.mult)
                e.drain().then_inc(S["dv"], 1)
            ev("vector", f_d2)
            n_mn = bump("dv")

            def f_a1b(e, n=n_mn):
                e.wait_ge(S["dv"], n)
                e.activation(ln1["m2"][:], ln1["mneg"][:], AF.Square)
                e.activation(ln1["mnegA"][:], ln1["mneg"][:], AF.Identity)
                e.drain().then_inc(S["ac"], 1)
            ev("scalar", f_a1b)
            n_m2 = bump("ac")

            def f_d2b(e, n=n_m2):
                e.wait_ge(S["ac"], n)
                e.scalar_tensor_tensor(ln1["u"][:], ln1["sxx"][:], 1.0 / C,
                                       ln1["m2"][:], ALU.mult, ALU.subtract)
                e.drain().then_inc(S["dv"], 1)
            ev("vector", f_d2b)
            n_u = bump("dv")

            def f_a2(e, n=n_u):
                e.wait_ge(S["dv"], n)
                e.activation(ln1["sig"][:], ln1["u"][:], AF.Sqrt,
                             bias=csts[:, 6:7], scale=1.0)
                e.drain().then_inc(S["ac"], 1)
            ev("scalar", f_a2)
            n_sig = bump("ac")

            def f_d2c(e, n=n_sig):
                e.wait_ge(S["ac"], n)
                e.reciprocal(ln1["rinv"][:], ln1["sig"][:])
                e.drain().then_inc(S["dv"], 1)
            ev("vector", f_d2c)
            n_ri = bump("dv")

            def f_a2b(e, n=n_ri):
                e.wait_ge(S["dv"], n)
                e.activation(ln1["rstd"][:], ln1["rinv"][:], AF.Identity)
                e.drain().then_inc(S["ac"], 1)
            ev("scalar", f_a2b)
            n_rstd = bump("ac")
            n_xn = n_rstd

            def f_d3(e, buf=buf, n=n_rstd, g=prev_txn):
                e.wait_ge(S["ac"], n)
                if g:
                    e.wait_ge(S["mm"], g)
                mb = ln1["mnegA"][:].broadcast_to([128, C])
                rb = ln1["rstd"][:].broadcast_to([128, C])
                e.tensor_tensor(qt[:], xtile[buf][:], mb, ALU.add)
                e.tensor_tensor(xn[:], qt[:], rb, ALU.mult)
                xg = xn[:].rearrange("p (g j) -> p g j", j=32)
                e.tensor_reduce(gmax[:].unsqueeze(2), xg, AX.X, ALU.max,
                                apply_absolute_value=True)
                e.drain().then_inc(S["dv"], 1)
            ev("vector", f_d3)
            n_xnr = val("dv") + 1
            if debug_stats:
                def f_dl(e, b=b, tau=tau, n=n_xnr):
                    e.wait_ge(S["dv"], n)
                    e.dma_start(out=bass.AP(dbgl_h.tensor, (b * nt + tau) * 256,
                                            [[2, 128], [1, 1]]),
                                in_=ln1["mneg"][:],
                                allow_slow_non_contiguous=True).then_inc(S["sodd"], 16)
                    e.dma_start(out=bass.AP(dbgl_h.tensor, (b * nt + tau) * 256 + 1,
                                            [[2, 128], [1, 1]]),
                                in_=ln1["rstd"][:],
                                allow_slow_non_contiguous=True).then_inc(S["sodd"], 16)
                ev("sync", f_dl)
                bump("sodd", 32)
            x_gate[buf] = [("dv", n_xnr)]
            n_rs = emit_scale_chain()

            def f_d5(e, n=n_rs, g=prev_txd_x):
                e.wait_ge(S["ac"], n)
                if g:
                    e.wait_ge(S["mm"], g)
                emit_quant_dve(e, xn)
                e.drain().then_inc(S["dvt"], 1)
            ev("vector", f_d5)
            n_xdr = bump("dvt")

            slot = tau % 4

            def f_t1(e, n=n_xnr, g=gA):
                e.wait_ge(S["dv"], n)
                W(e, g)
                for ct in range(8):
                    e.transpose(pA[:, ct * 128:(ct + 1) * 128],
                                xn[:, ct * 128:(ct + 1) * 128], eyef[:])
                e.drain().then_inc(S["mm"], 1)
            ev("tensor", f_t1)
            prev_txn = bump("mm")

            def f_ax(e, n=prev_txn, slot=slot, g=xnT_gate):
                e.wait_ge(S["mm"], n)
                W(e, g)
                for ct in range(8):
                    e.activation(xnT[:, ct * 512 + slot * 128:
                                     ct * 512 + slot * 128 + 128],
                                 pA[:, ct * 128:(ct + 1) * 128], AF.Identity)
                e.drain().then_inc(S["act_t"], 1)
            ev("scalar", f_ax)
            gA = ("act_t", bump("act_t"))
            n_xnT = gA[1]

            def f_t2(e, n=n_xdr, g=gA):
                e.wait_ge(S["dvt"], n)
                W(e, g)
                for ct in range(8):
                    e.transpose(pA[:, ct * 128:(ct + 1) * 128],
                                qa[:, ct * 128:(ct + 1) * 128], eyef[:])
                e.drain().then_inc(S["mm"], 1)
            ev("tensor", f_t2)
            prev_txd_x = bump("mm")

            def f_axd(e, n=prev_txd_x, g=xd_store_gate):
                e.wait_ge(S["mm"], n)
                W(e, g)
                e.activation(xdTout[:], pA[:, :], AF.Identity)
                e.drain().then_inc(S["act_t"], 1)
            ev("scalar", f_axd)
            gA = ("act_t", bump("act_t"))
            n_xdT = gA[1]

            def f_sxd(e, tau=tau, n=n_xdT):
                e.wait_ge(S["act_t"], n)
                e.dma_start(out=xdscr[tau, :, :], in_=xdTout[:]).then_inc(S["sxd"], 16)
            ev("sync", f_sxd)
            xd_store_gate = ("sxd", bump("sxd", 16))

            states_prev = ("mm", val("mm"))
            for rk, wnm, bank in (("k0", "wk0", pB), ("v0", "wv0", pA),
                                  ("k1", "wk1", pB), ("v1", "wv1", pA)):
                gbank = gB if bank is pB else gA

                kseg = {"k0": 0, "v0": 1, "k1": 2, "v1": 3}[rk]

                def f_tr(e, wnm=wnm, bank=bank, slot=slot, nx=n_xnT, g=gbank,
                         kseg=kseg):
                    e.wait_ge(S["act_t"], nx)
                    W(e, g)
                    e.wait_ge(S["gp"], n_rowsbf)
                    for ct in range(8):
                        lhs = xnT[:, ct * 512 + slot * 128:
                                  ct * 512 + slot * 128 + 128]
                        for j in range(2):
                            e.matmul(bank[:, j * 512:(j + 1) * 512], lhs,
                                     slab[wnm][:, ct * C + j * 512:
                                               ct * C + (j + 1) * 512],
                                     start=(ct == 0), stop=False)
                    for j in range(2):
                        e.matmul(bank[:, j * 512:(j + 1) * 512], onesb[0:1, :],
                                 krow[0:1, kseg * C + j * 512:
                                      kseg * C + (j + 1) * 512],
                                 start=False, stop=True)
                    e.drain().then_inc(S["mme"], 1)
                ev("tensor", f_tr)
                n_round = bump("mme")

                def f_ake(e, rk=rk, bank=bank, n=n_round, g=states_prev):
                    e.wait_ge(S["mme"], n)
                    W(e, g)
                    e.activation(ks_sb[rk][:], bank[:, :], AF.Identity,
                                 bias=0.0, scale=csts[:, 5:6])
                    e.drain().then_inc(S["free"], 1)
                    e.drain().then_inc(S["ac"], 1)
                ev("scalar", f_ake)
                nf = bump("free")
                if bank is pB:
                    gB = ("free", nf)
                else:
                    gA = ("free", nf)
                n_ks = bump("ac")

                if rk in ("v0", "v1"):
                    k = "0" if rk == "v0" else "1"
                    pS = pK0 if k == "0" else pK1

                    def f_ts(e, k=k, pS=pS, tau=tau, n=n_ks):
                        e.wait_ge(S["ac"], n)
                        for h in range(H):
                            e.matmul(pS[0:64, h * 64:(h + 1) * 64],
                                     ks_sb["k" + k][:, h * 64:(h + 1) * 64],
                                     ks_sb["v" + k][:, h * 64:(h + 1) * 64],
                                     start=(tau == 0 and h % 8 == 0),
                                     stop=(tau == nt - 1 and h % 8 == 7))
                        e.drain().then_inc(S["mm"], 1)
                    ev("tensor", f_ts)
                    bump("mm")

            if slot == 3 or tau == nt - 1:
                span0 = tau - slot
                nsp = slot + 1
                for half in range(2):
                    def f_tq(e, half=half, nsp=nsp, ga=gA, gb=gB):
                        W(e, ga)
                        W(e, gb)
                        for dt in range(4):
                            d0 = half * 4 + dt
                            bankq = pA if dt < 2 else pB
                            for ct in range(8):
                                lhs = slab["wq"][:, ct * C + d0 * 128:
                                                 ct * C + d0 * 128 + 128]
                                mv = xnT[:, ct * 512: ct * 512 + nsp * 128]
                                e.matmul(bankq[:, (dt % 2) * 512:
                                               (dt % 2) * 512 + nsp * 128],
                                         lhs, mv, start=(ct == 0), stop=(ct == 7))
                        e.drain().then_inc(S["mme"], 1)
                    ev("tensor", f_tq)
                    n_q = bump("mme")

                    def f_aq(e, half=half, nsp=nsp, n=n_q, g=q_store_gate):
                        e.wait_ge(S["mme"], n)
                        W(e, g)
                        for dt in range(4):
                            d0 = half * 4 + dt
                            bankq = pA if dt < 2 else pB
                            e.activation(qevac[:, dt * 512:dt * 512 + nsp * 128],
                                         bankq[:, (dt % 2) * 512:
                                               (dt % 2) * 512 + nsp * 128],
                                         AF.Identity, bias=bq_fm[:, d0:d0 + 1])
                        e.drain().then_inc(S["free"], 2)
                        e.drain().then_inc(S["act_t"], 1)
                    ev("scalar", f_aq)
                    nf = bump("free", 2)
                    gA = ("free", nf)
                    gB = ("free", nf)
                    n_qe = bump("act_t")

                    def f_sq(e, half=half, span0=span0, nsp=nsp, n=n_qe,
                             b=b, spi=tau // 4):
                        e.wait_ge(S["act_t"], n)
                        if debug_stats and half == 0:
                            e.dma_start(out=bass.AP(dbgx_h.tensor,
                                        (b * (nt // 4 + 1) + spi) * 128 * 4096,
                                        [[4096, 128], [1, 4096]]),
                                        in_=xnT[:]).then_inc(S["sodd"], 16)
                        if debug_stats:
                            e.dma_start(out=bass.AP(dbgv_h.tensor,
                                        ((b * (nt // 4 + 1) + spi) * 2 + half) * 128 * 2048,
                                        [[2048, 128], [1, 2048]]),
                                        in_=qevac[:]).then_inc(S["sodd"], 16)
                        for t in range(nsp):
                            dst = bass.AP(qscr.tensor,
                                          (span0 + t) * 128 * 1024 + half * 4 * 128,
                                          [[1024, 128], [128, 4], [1, 128]])
                            srcap = bass.AP(qevac, t * 128,
                                            [[4 * 512, 128], [512, 4], [1, 128]])
                            e.dma_start(out=dst, in_=srcap).then_inc(S["sq"], 16)
                    ev("sync", f_sq)
                    if debug_stats:
                        bump("sodd", 16 if half else 32)
                    q_store_gate = ("sq", bump("sq", 16 * nsp))
                xnT_gate = ("mme", val("mme"))

        stages[f"phase1_b{b}"] = len(events)
        n_states_done = val("mm")
        og = (out_gate[0], out_gate[1])

        def f_dst(e, n=n_states_done, og=og):
            e.wait_ge(S["mm"], n)
            for g in og:
                W(e, g)
            e.memset(state_sb[:], 0.0)
            e.tensor_copy(kst[0:64, :], pK0[0:64, :])
            ev_src0 = bass.AP(kst, 0, [[C, 64], [128, 8], [1, 64]])
            ev_src1 = bass.AP(pK1, 0, [[C, 64], [128, 8], [1, 64]])
            ev_dst = bass.AP(state_sb, 0, [[8 * 128, 64], [128, 8], [1, 64]])
            e.tensor_tensor(ev_dst, ev_src0, ev_src1, ALU.mult)
            od_src0 = bass.AP(kst, 64, [[C, 64], [128, 8], [1, 64]])
            od_src1 = bass.AP(pK1, 64, [[C, 64], [128, 8], [1, 64]])
            e.tensor_tensor(oddst[:].rearrange("p (g j) -> p g j", j=64),
                            od_src0, od_src1, ALU.mult)
            e.drain().then_inc(S["dv"], 1)
        ev("vector", f_dst)
        n_odd = bump("dv")

        def f_sodd(e, n=n_odd):
            e.wait_ge(S["dv"], n)
            dst = bass.AP(state_sb, 64 * (8 * 128) + 64,
                          [[8 * 128, 64], [128, 8], [1, 64]])
            e.dma_start(out=dst, in_=oddst[:]).then_inc(S["sodd"], 16)
        ev("sync", f_sodd)
        n_stdma = bump("sodd", 16)

        def f_dsrs(e, n=n_stdma):
            e.wait_ge(S["sodd"], n)
            sg = state_sb[:].rearrange("p (g j) -> p g j", j=128)
            e.tensor_reduce(srs_sb[:].unsqueeze(2), sg, AX.X, ALU.add)
            e.drain().then_inc(S["dvt"], 1)
        ev("vector", f_dsrs)
        n_srs = bump("dvt")

        def f_asrs(e, n=n_srs):
            e.wait_ge(S["dvt"], n)
            e.activation(srs_bf[:], srs_sb[:], AF.Identity)
            e.drain().then_inc(S["ac"], 1)
        ev("scalar", f_asrs)
        state_ready = ("ac", bump("ac"))

        stages[f"states_b{b}"] = len(events)
        prev_norm = None
        sq_all = ("sq", val("sq"))
        sxd_all = ("sxd", val("sxd"))
        for tau in range(nt):
            stages[f"p2_b{b}_t{tau}"] = len(events)
            buf = tau % 2
            psem = "p0" if buf == 0 else "p1"

            def f_s2(e, b=b, tau=tau, buf=buf, psem=psem,
                     gx=tuple(x_gate[buf]), gs=stage_gate[buf],
                     gq=sq_all, gxd=sxd_all):
                for g in gx:
                    W(e, g)
                W(e, gs)
                W(e, gq)
                W(e, gxd)
                src = bass.AP(x_h.tensor, (b * seq + tau * 128) * C,
                              [[C, 128], [1, C]])
                e.dma_start(out=xtile[buf][:], in_=src).then_inc(S[psem], 16)
                e.dma_start(out=qstage[buf][:], in_=qscr[tau, :, :]).then_inc(S[psem], 16)
                e.dma_start(out=xdstage[buf][:], in_=xdscr[tau, :, :]).then_inc(S[psem], 16)
            ev("sync", f_s2)
            n_ld2 = bump(psem, 48)

            def f_tam(e, buf=buf, n=n_ld2, psem=psem, ns=state_ready,
                      pn=prev_norm, ga=gA):
                e.wait_ge(S[psem], n)
                W(e, ns)
                W(e, pn)
                W(e, ga)
                for j in range(8):
                    e.matmul(pA[:, j * 128:(j + 1) * 128],
                             state_sb[:, j * 128:(j + 1) * 128],
                             qstage[buf][:, j * 128:(j + 1) * 128],
                             start=True, stop=True)
                for j in range(8):
                    e.matmul(pK0[0:1, 0:128], srs_bf[:, j:j + 1],
                             qstage[buf][:, j * 128:(j + 1) * 128],
                             start=(j == 0), stop=(j == 7))
                e.drain().then_inc(S["mm"], 1)
            ev("tensor", f_tam)
            n_am = bump("mm")

            def f_asq(e, n=n_am):
                e.wait_ge(S["mm"], n)
                e.activation(sq_sb[:], pA[:, :], AF.Square)
                e.drain().then_inc(S["act_t"], 1)
            ev("scalar", f_asq)
            n_sq2 = bump("act_t")

            def f_tsq(e, n=n_sq2):
                e.wait_ge(S["act_t"], n)
                for j in range(8):
                    e.matmul(pK0[0:1, 128:256], onescol[:, 0:1],
                             sq_sb[:, j * 128:(j + 1) * 128],
                             start=(j == 0), stop=(j == 7))
                e.drain().then_inc(S["mm"], 1)
            ev("tensor", f_tsq)
            n_sums = bump("mm")

            def f_dstat(e, n=n_sums):
                e.wait_ge(S["mm"], n)
                e.tensor_scalar(mrow[:], pK0[0:1, 0:128], 1.0 / C, None, ALU.mult)
                e.tensor_tensor(m2row[:], mrow[:], mrow[:], ALU.mult)
                e.scalar_tensor_tensor(vrow[:], pK0[0:1, 128:256], 1.0 / C,
                                       m2row[:], ALU.mult, ALU.subtract)
                e.tensor_scalar(vrow[:], vrow[:], 1e-5, None, ALU.add)
                e.reciprocal(rqrow[:], vrow[:])
                e.tensor_copy(mrow_bf[:], mrow[:])
                e.drain().then_inc(S["dv"], 1)
            ev("vector", f_dstat)
            n_var = bump("dv")

            def f_ar(e, n=n_var):
                e.wait_ge(S["dv"], n)
                e.activation(rrow[:], rqrow[:], AF.Sqrt)
                e.drain().then_inc(S["ac"], 1)
            ev("scalar", f_ar)
            n_rr = bump("ac")

            def f_tbc(e, n=n_rr):
                e.wait_ge(S["ac"], n)
                e.matmul(pK0[:, 256:384], onesb[0:1, :], mrow_bf[:],
                         start=True, stop=True)
                e.matmul(pK0[:, 384:512], onesb[0:1, :], rrow[:],
                         start=True, stop=True)
                e.drain().then_inc(S["mm"], 1)
            ev("tensor", f_tbc)
            n_bc = bump("mm")

            def f_abc(e, n=n_bc):
                e.wait_ge(S["mm"], n)
                e.activation(mtile2[:], pK0[:, 256:512], AF.Identity)
                e.drain().then_inc(S["act_t"], 1)
            ev("scalar", f_abc)
            n_bc2 = bump("act_t")

            def f_dn(e, n=n_bc2):
                e.wait_ge(S["act_t"], n)
                g3 = lambda ap: ap.rearrange("p (g j) -> p g j", j=128)
                mt = mtile2[:, 0:128].unsqueeze(1).broadcast_to([128, 8, 128])
                rt2 = mtile2[:, 128:256].unsqueeze(1).broadcast_to([128, 8, 128])
                e.tensor_tensor(g3(qt[:]), g3(pA[:, :]), mt, ALU.subtract)
                e.tensor_tensor(g3(ln_sb[:]), g3(qt[:]), rt2, ALU.mult)
                e.drain().then_inc(S["dvt"], 1)
            ev("vector", f_dn)
            prev_norm = ("dvt", bump("dvt"))
            gA = prev_norm
            if debug_stats:
                def f_dbg(e, b=b, tau=tau, buf=buf, n=prev_norm[1], n2=n_sq2):
                    e.wait_ge(S["dvt"], n)
                    e.dma_start(out=bass.AP(dbg_h.tensor,
                                            (b * nt + tau) * 128 * 256,
                                            [[256, 128], [1, 256]]),
                                in_=mtile2[:]).then_inc(S["sodd"], 16)
                    e.dma_start(out=bass.AP(dbgq_h.tensor,
                                            (b * nt + tau) * 128 * 1024,
                                            [[1024, 128], [1, 1024]]),
                                in_=qstage[buf][:]).then_inc(S["sodd"], 16)
                    e.dma_start(out=bass.AP(dbgs_h.tensor,
                                            (b * nt + tau) * 128 * 1024,
                                            [[1024, 128], [1, 1024]]),
                                in_=sq_sb[:]).then_inc(S["sodd"], 16)
                ev("sync", f_dbg)
                bump("sodd", 48)

            def f_to(e, buf=buf, n=prev_norm[1], g=gB):
                e.wait_ge(S["dvt"], n)
                W(e, g)
                for ct in range(8):
                    for j in range(2):
                        e.matmul(pB[:, j * 512:(j + 1) * 512],
                                 ln_sb[:, ct * 128:(ct + 1) * 128],
                                 slab["wo"][:, ct * C + j * 512:
                                           ct * C + (j + 1) * 512],
                                 start=(ct == 0), stop=False)
                for ct in range(8):
                    for j in range(2):
                        e.matmul(pB[:, j * 512:(j + 1) * 512],
                                 xdstage[buf][:, ct * 128:(ct + 1) * 128],
                                 slab["wdt"][:, ct * C + j * 512:
                                            ct * C + (j + 1) * 512],
                                 start=False, stop=False)
                for j in range(2):
                    e.matmul(pB[:, j * 512:(j + 1) * 512], onesb[0:3, :],
                             rows3[0:3, j * 512:(j + 1) * 512],
                             start=False, stop=True)
                e.drain().then_inc(S["mme"], 1)
            ev("tensor", f_to)
            n_out = bump("mme")
            stage_gate[buf] = ("mme", n_out)

            def f_do(e, buf=buf, n=n_out, g=out_gate[buf]):
                e.wait_ge(S["mme"], n)
                W(e, g)
                e.scalar_tensor_tensor(out_sb[buf][:], pB[:, :], 1.0,
                                       xtile[buf][:], ALU.mult, ALU.add)
                e.drain().then_inc(S["free"], 1)
                e.drain().then_inc(S["dv"], 1)
            ev("vector", f_do)
            gB = ("free", bump("free"))
            n_o = bump("dv")
            x_gate[buf] = [("dv", n_o)]

            osem = "so0" if buf == 0 else "so1"

            def f_so(e, b=b, tau=tau, buf=buf, n=n_o, osem=osem):
                e.wait_ge(S["dv"], n)
                dst = bass.AP(y_h.tensor, (b * seq + tau * 128) * C,
                              [[C, 128], [1, C]])
                e.dma_start(out=dst, in_=out_sb[buf][:]).then_inc(S[osem], 16)
            ev("sync", f_so)
            out_gate[buf] = (osem, bump(osem, 16))

    stages["all"] = len(events)
    import os as _os
    if _os.environ.get("PRINT_STAGES"):
        print("STAGES:", stages)
    if stop_stage is not None:
        cut = stop_stage if isinstance(stop_stage, int) else stages[stop_stage]
        del events[cut:]
    for name in ("vector", "scalar", "tensor", "gpsimd"):
        def f_fin(e):
            e.drain().then_inc(S["done"], 1)
        ev(name, f_fin)
    n_done = bump("done", 4)
    final_counts = dict(cnt)

    def f_fins(e):
        if stop_stage is None:
            for nm in ("so0", "so1", "sxd", "sq", "sodd"):
                e.wait_ge(S[nm], final_counts[nm])
        e.wait_ge(S["done"], n_done)
        if debug_tail:
            ntail = 0
            fslots = [xn, qa, out_sb[0], out_sb[1], xtile[0], xtile[1]]
            for i, t in enumerate(fslots):
                e.dma_start(out=bass.AP(tailf_h.tensor, i * 128 * 1024,
                                        [[1024, 128], [1, 1024]]),
                            in_=t[:]).then_inc(S["wld"], 16)
                ntail += 16
            for j, nm in enumerate(("sx", "sxx", "mneg", "m2", "u", "sig", "rstd")):
                e.dma_start(out=bass.AP(tailf_h.tensor, 6 * 128 * 1024 + j,
                                        [[1024, 128], [1, 1]]),
                            in_=ln1[nm][:],
                            allow_slow_non_contiguous=True).then_inc(S["wld"], 16)
                ntail += 16
            for j, t in enumerate((gmax, s0, rcp, rs_t, ss_t)):
                e.dma_start(out=bass.AP(tailf_h.tensor,
                                        6 * 128 * 1024 + 32 + j * 32,
                                        [[1024, 128], [1, 32]]),
                            in_=t[:],
                            allow_slow_non_contiguous=True).then_inc(S["wld"], 16)
                ntail += 16
            e.dma_start(out=bass.AP(tailf_h.tensor, 7 * 128 * 1024,
                                    [[1024, 128], [1, 256]]),
                        in_=mtile2[:],
                        allow_slow_non_contiguous=True).then_inc(S["wld"], 16)
            ntail += 16
            bslots = [state_sb, ks_sb["k0"], ks_sb["v0"], ks_sb["k1"],
                      ks_sb["v1"], None, None, None, None, qstage[0],
                      xdstage[0], sq_sb, ln_sb, None, None]
            for i, t in enumerate(bslots):
                if t is None:
                    continue
                e.dma_start(out=bass.AP(tailb_h.tensor, i * 128 * 1024,
                                        [[1024, 128], [1, 1024]]),
                            in_=t[:]).then_inc(S["wld"], 16)
                ntail += 16
            for j in range(4):
                e.dma_start(out=bass.AP(tailb_h.tensor, (5 + j) * 128 * 1024,
                                        [[1024, 128], [1, 1024]]),
                            in_=xnT[:, j * 1024:(j + 1) * 1024]).then_inc(S["wld"], 16)
                ntail += 16
            e.dma_start(out=bass.AP(tailb_h.tensor, 13 * 128 * 1024,
                                    [[1024, 128], [1, 1024]]),
                        in_=slab["wdt"][:, 0:1024]).then_inc(S["wld"], 16)
            ntail += 16
            e.dma_start(out=bass.AP(tailb_h.tensor, 14 * 128 * 1024,
                                    [[1024, 64], [1, 512]]),
                        in_=oddst[:],
                        allow_slow_non_contiguous=True).then_inc(S["wld"], 16)
            ntail += 16
            for tt in range(nt):
                e.dma_start(out=bass.AP(tailq_h.tensor, tt * 128 * 1024,
                                        [[1024, 128], [1, 1024]]),
                            in_=qscr[tt, :, :]).then_inc(S["wld"], 16)
                e.dma_start(out=bass.AP(tailq_h.tensor,
                                        (nt + tt) * 128 * 1024,
                                        [[1024, 128], [1, 1024]]),
                            in_=xdscr[tt, :, :]).then_inc(S["wld"], 16)
                ntail += 32
            e.wait_ge(S["wld"], final_counts["wld"] + ntail)
        for s in S.values():
            e.sem_clear(s)
    ev("sync", f_fins)

    with nc.Block() as block:
        @block.sync
        def _(eng):
            for name, fn in events:
                if name == "sync":
                    fn(eng)

        @block.vector
        def _(eng):
            for name, fn in events:
                if name == "vector":
                    fn(eng)

        @block.scalar
        def _(eng):
            for name, fn in events:
                if name == "scalar":
                    fn(eng)

        @block.tensor
        def _(eng):
            for name, fn in events:
                if name == "tensor":
                    fn(eng)

        @block.gpsimd
        def _(eng):
            for name, fn in events:
                if name == "gpsimd":
                    fn(eng)

    stk.close()
    return nc


def make_base_inputs(Wq, bq, Wk, bk, Wv, bv, Wo, bo, ln_g, ln_b, Wffn, bffn):
    brows = np.stack([
        np.asarray(bq, np.float32), np.asarray(bk[0], np.float32),
        np.asarray(bk[1], np.float32), np.asarray(bv[0], np.float32),
        np.asarray(bv[1], np.float32), np.asarray(bo, np.float32),
        np.asarray(bffn, np.float32), np.asarray(ln_b, np.float32),
    ]).astype(np.float32)
    return {
        "wq": np.ascontiguousarray(np.asarray(Wq, np.float32)),
        "wk0": np.ascontiguousarray(np.asarray(Wk[0], np.float32)),
        "wk1": np.ascontiguousarray(np.asarray(Wk[1], np.float32)),
        "wv0": np.ascontiguousarray(np.asarray(Wv[0], np.float32)),
        "wv1": np.ascontiguousarray(np.asarray(Wv[1], np.float32)),
        "wo": np.ascontiguousarray(np.asarray(Wo, np.float32)),
        "wffn": np.ascontiguousarray(np.asarray(Wffn, np.float32)),
        "brows": brows,
        "bq_fm": np.ascontiguousarray(np.asarray(bq, np.float32).reshape(8, 128).T),
        "g_fm": np.ascontiguousarray(np.asarray(ln_g, np.float32).reshape(8, 128).T),
        "b_colb": np.ascontiguousarray(
            np.asarray(ln_b, np.float32).reshape(8, 128).T.astype(BF)),
        "eyef": np.eye(128, dtype=np.float32),
        "eyeb": np.eye(128, dtype=BF),
        "onesb": np.ones((3, 128), dtype=BF),
        "onesf": np.ones((1, 128), dtype=np.float32),
        "onescol": np.ones((128, 1), dtype=BF),
        "csts": np.tile(np.array([1.0 / 7.5, 1e-30, 8.0, 0.125, 1.0 / 1024,
                                  1.0 / 64, 1e-5, 0],
                                 dtype=np.float32), (128, 1)),
    }


def kernel(L0_attn_x, L0_attn_Wq, L0_attn_bq, L0_attn_Wk, L0_attn_bk,
           L0_attn_Wv, L0_attn_bv, L0_attn_Wo, L0_attn_bo,
           L0_attn_ln_g, L0_attn_ln_b, L0_ffn_weight, L0_ffn_bias):
    x = np.ascontiguousarray(np.asarray(L0_attn_x, dtype=np.float32))
    B = x.shape[0]
    ncores = 8
    per = B // ncores
    nc = build_program(nb=per, nt=NT)
    base = make_base_inputs(L0_attn_Wq, L0_attn_bq, L0_attn_Wk, L0_attn_bk,
                            L0_attn_Wv, L0_attn_bv, L0_attn_Wo, L0_attn_bo,
                            L0_attn_ln_g, L0_attn_ln_b, L0_ffn_weight,
                            L0_ffn_bias)
    in_maps = []
    for c in range(ncores):
        m = dict(base)
        m["x"] = np.ascontiguousarray(x[c * per:(c + 1) * per])
        in_maps.append(m)
    res = run_bass_kernel_spmd(nc, in_maps, core_ids=list(range(ncores)))
    out = np.concatenate([np.asarray(r["y"]) for r in res.results], axis=0)
    return out.astype(np.float32)


# version 04
# speedup vs baseline: 1.0394x; 1.0394x over previous
import numpy as np
import ml_dtypes
import concourse.bass as bass
import concourse.mybir as mybir
from concourse.bass_utils import run_bass_kernel_spmd
from contextlib import ExitStack

F32 = mybir.dt.float32
BF16 = mybir.dt.bfloat16
I32 = mybir.dt.int32
AF = mybir.ActivationFunctionType
ALU = mybir.AluOpType
AX = mybir.AxisListType
BF = ml_dtypes.bfloat16

C = 1024
H = 16
NT = 16
SCALE = 1.0 / 64.0
BOOT = 64


def build_program(nb=4, nt=NT, stop_stage=None, detect_races=False,
                  debug_stats=False, debug_tail=False):
    nc = bass.Bass("TRN2", target_bir_lowering=False, detect_race_conditions=detect_races)
    seq = nt * 128

    x_h = nc.dram_tensor("x", [nb, seq, C], F32, kind="ExternalInput").ap()
    w_h = {nm: nc.dram_tensor(nm, [C, C], F32, kind="ExternalInput").ap()
           for nm in ("wq", "wk0", "wk1", "wv0", "wv1", "wo")}
    wffn_h = nc.dram_tensor("wffn", [C, C], F32, kind="ExternalInput").ap()
    rows_h = nc.dram_tensor("brows", [8, C], F32, kind="ExternalInput").ap()
    bq_fm_h = nc.dram_tensor("bq_fm", [128, 8], F32, kind="ExternalInput").ap()
    g_fm_h = nc.dram_tensor("g_fm", [128, 8], F32, kind="ExternalInput").ap()
    bcol_h = nc.dram_tensor("b_colb", [128, 8], BF16, kind="ExternalInput").ap()
    eyef_h = nc.dram_tensor("eyef", [128, 128], F32, kind="ExternalInput").ap()
    eyeb_h = nc.dram_tensor("eyeb", [128, 128], BF16, kind="ExternalInput").ap()
    onesb_h = nc.dram_tensor("onesb", [3, 128], BF16, kind="ExternalInput").ap()
    onesf_h = nc.dram_tensor("onesf", [1, 128], F32, kind="ExternalInput").ap()
    onescol_h = nc.dram_tensor("onescol", [128, 1], BF16, kind="ExternalInput").ap()
    csts_h = nc.dram_tensor("csts", [128, 8], F32, kind="ExternalInput").ap()
    y_h = nc.dram_tensor("y", [nb, seq, C], F32, kind="ExternalOutput").ap()
    qscr = nc.dram_tensor("qscr", [nt, 128, 8 * 128], BF16).ap()
    dbg_h = (nc.dram_tensor("dbg", [nb, nt, 128, 256], F32, kind="ExternalOutput").ap()
             if debug_stats else None)
    dbgq_h = (nc.dram_tensor("dbgq", [nb, nt, 128, 1024], BF16, kind="ExternalOutput").ap()
              if debug_stats else None)
    dbgs_h = (nc.dram_tensor("dbgs", [nb, nt, 128, 1024], BF16, kind="ExternalOutput").ap()
              if debug_stats else None)
    dbgv_h = (nc.dram_tensor("dbgv", [nb * (nt // 4 + 1), 2, 128, 2048], BF16,
                             kind="ExternalOutput").ap() if debug_stats else None)
    dbgx_h = (nc.dram_tensor("dbgx", [nb * (nt // 4 + 1), 128, 4096], BF16,
                             kind="ExternalOutput").ap() if debug_stats else None)
    dbgl_h = (nc.dram_tensor("dbgl", [nb, nt, 128, 2], F32,
                             kind="ExternalOutput").ap() if debug_stats else None)
    xdscr = nc.dram_tensor("xdscr", [nt, 128, 8 * 128], BF16).ap()
    tailf_h = (nc.dram_tensor("tailf", [8, 128, 1024], F32, kind="ExternalOutput").ap()
               if debug_tail else None)
    tailb_h = (nc.dram_tensor("tailb", [15, 128, 1024], BF16, kind="ExternalOutput").ap()
               if debug_tail else None)
    tailp_h = (nc.dram_tensor("tailp", [3, 128, 1024], F32, kind="ExternalOutput").ap()
               if debug_tail else None)
    tailq_h = (nc.dram_tensor("tailq", [2, nt, 128, 1024], BF16, kind="ExternalOutput").ap()
               if debug_tail else None)

    stk = ExitStack()
    sbt = lambda n, s, d: stk.enter_context(nc.sbuf_tensor(n, s, d))
    psa = lambda n, s: stk.enter_context(nc.psum_tensor(n, s, F32))
    mksem = lambda n: stk.enter_context(nc.semaphore(n))

    slab = {nm: sbt(nm + "_s", [128, 8 * C], BF16) for nm in
            ("wq", "wk0", "wk1", "wv0", "wv1", "wo", "wdt")}
    rows3 = sbt("rows3", [3, C], BF16)
    bq_fm = sbt("bqfm", [128, 8], F32)
    g_fm = sbt("gfm", [128, 8], F32)
    bcol_bf = sbt("bcolb", [128, 8], BF16)
    eyef = sbt("eyefs", [128, 128], F32)
    eyeb = sbt("eyebs", [128, 128], BF16)
    onesb = sbt("onesbs", [3, 128], BF16)
    onesf = sbt("onesfs", [1, 128], F32)
    onescol = sbt("onescols", [128, 1], BF16)
    csts = sbt("cstss", [128, 8], F32)
    krow = sbt("krow", [1, 4 * C], BF16)

    xtile = [sbt(f"xt{i}", [128, C], F32) for i in range(2)]
    xn = sbt("xn", [128, C], F32)
    xnT = sbt("xnT", [128, 4 * 8 * 128], BF16)
    ks_sb = {k: sbt("ks_" + k, [128, C], BF16) for k in ("k0", "k1", "v0", "v1")}
    qt = sbt("q_t", [128, C], F32)
    qa = sbt("q_a", [128, C], F32)
    qb = sbt("q_b", [128, C], F32)
    qw = sbt("q_w", [128, C], F32)
    qit = sbt("q_it", [128, C], I32)
    gmax = sbt("gmaxs", [128, 64], F32)
    s0 = sbt("s0s", [128, 64], F32)
    rcp = sbt("rcps", [128, 64], F32)
    rs_t = sbt("rss", [128, 64], F32)
    ss_t = sbt("sss", [128, 64], F32)
    sq_dump = sbt("sqdump", [128, C], BF16)
    ln1 = {n: sbt("l_" + n, [128, 2], F32) for n in
           ("sx", "sxx", "mneg", "m2", "u", "sig", "rinv", "rstd", "mnegA")}
    state_sb = sbt("state", [128, 8 * 128], BF16)
    srs_sb = sbt("srs", [128, 8], F32)
    srs_bf = sbt("srsbf", [128, 8], BF16)
    oddst = sbt("oddst", [64, 8 * 64], BF16)
    qstage = [sbt(f"qstage{i}", [128, 8 * 128], BF16) for i in range(2)]
    xdstage = [sbt(f"xdstage{i}", [128, 8 * 128], BF16) for i in range(2)]
    xdTout = sbt("xdTout", [128, 8 * 128], BF16)
    sq_sb = sbt("sqsb", [128, 8 * 128], BF16)
    ln_sb = sbt("lnsb", [128, 8 * 128], BF16)
    mrow = sbt("mrow", [1, 128], F32)
    mrow_bf = sbt("mrowbf", [1, 128], BF16)
    rrow = sbt("rrow", [1, 128], BF16)
    vrow = sbt("vrow", [1, 128], F32)
    m2row = sbt("m2row", [1, 128], F32)
    rqrow = sbt("rqrow", [1, 128], F32)
    mtile2 = sbt("mtile2", [128, 256], F32)
    out_sb = [sbt(f"outsb{i}", [128, C], F32) for i in range(2)]
    qevac = sbt("qevac", [128, 4 * 512], BF16)
    wfstage = out_sb
    kst = out_sb[0]

    pA = psa("pA", [128, C])
    pB = psa("pB", [128, C])
    pK0 = psa("pK0", [128, C])
    pK1 = psa("pK1", [128, C])

    semnames = ("x0", "x1", "x2", "p0", "p1", "wf0", "wf1", "wld", "sodd",
                "sxd", "sq", "so0", "so1",
                "dv", "ac", "dvt", "act_t", "mm", "mme", "free", "gp", "done")
    S = {n: mksem("s_" + n) for n in semnames}
    cnt = {n: 0 for n in semnames}
    events = []
    stages = {}

    def ev(engine, fn):
        events.append((engine, fn))

    def bump(name, by=1):
        cnt[name] += by
        return cnt[name]

    def val(name):
        return cnt[name]

    def W(e, gate):
        if gate is not None and gate[1] > 0:
            e.wait_ge(S[gate[0]], gate[1])

    def f_sync0(e):
        for s in S.values():
            e.sem_clear(s)
        e.sem_inc(S["done"], BOOT)
        for dst, src in ((bq_fm, bq_fm_h), (g_fm, g_fm_h),
                         (bcol_bf, bcol_h), (eyef, eyef_h), (eyeb, eyeb_h),
                         (onesb, onesb_h), (onesf, onesf_h),
                         (onescol, onescol_h), (csts, csts_h)):
            e.dma_start(out=dst[:], in_=src[:]).then_inc(S["wld"], 16)
    ev("sync", f_sync0)
    bump("done", BOOT)
    n_consts = bump("wld", 144)

    def f_gp0(e):
        e.wait_ge(S["done"], BOOT)
        for nm in ("wk0", "wv0", "wk1", "wv1", "wq", "wo"):
            src = bass.AP(w_h[nm].tensor, 0, [[C, 128], [128 * C, 8], [1, C]])
            e.dma_start(out=slab[nm][:], in_=src).then_inc(S["gp"], 16)
        e.dma_start(out=rows3[1:2, :],
                    in_=bass.AP(rows_h.tensor, 5 * C, [[C, 1], [1, C]])).then_inc(S["gp"], 16)
        e.dma_start(out=rows3[2:3, :],
                    in_=bass.AP(rows_h.tensor, 6 * C, [[C, 1], [1, C]])).then_inc(S["gp"], 16)
        for i, r in enumerate((1, 3, 2, 4)):
            e.dma_start(out=krow[0:1, i * C:(i + 1) * C],
                        in_=bass.AP(rows_h.tensor, r * C,
                                    [[C, 1], [1, C]])).then_inc(S["gp"], 16)
    ev("gpsimd", f_gp0)
    n_wslabs = bump("gp", 96)
    n_rowsbf = bump("gp", 96)

    def f_dve0(e):
        e.wait_ge(S["done"], BOOT)
        e.wait_ge(S["wld"], n_consts)
        e.wait_ge(S["gp"], n_rowsbf)
        for nm in ("wq", "wk0", "wk1", "wv0", "wv1"):
            for ct in range(8):
                d0 = ct * C + ct * 128
                e.tensor_tensor(slab[nm][:, d0:d0 + 128],
                                slab[nm][:, d0:d0 + 128], eyeb[:], ALU.add)
        e.drain().then_inc(S["dvt"], 1)
    ev("vector", f_dve0)
    n_wplusI = bump("dvt")

    def f_t0(e):
        e.wait_ge(S["done"], BOOT)
        e.wait_ge(S["gp"], n_rowsbf)
        e.wait_ge(S["dvt"], n_wplusI)
        for j in range(2):
            for ct in range(8):
                e.matmul(pA[0:1, j * 512:(j + 1) * 512], bcol_bf[:, ct:ct + 1],
                         slab["wo"][:, ct * C + j * 512: ct * C + (j + 1) * 512],
                         start=(ct == 0), stop=(ct == 7))
        e.drain().then_inc(S["mm"], 1)
    ev("tensor", f_t0)
    n_bwo_mm = bump("mm")

    def f_a0(e):
        e.wait_ge(S["done"], BOOT)
        e.wait_ge(S["wld"], n_consts)
        e.wait_ge(S["gp"], n_rowsbf)
        e.wait_ge(S["mm"], n_bwo_mm)
        e.activation(rows3[0:1, :], pA[0:1, :], AF.Identity)
        e.drain().then_inc(S["act_t"], 1)
    ev("scalar", f_a0)
    gA = ("act_t", bump("act_t"))

    def f_awo(e, g=gA):
        e.wait_ge(S[g[0]], g[1])
        for ct in range(8):
            e.activation(slab["wo"][:, ct * C:(ct + 1) * C],
                         slab["wo"][:, ct * C:(ct + 1) * C],
                         AF.Identity, bias=0.0, scale=g_fm[:, ct:ct + 1])
        e.drain().then_inc(S["ac"], 1)
    ev("scalar", f_awo)
    n_wo_ready = bump("ac")

    gB = ("free", 0)

    def emit_scale_chain(off=0, n_gm=None):
        if n_gm is None:
            n_gm = bump("dv")
        sl = slice(off, off + 32)

        def f_ac(e, n=n_gm, sl=sl):
            e.wait_ge(S["dv"], n)
            e.activation(s0[:, sl], gmax[:, sl], AF.Identity,
                         bias=csts[:, 1:2], scale=csts[:, 0:1])
            e.drain().then_inc(S["ac"], 1)
        ev("scalar", f_ac)
        n_s0 = bump("ac")

        def f_dr(e, n=n_s0, sl=sl):
            e.wait_ge(S["ac"], n)
            e.reciprocal(rcp[:, sl], s0[:, sl])
            e.drain().then_inc(S["dv"], 1)
        ev("vector", f_dr)
        n_rc = bump("dv")

        def f_ac2(e, n=n_rc, sl=sl):
            e.wait_ge(S["dv"], n)
            e.activation(rs_t[:, sl], rcp[:, sl], AF.Identity,
                         bias=0.0, scale=csts[:, 2:3])
            e.activation(ss_t[:, sl], s0[:, sl], AF.Identity,
                         bias=0.0, scale=csts[:, 3:4])
            e.drain().then_inc(S["ac"], 1)
        ev("scalar", f_ac2)
        return bump("ac")

    def emit_quant_dve(e, src, off=0):
        g3 = lambda ap: ap.rearrange("p (g j) -> p g j", j=32)
        rs_b = rs_t[:, off:off + 32].unsqueeze(2).broadcast_to([128, 32, 32])
        ss_b = ss_t[:, off:off + 32].unsqueeze(2).broadcast_to([128, 32, 32])
        e.tensor_tensor(g3(qt[:]), g3(src[:]), rs_b, ALU.mult)
        e.tensor_scalar(qa[:], qt[:], 16.0, -16.0, ALU.min, ALU.max)
        e.tensor_scalar(qb[:], qt[:], 32.0, -32.0, ALU.min, ALU.max)
        e.tensor_scalar(qw[:], qt[:], 0.25, 64.5, ALU.mult, ALU.add)
        e.scalar_tensor_tensor(qw[:], qb[:], 0.25, qw[:], ALU.mult, ALU.add)
        e.scalar_tensor_tensor(qw[:], qa[:], 0.5, qw[:], ALU.mult, ALU.add)
        e.tensor_copy(qit[:], qw[:])
        e.tensor_tensor(qa[:], qw[:], qit[:], ALU.is_ge)
        e.tensor_scalar(qb[:], qit[:], 1.0, None, ALU.add)
        e.tensor_tensor(qb[:], qw[:], qb[:], ALU.is_ge)
        e.tensor_tensor(qa[:], qa[:], qb[:], ALU.add)
        e.scalar_tensor_tensor(qt[:], qit[:], -65.0, qa[:], ALU.add, ALU.add)
        e.tensor_scalar(qa[:], qt[:], 24.0, -24.0, ALU.min, ALU.max)
        e.tensor_scalar(qb[:], qt[:], 16.0, -16.0, ALU.min, ALU.max)
        e.tensor_scalar(qw[:], qt[:], 4.0, None, ALU.mult)
        e.scalar_tensor_tensor(qw[:], qa[:], -2.0, qw[:], ALU.mult, ALU.add)
        e.scalar_tensor_tensor(qw[:], qb[:], -1.0, qw[:], ALU.mult, ALU.add)
        e.tensor_tensor(g3(qa[:]), g3(qw[:]), ss_b, ALU.mult)

    def f_swf(e):
        for j in range(2):
            src = bass.AP(wffn_h.tensor, j * 128 * C, [[C, 128], [1, C]])
            e.dma_start(out=wfstage[j][:], in_=src).then_inc(S["wf" + str(j)], 16)
    ev("sync", f_swf)
    bump("wf0", 16)
    bump("wf1", 16)

    prev_txd = 0
    for jt in range(8):
        wsem = "wf" + str(jt % 2)
        n_have = val(wsem)

        def f_dg(e, jt=jt, n=n_have, wsem=wsem):
            e.wait_ge(S[wsem], n)
            xg = wfstage[jt % 2][:].rearrange("p (g j) -> p g j", j=32)
            e.tensor_reduce(gmax[:, 0:32].unsqueeze(2), xg, AX.X, ALU.max,
                            apply_absolute_value=True)
            e.drain().then_inc(S["dv"], 1)
        ev("vector", f_dg)
        n_rs = emit_scale_chain()

        def f_dq(e, jt=jt, n=n_rs, g=prev_txd):
            e.wait_ge(S["ac"], n)
            if g:
                e.wait_ge(S["mm"], g)
            emit_quant_dve(e, wfstage[jt % 2])
            e.drain().then_inc(S["dvt"], 1)
        ev("vector", f_dq)
        n_xd = bump("dvt")

        if jt + 2 < 8:
            def f_sn(e, jt=jt, n=n_xd):
                e.wait_ge(S["dvt"], n)
                src = bass.AP(wffn_h.tensor, (jt + 2) * 128 * C, [[C, 128], [1, C]])
                e.dma_start(out=wfstage[jt % 2][:], in_=src).then_inc(
                    S["wf" + str(jt % 2)], 16)
            ev("sync", f_sn)
            bump("wf" + str(jt % 2), 16)

        def f_tt(e, n=n_xd, g=gA):
            e.wait_ge(S["dvt"], n)
            W(e, g)
            for ct in range(8):
                e.transpose(pA[:, ct * 128:(ct + 1) * 128],
                            qa[:, ct * 128:(ct + 1) * 128], eyef[:])
            e.drain().then_inc(S["mm"], 1)
        ev("tensor", f_tt)
        prev_txd = bump("mm")

        def f_ae(e, jt=jt, n=prev_txd):
            e.wait_ge(S["mm"], n)
            for ct in range(8):
                e.activation(slab["wdt"][:, ct * C + jt * 128: ct * C + jt * 128 + 128],
                             pA[:, ct * 128:(ct + 1) * 128], AF.Identity)
            e.drain().then_inc(S["act_t"], 1)
        ev("scalar", f_ae)
        gA = ("act_t", bump("act_t"))

    stages["startup_ffn"] = len(events)
    x_gate = [[], []]
    stage_gate = [None, None]
    out_gate = [None, None]
    xnT_gate = None
    xd_store_gate = None
    q_store_gate = None
    state_ready = None
    prev_txn = 0
    prev_txd_x = prev_txd

    for b in range(nb):
        for tp in range(nt // 2):
            pair = (2 * tp, 2 * tp + 1)
            for tau in pair:
                pb = tau % 2
                xb = tau % 2
                xsem = f"x{xb}"

                def f_sx(e, b=b, tau=tau, xb=xb, gs=tuple(x_gate[xb]),
                         xsem=xsem):
                    for g in gs:
                        W(e, g)
                    src = bass.AP(x_h.tensor, (b * seq + tau * 128) * C,
                                  [[C, 128], [1, C]])
                    e.dma_start(out=xtile[xb][:], in_=src).then_inc(S[xsem], 16)
                ev("sync", f_sx)
                n_x = bump(xsem, 16)

                def f_a1(e, xb=xb, pb=pb, n=n_x, xsem=xsem):
                    e.wait_ge(S[xsem], n)
                    e.activation(sq_dump[:], xtile[xb][:], AF.Square,
                                 accum_out=ln1["sxx"][:, pb:pb + 1])
                    e.activation(sq_dump[:], xtile[xb][:], AF.Identity,
                                 accum_out=ln1["sx"][:, pb:pb + 1])
                    e.drain().then_inc(S["ac"], 1)
                ev("scalar", f_a1)
                n_sxx = bump("ac")

            def f_d2(e, n=n_sxx):
                e.wait_ge(S["ac"], n)
                e.tensor_scalar(ln1["mneg"][:], ln1["sx"][:], -1.0 / C, None,
                                ALU.mult)
                e.drain().then_inc(S["dv"], 1)
            ev("vector", f_d2)
            n_mn = bump("dv")

            def f_a1b(e, n=n_mn):
                e.wait_ge(S["dv"], n)
                e.activation(ln1["m2"][:], ln1["mneg"][:], AF.Square)
                e.activation(ln1["mnegA"][:], ln1["mneg"][:], AF.Identity)
                e.drain().then_inc(S["ac"], 1)
            ev("scalar", f_a1b)
            n_m2 = bump("ac")

            def f_d2b(e, n=n_m2):
                e.wait_ge(S["ac"], n)
                e.scalar_tensor_tensor(ln1["u"][:], ln1["sxx"][:], 1.0 / C,
                                       ln1["m2"][:], ALU.mult, ALU.subtract)
                e.drain().then_inc(S["dv"], 1)
            ev("vector", f_d2b)
            n_u = bump("dv")

            def f_a2(e, n=n_u):
                e.wait_ge(S["dv"], n)
                e.activation(ln1["sig"][:], ln1["u"][:], AF.Sqrt,
                             bias=csts[:, 6:7], scale=1.0)
                e.drain().then_inc(S["ac"], 1)
            ev("scalar", f_a2)
            n_sig = bump("ac")

            def f_d2c(e, n=n_sig):
                e.wait_ge(S["ac"], n)
                e.reciprocal(ln1["rinv"][:], ln1["sig"][:])
                e.drain().then_inc(S["dv"], 1)
            ev("vector", f_d2c)
            n_ri = bump("dv")

            def f_a2b(e, n=n_ri):
                e.wait_ge(S["dv"], n)
                e.activation(ln1["rstd"][:], ln1["rinv"][:], AF.Identity)
                e.drain().then_inc(S["ac"], 1)
            ev("scalar", f_a2b)
            n_rstd = bump("ac")

            for tau in pair:
              pb = tau % 2
              xb = tau % 2
              if True:
                def f_d3(e, pb=pb, xb=xb, n=n_rstd, g=prev_txn):
                    e.wait_ge(S["ac"], n)
                    if g:
                        e.wait_ge(S["mm"], g)
                    mb = ln1["mnegA"][:, pb:pb + 1].broadcast_to([128, C])
                    rb = ln1["rstd"][:, pb:pb + 1].broadcast_to([128, C])
                    e.tensor_tensor(qt[:], xtile[xb][:], mb, ALU.add)
                    e.tensor_tensor(xn[:], qt[:], rb, ALU.mult)
                    xg = xn[:].rearrange("p (g j) -> p g j", j=32)
                    e.tensor_reduce(
                        gmax[:, pb * 32:pb * 32 + 32].unsqueeze(2), xg,
                        AX.X, ALU.max, apply_absolute_value=True)
                    e.drain().then_inc(S["dv"], 1)
                ev("vector", f_d3)
                n_xnr = val("dv") + 1
                if debug_stats:
                    def f_dl(e, b=b, tau=tau, pb=pb, n=n_xnr):
                        e.wait_ge(S["dv"], n)
                        e.dma_start(out=bass.AP(dbgl_h.tensor, (b * nt + tau) * 256,
                                                [[2, 128], [1, 1]]),
                                    in_=ln1["mneg"][:, pb:pb + 1],
                                    allow_slow_non_contiguous=True).then_inc(S["sodd"], 16)
                        e.dma_start(out=bass.AP(dbgl_h.tensor, (b * nt + tau) * 256 + 1,
                                                [[2, 128], [1, 1]]),
                                    in_=ln1["rstd"][:, pb:pb + 1],
                                    allow_slow_non_contiguous=True).then_inc(S["sodd"], 16)
                    ev("sync", f_dl)
                    bump("sodd", 32)
                x_gate[xb] = [("dv", n_xnr)]
                n_rs = emit_scale_chain(off=pb * 32)

                def f_d5(e, n=n_rs, g=prev_txd_x, pb=pb):
                    e.wait_ge(S["ac"], n)
                    if g:
                        e.wait_ge(S["mm"], g)
                    emit_quant_dve(e, xn, off=pb * 32)
                    e.drain().then_inc(S["dvt"], 1)
                ev("vector", f_d5)
                n_xdr = bump("dvt")

                slot = tau % 4

                def f_t1(e, n=n_xnr, g=gA):
                    e.wait_ge(S["dv"], n)
                    W(e, g)
                    for ct in range(8):
                        e.transpose(pA[:, ct * 128:(ct + 1) * 128],
                                    xn[:, ct * 128:(ct + 1) * 128],
                                    eyef[:])
                    e.drain().then_inc(S["mm"], 1)
                ev("tensor", f_t1)
                prev_txn = bump("mm")

                def f_ax(e, n=prev_txn, slot=slot, g=xnT_gate):
                    e.wait_ge(S["mm"], n)
                    W(e, g)
                    for ct in range(8):
                        e.activation(xnT[:, ct * 512 + slot * 128:
                                         ct * 512 + slot * 128 + 128],
                                     pA[:, ct * 128:(ct + 1) * 128], AF.Identity)
                    e.drain().then_inc(S["act_t"], 1)
                ev("scalar", f_ax)
                gA = ("act_t", bump("act_t"))
                n_xnT = gA[1]

                def f_t2(e, n=n_xdr, g=gA):
                    e.wait_ge(S["dvt"], n)
                    W(e, g)
                    for ct in range(8):
                        e.transpose(pA[:, ct * 128:(ct + 1) * 128],
                                    qa[:, ct * 128:(ct + 1) * 128], eyef[:])
                    e.drain().then_inc(S["mm"], 1)
                ev("tensor", f_t2)
                prev_txd_x = bump("mm")

                def f_axd(e, n=prev_txd_x, g=xd_store_gate):
                    e.wait_ge(S["mm"], n)
                    W(e, g)
                    e.activation(xdTout[:], pA[:, :], AF.Identity)
                    e.drain().then_inc(S["act_t"], 1)
                ev("scalar", f_axd)
                gA = ("act_t", bump("act_t"))
                n_xdT = gA[1]

                def f_sxd(e, tau=tau, n=n_xdT):
                    e.wait_ge(S["act_t"], n)
                    e.dma_start(out=xdscr[tau, :, :], in_=xdTout[:]).then_inc(S["sxd"], 16)
                ev("sync", f_sxd)
                xd_store_gate = ("sxd", bump("sxd", 16))

                states_prev = ("mm", val("mm"))
                for rk, wnm, bank in (("k0", "wk0", pB), ("v0", "wv0", pA),
                                      ("k1", "wk1", pB), ("v1", "wv1", pA)):
                    gbank = gB if bank is pB else gA

                    kseg = {"k0": 0, "v0": 1, "k1": 2, "v1": 3}[rk]

                    def f_tr(e, wnm=wnm, bank=bank, slot=slot, nx=n_xnT, g=gbank,
                             kseg=kseg):
                        e.wait_ge(S["act_t"], nx)
                        W(e, g)
                        e.wait_ge(S["gp"], n_rowsbf)
                        for ct in range(8):
                            lhs = xnT[:, ct * 512 + slot * 128:
                                      ct * 512 + slot * 128 + 128]
                            for j in range(2):
                                e.matmul(bank[:, j * 512:(j + 1) * 512], lhs,
                                         slab[wnm][:, ct * C + j * 512:
                                                   ct * C + (j + 1) * 512],
                                         start=(ct == 0), stop=False)
                        for j in range(2):
                            e.matmul(bank[:, j * 512:(j + 1) * 512], onesb[0:1, :],
                                     krow[0:1, kseg * C + j * 512:
                                          kseg * C + (j + 1) * 512],
                                     start=False, stop=True)
                        e.drain().then_inc(S["mme"], 1)
                    ev("tensor", f_tr)
                    n_round = bump("mme")

                    def f_ake(e, rk=rk, bank=bank, n=n_round, g=states_prev):
                        e.wait_ge(S["mme"], n)
                        W(e, g)
                        e.activation(ks_sb[rk][:], bank[:, :], AF.Identity,
                                     bias=0.0, scale=csts[:, 5:6])
                        e.drain().then_inc(S["free"], 1)
                        e.drain().then_inc(S["ac"], 1)
                    ev("scalar", f_ake)
                    nf = bump("free")
                    if bank is pB:
                        gB = ("free", nf)
                    else:
                        gA = ("free", nf)
                    n_ks = bump("ac")

                    if rk in ("v0", "v1"):
                        k = "0" if rk == "v0" else "1"
                        pS = pK0 if k == "0" else pK1

                        def f_ts(e, k=k, pS=pS, tau=tau, n=n_ks):
                            e.wait_ge(S["ac"], n)
                            for h in range(H):
                                e.matmul(pS[0:64, h * 64:(h + 1) * 64],
                                         ks_sb["k" + k][:, h * 64:(h + 1) * 64],
                                         ks_sb["v" + k][:, h * 64:(h + 1) * 64],
                                         start=(tau == 0 and h % 8 == 0),
                                         stop=(tau == nt - 1 and h % 8 == 7))
                            e.drain().then_inc(S["mm"], 1)
                        ev("tensor", f_ts)
                        bump("mm")

                if slot == 3 or tau == nt - 1:
                    span0 = tau - slot
                    nsp = slot + 1
                    for half in range(2):
                        def f_tq(e, half=half, nsp=nsp, ga=gA, gb=gB):
                            W(e, ga)
                            W(e, gb)
                            for dt in range(4):
                                d0 = half * 4 + dt
                                bankq = pA if dt < 2 else pB
                                for ct in range(8):
                                    lhs = slab["wq"][:, ct * C + d0 * 128:
                                                     ct * C + d0 * 128 + 128]
                                    mv = xnT[:, ct * 512: ct * 512 + nsp * 128]
                                    e.matmul(bankq[:, (dt % 2) * 512:
                                                   (dt % 2) * 512 + nsp * 128],
                                             lhs, mv, start=(ct == 0), stop=(ct == 7))
                            e.drain().then_inc(S["mme"], 1)
                        ev("tensor", f_tq)
                        n_q = bump("mme")

                        def f_aq(e, half=half, nsp=nsp, n=n_q, g=q_store_gate):
                            e.wait_ge(S["mme"], n)
                            W(e, g)
                            for dt in range(4):
                                d0 = half * 4 + dt
                                bankq = pA if dt < 2 else pB
                                e.activation(qevac[:, dt * 512:dt * 512 + nsp * 128],
                                             bankq[:, (dt % 2) * 512:
                                                   (dt % 2) * 512 + nsp * 128],
                                             AF.Identity, bias=bq_fm[:, d0:d0 + 1])
                            e.drain().then_inc(S["free"], 2)
                            e.drain().then_inc(S["act_t"], 1)
                        ev("scalar", f_aq)
                        nf = bump("free", 2)
                        gA = ("free", nf)
                        gB = ("free", nf)
                        n_qe = bump("act_t")

                        def f_sq(e, half=half, span0=span0, nsp=nsp, n=n_qe,
                                 b=b, spi=tau // 4):
                            e.wait_ge(S["act_t"], n)
                            if debug_stats and half == 0:
                                e.dma_start(out=bass.AP(dbgx_h.tensor,
                                            (b * (nt // 4 + 1) + spi) * 128 * 4096,
                                            [[4096, 128], [1, 4096]]),
                                            in_=xnT[:]).then_inc(S["sodd"], 16)
                            if debug_stats:
                                e.dma_start(out=bass.AP(dbgv_h.tensor,
                                            ((b * (nt // 4 + 1) + spi) * 2 + half) * 128 * 2048,
                                            [[2048, 128], [1, 2048]]),
                                            in_=qevac[:]).then_inc(S["sodd"], 16)
                            for t in range(nsp):
                                dst = bass.AP(qscr.tensor,
                                              (span0 + t) * 128 * 1024 + half * 4 * 128,
                                              [[1024, 128], [128, 4], [1, 128]])
                                srcap = bass.AP(qevac, t * 128,
                                                [[4 * 512, 128], [512, 4], [1, 128]])
                                e.dma_start(out=dst, in_=srcap).then_inc(S["sq"], 16)
                        ev("sync", f_sq)
                        if debug_stats:
                            bump("sodd", 16 if half else 32)
                        q_store_gate = ("sq", bump("sq", 16 * nsp))
                    xnT_gate = ("mme", val("mme"))

        stages[f"phase1_b{b}"] = len(events)
        n_states_done = val("mm")
        og = (out_gate[0], out_gate[1])

        def f_dst(e, n=n_states_done, og=og):
            e.wait_ge(S["mm"], n)
            for g in og:
                W(e, g)
            e.memset(state_sb[:], 0.0)
            e.tensor_copy(kst[0:64, :], pK0[0:64, :])
            ev_src0 = bass.AP(kst, 0, [[C, 64], [128, 8], [1, 64]])
            ev_src1 = bass.AP(pK1, 0, [[C, 64], [128, 8], [1, 64]])
            ev_dst = bass.AP(state_sb, 0, [[8 * 128, 64], [128, 8], [1, 64]])
            e.tensor_tensor(ev_dst, ev_src0, ev_src1, ALU.mult)
            od_src0 = bass.AP(kst, 64, [[C, 64], [128, 8], [1, 64]])
            od_src1 = bass.AP(pK1, 64, [[C, 64], [128, 8], [1, 64]])
            e.tensor_tensor(oddst[:].rearrange("p (g j) -> p g j", j=64),
                            od_src0, od_src1, ALU.mult)
            e.drain().then_inc(S["dv"], 1)
        ev("vector", f_dst)
        n_odd = bump("dv")

        def f_sodd(e, n=n_odd):
            e.wait_ge(S["dv"], n)
            dst = bass.AP(state_sb, 64 * (8 * 128) + 64,
                          [[8 * 128, 64], [128, 8], [1, 64]])
            e.dma_start(out=dst, in_=oddst[:]).then_inc(S["sodd"], 16)
        ev("sync", f_sodd)
        n_stdma = bump("sodd", 16)

        def f_dsrs(e, n=n_stdma):
            e.wait_ge(S["sodd"], n)
            sg = state_sb[:].rearrange("p (g j) -> p g j", j=128)
            e.tensor_reduce(srs_sb[:].unsqueeze(2), sg, AX.X, ALU.add)
            e.drain().then_inc(S["dvt"], 1)
        ev("vector", f_dsrs)
        n_srs = bump("dvt")

        def f_asrs(e, n=n_srs):
            e.wait_ge(S["dvt"], n)
            e.activation(srs_bf[:], srs_sb[:], AF.Identity)
            e.drain().then_inc(S["ac"], 1)
        ev("scalar", f_asrs)
        state_ready = ("ac", bump("ac"))

        stages[f"states_b{b}"] = len(events)
        prev_norm = None
        sq_all = ("sq", val("sq"))
        sxd_all = ("sxd", val("sxd"))
        for tau in range(nt):
            stages[f"p2_b{b}_t{tau}"] = len(events)
            buf = tau % 2
            psem = "p0" if buf == 0 else "p1"

            def f_s2(e, b=b, tau=tau, buf=buf, psem=psem,
                     gx=tuple(x_gate[buf]), gs=stage_gate[buf],
                     gq=sq_all, gxd=sxd_all):
                for g in gx:
                    W(e, g)
                W(e, gs)
                W(e, gq)
                W(e, gxd)
                src = bass.AP(x_h.tensor, (b * seq + tau * 128) * C,
                              [[C, 128], [1, C]])
                e.dma_start(out=xtile[buf][:], in_=src).then_inc(S[psem], 16)
                e.dma_start(out=qstage[buf][:], in_=qscr[tau, :, :]).then_inc(S[psem], 16)
                e.dma_start(out=xdstage[buf][:], in_=xdscr[tau, :, :]).then_inc(S[psem], 16)
            ev("sync", f_s2)
            n_ld2 = bump(psem, 48)

            def f_tam(e, buf=buf, n=n_ld2, psem=psem, ns=state_ready,
                      pn=prev_norm, ga=gA):
                e.wait_ge(S[psem], n)
                W(e, ns)
                W(e, pn)
                W(e, ga)
                for j in range(8):
                    e.matmul(pA[:, j * 128:(j + 1) * 128],
                             state_sb[:, j * 128:(j + 1) * 128],
                             qstage[buf][:, j * 128:(j + 1) * 128],
                             start=True, stop=True)
                for j in range(8):
                    e.matmul(pK0[0:1, 0:128], srs_bf[:, j:j + 1],
                             qstage[buf][:, j * 128:(j + 1) * 128],
                             start=(j == 0), stop=(j == 7))
                e.drain().then_inc(S["mm"], 1)
            ev("tensor", f_tam)
            n_am = bump("mm")

            def f_asq(e, n=n_am):
                e.wait_ge(S["mm"], n)
                e.activation(sq_sb[:], pA[:, :], AF.Square)
                e.drain().then_inc(S["act_t"], 1)
            ev("scalar", f_asq)
            n_sq2 = bump("act_t")

            def f_tsq(e, n=n_sq2):
                e.wait_ge(S["act_t"], n)
                for j in range(8):
                    e.matmul(pK0[0:1, 128:256], onescol[:, 0:1],
                             sq_sb[:, j * 128:(j + 1) * 128],
                             start=(j == 0), stop=(j == 7))
                e.drain().then_inc(S["mm"], 1)
            ev("tensor", f_tsq)
            n_sums = bump("mm")

            def f_dstat(e, n=n_sums):
                e.wait_ge(S["mm"], n)
                e.tensor_scalar(mrow[:], pK0[0:1, 0:128], 1.0 / C, None, ALU.mult)
                e.tensor_tensor(m2row[:], mrow[:], mrow[:], ALU.mult)
                e.scalar_tensor_tensor(vrow[:], pK0[0:1, 128:256], 1.0 / C,
                                       m2row[:], ALU.mult, ALU.subtract)
                e.tensor_scalar(vrow[:], vrow[:], 1e-5, None, ALU.add)
                e.reciprocal(rqrow[:], vrow[:])
                e.tensor_copy(mrow_bf[:], mrow[:])
                e.drain().then_inc(S["dv"], 1)
            ev("vector", f_dstat)
            n_var = bump("dv")

            def f_ar(e, n=n_var):
                e.wait_ge(S["dv"], n)
                e.activation(rrow[:], rqrow[:], AF.Sqrt)
                e.drain().then_inc(S["ac"], 1)
            ev("scalar", f_ar)
            n_rr = bump("ac")

            def f_tbc(e, n=n_rr):
                e.wait_ge(S["ac"], n)
                e.matmul(pK0[:, 256:384], onesb[0:1, :], mrow_bf[:],
                         start=True, stop=True)
                e.matmul(pK0[:, 384:512], onesb[0:1, :], rrow[:],
                         start=True, stop=True)
                e.drain().then_inc(S["mm"], 1)
            ev("tensor", f_tbc)
            n_bc = bump("mm")

            def f_abc(e, n=n_bc):
                e.wait_ge(S["mm"], n)
                e.activation(mtile2[:], pK0[:, 256:512], AF.Identity)
                e.drain().then_inc(S["act_t"], 1)
            ev("scalar", f_abc)
            n_bc2 = bump("act_t")

            def f_dn(e, n=n_bc2):
                e.wait_ge(S["act_t"], n)
                g3 = lambda ap: ap.rearrange("p (g j) -> p g j", j=128)
                mt = mtile2[:, 0:128].unsqueeze(1).broadcast_to([128, 8, 128])
                rt2 = mtile2[:, 128:256].unsqueeze(1).broadcast_to([128, 8, 128])
                e.tensor_tensor(g3(qt[:]), g3(pA[:, :]), mt, ALU.subtract)
                e.tensor_tensor(g3(ln_sb[:]), g3(qt[:]), rt2, ALU.mult)
                e.drain().then_inc(S["dvt"], 1)
            ev("vector", f_dn)
            prev_norm = ("dvt", bump("dvt"))
            gA = prev_norm
            if debug_stats:
                def f_dbg(e, b=b, tau=tau, buf=buf, n=prev_norm[1], n2=n_sq2):
                    e.wait_ge(S["dvt"], n)
                    e.dma_start(out=bass.AP(dbg_h.tensor,
                                            (b * nt + tau) * 128 * 256,
                                            [[256, 128], [1, 256]]),
                                in_=mtile2[:]).then_inc(S["sodd"], 16)
                    e.dma_start(out=bass.AP(dbgq_h.tensor,
                                            (b * nt + tau) * 128 * 1024,
                                            [[1024, 128], [1, 1024]]),
                                in_=qstage[buf][:]).then_inc(S["sodd"], 16)
                    e.dma_start(out=bass.AP(dbgs_h.tensor,
                                            (b * nt + tau) * 128 * 1024,
                                            [[1024, 128], [1, 1024]]),
                                in_=sq_sb[:]).then_inc(S["sodd"], 16)
                ev("sync", f_dbg)
                bump("sodd", 48)

            def f_to(e, buf=buf, n=prev_norm[1], g=gB):
                e.wait_ge(S["dvt"], n)
                W(e, g)
                for ct in range(8):
                    for j in range(2):
                        e.matmul(pB[:, j * 512:(j + 1) * 512],
                                 ln_sb[:, ct * 128:(ct + 1) * 128],
                                 slab["wo"][:, ct * C + j * 512:
                                           ct * C + (j + 1) * 512],
                                 start=(ct == 0), stop=False)
                for ct in range(8):
                    for j in range(2):
                        e.matmul(pB[:, j * 512:(j + 1) * 512],
                                 xdstage[buf][:, ct * 128:(ct + 1) * 128],
                                 slab["wdt"][:, ct * C + j * 512:
                                            ct * C + (j + 1) * 512],
                                 start=False, stop=False)
                for j in range(2):
                    e.matmul(pB[:, j * 512:(j + 1) * 512], onesb[0:3, :],
                             rows3[0:3, j * 512:(j + 1) * 512],
                             start=False, stop=True)
                e.drain().then_inc(S["mme"], 1)
            ev("tensor", f_to)
            n_out = bump("mme")
            stage_gate[buf] = ("mme", n_out)

            def f_do(e, buf=buf, n=n_out, g=out_gate[buf]):
                e.wait_ge(S["mme"], n)
                W(e, g)
                e.scalar_tensor_tensor(out_sb[buf][:], pB[:, :], 1.0,
                                       xtile[buf][:], ALU.mult, ALU.add)
                e.drain().then_inc(S["free"], 1)
                e.drain().then_inc(S["dv"], 1)
            ev("vector", f_do)
            gB = ("free", bump("free"))
            n_o = bump("dv")
            x_gate[buf] = [("dv", n_o)]

            osem = "so0" if buf == 0 else "so1"

            def f_so(e, b=b, tau=tau, buf=buf, n=n_o, osem=osem):
                e.wait_ge(S["dv"], n)
                dst = bass.AP(y_h.tensor, (b * seq + tau * 128) * C,
                              [[C, 128], [1, C]])
                e.dma_start(out=dst, in_=out_sb[buf][:]).then_inc(S[osem], 16)
            ev("sync", f_so)
            out_gate[buf] = (osem, bump(osem, 16))

    stages["all"] = len(events)
    import os as _os
    if _os.environ.get("PRINT_STAGES"):
        print("STAGES:", stages)
    if stop_stage is not None:
        cut = stop_stage if isinstance(stop_stage, int) else stages[stop_stage]
        del events[cut:]
    for name in ("vector", "scalar", "tensor", "gpsimd"):
        def f_fin(e):
            e.drain().then_inc(S["done"], 1)
        ev(name, f_fin)
    n_done = bump("done", 4)
    final_counts = dict(cnt)

    def f_fins(e):
        if stop_stage is None:
            for nm in ("so0", "so1", "sxd", "sq", "sodd"):
                e.wait_ge(S[nm], final_counts[nm])
        e.wait_ge(S["done"], n_done)
        if debug_tail:
            ntail = 0
            fslots = [xn, qa, out_sb[0], out_sb[1], xtile[0], xtile[1]]
            for i, t in enumerate(fslots):
                e.dma_start(out=bass.AP(tailf_h.tensor, i * 128 * 1024,
                                        [[1024, 128], [1, 1024]]),
                            in_=t[:]).then_inc(S["wld"], 16)
                ntail += 16
            for j, nm in enumerate(("sx", "sxx", "mneg", "m2", "u", "sig", "rstd")):
                e.dma_start(out=bass.AP(tailf_h.tensor, 6 * 128 * 1024 + j,
                                        [[1024, 128], [1, 1]]),
                            in_=ln1[nm][:],
                            allow_slow_non_contiguous=True).then_inc(S["wld"], 16)
                ntail += 16
            for j, t in enumerate((gmax, s0, rcp, rs_t, ss_t)):
                e.dma_start(out=bass.AP(tailf_h.tensor,
                                        6 * 128 * 1024 + 32 + j * 32,
                                        [[1024, 128], [1, 32]]),
                            in_=t[:],
                            allow_slow_non_contiguous=True).then_inc(S["wld"], 16)
                ntail += 16
            e.dma_start(out=bass.AP(tailf_h.tensor, 7 * 128 * 1024,
                                    [[1024, 128], [1, 256]]),
                        in_=mtile2[:],
                        allow_slow_non_contiguous=True).then_inc(S["wld"], 16)
            ntail += 16
            bslots = [state_sb, ks_sb["k0"], ks_sb["v0"], ks_sb["k1"],
                      ks_sb["v1"], None, None, None, None, qstage[0],
                      xdstage[0], sq_sb, ln_sb, None, None]
            for i, t in enumerate(bslots):
                if t is None:
                    continue
                e.dma_start(out=bass.AP(tailb_h.tensor, i * 128 * 1024,
                                        [[1024, 128], [1, 1024]]),
                            in_=t[:]).then_inc(S["wld"], 16)
                ntail += 16
            for j in range(4):
                e.dma_start(out=bass.AP(tailb_h.tensor, (5 + j) * 128 * 1024,
                                        [[1024, 128], [1, 1024]]),
                            in_=xnT[:, j * 1024:(j + 1) * 1024]).then_inc(S["wld"], 16)
                ntail += 16
            e.dma_start(out=bass.AP(tailb_h.tensor, 13 * 128 * 1024,
                                    [[1024, 128], [1, 1024]]),
                        in_=slab["wdt"][:, 0:1024]).then_inc(S["wld"], 16)
            ntail += 16
            e.dma_start(out=bass.AP(tailb_h.tensor, 14 * 128 * 1024,
                                    [[1024, 64], [1, 512]]),
                        in_=oddst[:],
                        allow_slow_non_contiguous=True).then_inc(S["wld"], 16)
            ntail += 16
            for tt in range(nt):
                e.dma_start(out=bass.AP(tailq_h.tensor, tt * 128 * 1024,
                                        [[1024, 128], [1, 1024]]),
                            in_=qscr[tt, :, :]).then_inc(S["wld"], 16)
                e.dma_start(out=bass.AP(tailq_h.tensor,
                                        (nt + tt) * 128 * 1024,
                                        [[1024, 128], [1, 1024]]),
                            in_=xdscr[tt, :, :]).then_inc(S["wld"], 16)
                ntail += 32
            e.wait_ge(S["wld"], final_counts["wld"] + ntail)
        for s in S.values():
            e.sem_clear(s)
    ev("sync", f_fins)

    with nc.Block() as block:
        @block.sync
        def _(eng):
            for name, fn in events:
                if name == "sync":
                    fn(eng)

        @block.vector
        def _(eng):
            for name, fn in events:
                if name == "vector":
                    fn(eng)

        @block.scalar
        def _(eng):
            for name, fn in events:
                if name == "scalar":
                    fn(eng)

        @block.tensor
        def _(eng):
            for name, fn in events:
                if name == "tensor":
                    fn(eng)

        @block.gpsimd
        def _(eng):
            for name, fn in events:
                if name == "gpsimd":
                    fn(eng)

    stk.close()
    return nc


def make_base_inputs(Wq, bq, Wk, bk, Wv, bv, Wo, bo, ln_g, ln_b, Wffn, bffn):
    brows = np.stack([
        np.asarray(bq, np.float32), np.asarray(bk[0], np.float32),
        np.asarray(bk[1], np.float32), np.asarray(bv[0], np.float32),
        np.asarray(bv[1], np.float32), np.asarray(bo, np.float32),
        np.asarray(bffn, np.float32), np.asarray(ln_b, np.float32),
    ]).astype(np.float32)
    return {
        "wq": np.ascontiguousarray(np.asarray(Wq, np.float32)),
        "wk0": np.ascontiguousarray(np.asarray(Wk[0], np.float32)),
        "wk1": np.ascontiguousarray(np.asarray(Wk[1], np.float32)),
        "wv0": np.ascontiguousarray(np.asarray(Wv[0], np.float32)),
        "wv1": np.ascontiguousarray(np.asarray(Wv[1], np.float32)),
        "wo": np.ascontiguousarray(np.asarray(Wo, np.float32)),
        "wffn": np.ascontiguousarray(np.asarray(Wffn, np.float32)),
        "brows": brows,
        "bq_fm": np.ascontiguousarray(np.asarray(bq, np.float32).reshape(8, 128).T),
        "g_fm": np.ascontiguousarray(np.asarray(ln_g, np.float32).reshape(8, 128).T),
        "b_colb": np.ascontiguousarray(
            np.asarray(ln_b, np.float32).reshape(8, 128).T.astype(BF)),
        "eyef": np.eye(128, dtype=np.float32),
        "eyeb": np.eye(128, dtype=BF),
        "onesb": np.ones((3, 128), dtype=BF),
        "onesf": np.ones((1, 128), dtype=np.float32),
        "onescol": np.ones((128, 1), dtype=BF),
        "csts": np.tile(np.array([1.0 / 7.5, 1e-30, 8.0, 0.125, 1.0 / 1024,
                                  1.0 / 64, 1e-5, 0],
                                 dtype=np.float32), (128, 1)),
    }


def kernel(L0_attn_x, L0_attn_Wq, L0_attn_bq, L0_attn_Wk, L0_attn_bk,
           L0_attn_Wv, L0_attn_bv, L0_attn_Wo, L0_attn_bo,
           L0_attn_ln_g, L0_attn_ln_b, L0_ffn_weight, L0_ffn_bias):
    x = np.ascontiguousarray(np.asarray(L0_attn_x, dtype=np.float32))
    B = x.shape[0]
    ncores = 8
    per = B // ncores
    nc = build_program(nb=per, nt=NT)
    base = make_base_inputs(L0_attn_Wq, L0_attn_bq, L0_attn_Wk, L0_attn_bk,
                            L0_attn_Wv, L0_attn_bv, L0_attn_Wo, L0_attn_bo,
                            L0_attn_ln_g, L0_attn_ln_b, L0_ffn_weight,
                            L0_ffn_bias)
    in_maps = []
    for c in range(ncores):
        m = dict(base)
        m["x"] = np.ascontiguousarray(x[c * per:(c + 1) * per])
        in_maps.append(m)
    res = run_bass_kernel_spmd(nc, in_maps, core_ids=list(range(ncores)))
    out = np.concatenate([np.asarray(r["y"]) for r in res.results], axis=0)
    return out.astype(np.float32)


# version 05
# speedup vs baseline: 1.0412x; 1.0017x over previous
import numpy as np
import ml_dtypes
import concourse.bass as bass
import concourse.mybir as mybir
from concourse.bass_utils import run_bass_kernel_spmd
from contextlib import ExitStack

F32 = mybir.dt.float32
BF16 = mybir.dt.bfloat16
I32 = mybir.dt.int32
AF = mybir.ActivationFunctionType
ALU = mybir.AluOpType
AX = mybir.AxisListType
BF = ml_dtypes.bfloat16

C = 1024
H = 16
NT = 16
SCALE = 1.0 / 64.0
BOOT = 64


def build_program(nb=4, nt=NT, stop_stage=None, detect_races=False,
                  debug_stats=False, debug_tail=False):
    nc = bass.Bass("TRN2", target_bir_lowering=False, detect_race_conditions=detect_races)
    seq = nt * 128

    x_h = nc.dram_tensor("x", [nb, seq, C], F32, kind="ExternalInput").ap()
    w_h = {nm: nc.dram_tensor(nm, [C, C], F32, kind="ExternalInput").ap()
           for nm in ("wq", "wk0", "wk1", "wv0", "wv1", "wo")}
    wffn_h = nc.dram_tensor("wffn", [C, C], F32, kind="ExternalInput").ap()
    rows_h = nc.dram_tensor("brows", [8, C], F32, kind="ExternalInput").ap()
    bq_fm_h = nc.dram_tensor("bq_fm", [128, 8], F32, kind="ExternalInput").ap()
    g_fm_h = nc.dram_tensor("g_fm", [128, 8], F32, kind="ExternalInput").ap()
    bcol_h = nc.dram_tensor("b_colb", [128, 8], F32, kind="ExternalInput").ap()
    eyef_h = nc.dram_tensor("eyef", [128, 128], F32, kind="ExternalInput").ap()
    eyeb_h = nc.dram_tensor("eyeb", [128, 128], BF16, kind="ExternalInput").ap()
    onesb_h = nc.dram_tensor("onesb", [3, 128], BF16, kind="ExternalInput").ap()
    onesf_h = nc.dram_tensor("onesf", [1, 128], F32, kind="ExternalInput").ap()
    onescol_h = nc.dram_tensor("onescol", [128, 1], BF16, kind="ExternalInput").ap()
    csts_h = nc.dram_tensor("csts", [128, 8], F32, kind="ExternalInput").ap()
    y_h = nc.dram_tensor("y", [nb, seq, C], F32, kind="ExternalOutput").ap()
    qscr = nc.dram_tensor("qscr", [nt, 128, 8 * 128], BF16).ap()
    dbg_h = (nc.dram_tensor("dbg", [nb, nt, 128, 256], F32, kind="ExternalOutput").ap()
             if debug_stats else None)
    dbgq_h = (nc.dram_tensor("dbgq", [nb, nt, 128, 1024], BF16, kind="ExternalOutput").ap()
              if debug_stats else None)
    dbgs_h = (nc.dram_tensor("dbgs", [nb, nt, 128, 1024], BF16, kind="ExternalOutput").ap()
              if debug_stats else None)
    dbgv_h = (nc.dram_tensor("dbgv", [nb * (nt // 4 + 1), 2, 128, 2048], BF16,
                             kind="ExternalOutput").ap() if debug_stats else None)
    dbgx_h = (nc.dram_tensor("dbgx", [nb * (nt // 4 + 1), 128, 4096], BF16,
                             kind="ExternalOutput").ap() if debug_stats else None)
    dbgl_h = (nc.dram_tensor("dbgl", [nb, nt, 128, 2], F32,
                             kind="ExternalOutput").ap() if debug_stats else None)
    xdscr = nc.dram_tensor("xdscr", [nt, 128, 8 * 128], BF16).ap()
    tailf_h = (nc.dram_tensor("tailf", [8, 128, 1024], F32, kind="ExternalOutput").ap()
               if debug_tail else None)
    tailb_h = (nc.dram_tensor("tailb", [15, 128, 1024], BF16, kind="ExternalOutput").ap()
               if debug_tail else None)
    tailp_h = (nc.dram_tensor("tailp", [3, 128, 1024], F32, kind="ExternalOutput").ap()
               if debug_tail else None)
    tailq_h = (nc.dram_tensor("tailq", [2, nt, 128, 1024], BF16, kind="ExternalOutput").ap()
               if debug_tail else None)

    stk = ExitStack()
    sbt = lambda n, s, d: stk.enter_context(nc.sbuf_tensor(n, s, d))
    psa = lambda n, s: stk.enter_context(nc.psum_tensor(n, s, F32))
    mksem = lambda n: stk.enter_context(nc.semaphore(n))

    slab = {nm: sbt(nm + "_s", [128, 8 * C], BF16) for nm in
            ("wq", "wk0", "wk1", "wv0", "wv1", "wo", "wdt")}
    rows3 = sbt("rows3", [3, C], BF16)
    bq_fm = sbt("bqfm", [128, 8], F32)
    g_fm = sbt("gfm", [128, 8], F32)
    bcol_bf = sbt("bcolb", [128, 8], BF16)
    eyef = sbt("eyefs", [128, 128], F32)
    eyeb = sbt("eyebs", [128, 128], BF16)
    onesb = sbt("onesbs", [3, 128], BF16)
    onesf = sbt("onesfs", [1, 128], F32)
    onescol = sbt("onescols", [128, 1], BF16)
    csts = sbt("cstss", [128, 8], F32)
    krow = sbt("krow", [1, 4 * C], BF16)
    rc75d = sbt("rc75d", [128, 1], F32)
    rc75A = sbt("rc75A", [128, 1], F32)

    xtile = [sbt(f"xt{i}", [128, C], F32) for i in range(2)]
    xn = sbt("xn", [128, C], F32)
    xnT = sbt("xnT", [128, 4 * 8 * 128], BF16)
    ks_sb = {k: sbt("ks_" + k, [128, C], BF16) for k in ("k0", "k1", "v0", "v1")}
    qt = sbt("q_t", [128, C], F32)
    qa = sbt("q_a", [128, C], F32)
    qb = sbt("q_b", [128, C], F32)
    qw = sbt("q_w", [128, C], F32)
    qit = sbt("q_it", [128, C], I32)
    gmax = sbt("gmaxs", [128, 64], F32)
    s0 = sbt("s0s", [128, 64], F32)
    rcp = sbt("rcps", [128, 64], F32)
    rs_t = sbt("rss", [128, 64], F32)
    ss_t = sbt("sss", [128, 64], F32)
    sq_dump = sbt("sqdump", [128, C], BF16)
    ln1 = {n: sbt("l_" + n, [128, 2], F32) for n in
           ("sx", "sxx", "mneg", "m2", "u", "sig", "rinv", "rstd", "mnegA")}
    state_sb = sbt("state", [128, 8 * 128], BF16)
    srs_sb = sbt("srs", [128, 8], F32)
    srs_bf = sbt("srsbf", [128, 8], BF16)
    oddst = sbt("oddst", [64, 8 * 64], BF16)
    qstage = [sbt(f"qstage{i}", [128, 8 * 128], BF16) for i in range(2)]
    xdstage = [sbt(f"xdstage{i}", [128, 8 * 128], BF16) for i in range(2)]
    xdTout = sbt("xdTout", [128, 8 * 128], BF16)
    sq_sb = sbt("sqsb", [128, 8 * 128], BF16)
    ln_sb = sbt("lnsb", [128, 8 * 128], BF16)
    mrow = sbt("mrow", [1, 128], F32)
    mrow_bf = sbt("mrowbf", [1, 128], BF16)
    rrow = sbt("rrow", [1, 128], BF16)
    vrow = sbt("vrow", [1, 128], F32)
    m2row = sbt("m2row", [1, 128], F32)
    rqrow = sbt("rqrow", [1, 128], F32)
    mtile2 = sbt("mtile2", [128, 256], F32)
    out_sb = [sbt(f"outsb{i}", [128, C], F32) for i in range(2)]
    qevac = sbt("qevac", [128, 4 * 512], BF16)
    wfstage = out_sb
    kst = out_sb[0]

    pA = psa("pA", [128, C])
    pB = psa("pB", [128, C])
    pK0 = psa("pK0", [128, C])
    pK1 = psa("pK1", [128, C])

    semnames = ("x0", "x1", "x2", "p0", "p1", "wf0", "wf1", "wld", "sodd",
                "sxd", "sq", "so0", "so1",
                "dv", "ac", "dvt", "act_t", "mm", "mme", "free", "gp", "done")
    S = {n: mksem("s_" + n) for n in semnames}
    cnt = {n: 0 for n in semnames}
    events = []
    stages = {}

    def ev(engine, fn):
        events.append((engine, fn))

    def bump(name, by=1):
        cnt[name] += by
        return cnt[name]

    def val(name):
        return cnt[name]

    def W(e, gate):
        if gate is not None and gate[1] > 0:
            e.wait_ge(S[gate[0]], gate[1])

    def f_sync0(e):
        for s in S.values():
            e.sem_clear(s)
        e.sem_inc(S["done"], BOOT)
        for dst, src in ((bq_fm, bq_fm_h), (g_fm, g_fm_h),
                         (eyef, eyef_h), (eyeb, eyeb_h),
                         (onesb, onesb_h), (onesf, onesf_h),
                         (onescol, onescol_h), (csts, csts_h)):
            e.dma_start(out=dst[:], in_=src[:]).then_inc(S["wld"], 16)
    ev("sync", f_sync0)
    bump("done", BOOT)
    n_consts = bump("wld", 128)

    def f_gp0(e):
        e.wait_ge(S["done"], BOOT)
        for nm in ("wk0", "wv0", "wk1", "wv1", "wq", "wo"):
            src = bass.AP(w_h[nm].tensor, 0, [[C, 128], [128 * C, 8], [1, C]])
            e.dma_start(out=slab[nm][:], in_=src).then_inc(S["gp"], 16)
        e.dma_start(out=rows3[1:2, :],
                    in_=bass.AP(rows_h.tensor, 5 * C, [[C, 1], [1, C]])).then_inc(S["gp"], 16)
        e.dma_start(out=rows3[2:3, :],
                    in_=bass.AP(rows_h.tensor, 6 * C, [[C, 1], [1, C]])).then_inc(S["gp"], 16)
        e.dma_start(out=bcol_bf[:], in_=bcol_h[:]).then_inc(S["gp"], 16)
        for i, r in enumerate((1, 3, 2, 4)):
            e.dma_start(out=krow[0:1, i * C:(i + 1) * C],
                        in_=bass.AP(rows_h.tensor, r * C,
                                    [[C, 1], [1, C]])).then_inc(S["gp"], 16)
    ev("gpsimd", f_gp0)
    n_wslabs = bump("gp", 96)
    n_rowsbf = bump("gp", 112)

    def f_dve0(e):
        e.wait_ge(S["done"], BOOT)
        e.wait_ge(S["wld"], n_consts)
        e.wait_ge(S["gp"], n_rowsbf)
        e.reciprocal(rc75d[:], csts[:, 0:1])
        for nm in ("wq", "wk0", "wk1", "wv0", "wv1"):
            for ct in range(8):
                d0 = ct * C + ct * 128
                e.tensor_tensor(slab[nm][:, d0:d0 + 128],
                                slab[nm][:, d0:d0 + 128], eyeb[:], ALU.add)
        e.drain().then_inc(S["dvt"], 1)
    ev("vector", f_dve0)
    n_wplusI = bump("dvt")

    def f_t0(e):
        e.wait_ge(S["done"], BOOT)
        e.wait_ge(S["gp"], n_rowsbf)
        e.wait_ge(S["dvt"], n_wplusI)
        for j in range(2):
            for ct in range(8):
                e.matmul(pA[0:1, j * 512:(j + 1) * 512], bcol_bf[:, ct:ct + 1],
                         slab["wo"][:, ct * C + j * 512: ct * C + (j + 1) * 512],
                         start=(ct == 0), stop=(ct == 7))
        e.drain().then_inc(S["mm"], 1)
    ev("tensor", f_t0)
    n_bwo_mm = bump("mm")

    def f_a0(e):
        e.wait_ge(S["done"], BOOT)
        e.wait_ge(S["wld"], n_consts)
        e.wait_ge(S["gp"], n_rowsbf)
        e.wait_ge(S["mm"], n_bwo_mm)
        e.activation(rows3[0:1, :], pA[0:1, :], AF.Identity)
        e.activation(rc75A[:], rc75d[:], AF.Identity)
        e.drain().then_inc(S["act_t"], 1)
    ev("scalar", f_a0)
    gA = ("act_t", bump("act_t"))

    def f_awo(e, g=gA):
        e.wait_ge(S[g[0]], g[1])
        for ct in range(8):
            e.activation(slab["wo"][:, ct * C:(ct + 1) * C],
                         slab["wo"][:, ct * C:(ct + 1) * C],
                         AF.Identity, bias=0.0, scale=g_fm[:, ct:ct + 1])
        e.drain().then_inc(S["ac"], 1)
    ev("scalar", f_awo)
    n_wo_ready = bump("ac")

    gB = ("free", 0)

    def emit_scale_chain(off=0, n_gm=None):
        if n_gm is None:
            n_gm = bump("dv")
        sl = slice(off, off + 32)

        def f_ac(e, n=n_gm, sl=sl):
            e.wait_ge(S["dv"], n)
            e.activation(s0[:, sl], gmax[:, sl], AF.Identity,
                         bias=csts[:, 1:2], scale=rc75A[:])
            e.drain().then_inc(S["ac"], 1)
        ev("scalar", f_ac)
        n_s0 = bump("ac")

        def f_dr(e, n=n_s0, sl=sl):
            e.wait_ge(S["ac"], n)
            e.reciprocal(rcp[:, sl], s0[:, sl])
            e.drain().then_inc(S["dv"], 1)
        ev("vector", f_dr)
        n_rc = bump("dv")

        def f_ac2(e, n=n_rc, sl=sl):
            e.wait_ge(S["dv"], n)
            e.activation(rs_t[:, sl], rcp[:, sl], AF.Identity,
                         bias=0.0, scale=csts[:, 2:3])
            e.activation(ss_t[:, sl], s0[:, sl], AF.Identity,
                         bias=0.0, scale=csts[:, 3:4])
            e.drain().then_inc(S["ac"], 1)
        ev("scalar", f_ac2)
        return bump("ac")

    def emit_quant_dve(e, src, off=0):
        g3 = lambda ap: ap.rearrange("p (g j) -> p g j", j=32)
        rs_b = rs_t[:, off:off + 32].unsqueeze(2).broadcast_to([128, 32, 32])
        ss_b = ss_t[:, off:off + 32].unsqueeze(2).broadcast_to([128, 32, 32])
        e.tensor_tensor(g3(qt[:]), g3(src[:]), rs_b, ALU.mult)
        e.tensor_scalar(qa[:], qt[:], 16.0, -16.0, ALU.min, ALU.max)
        e.tensor_scalar(qb[:], qt[:], 32.0, -32.0, ALU.min, ALU.max)
        e.tensor_scalar(qw[:], qt[:], 0.25, 64.5, ALU.mult, ALU.add)
        e.scalar_tensor_tensor(qw[:], qb[:], 0.25, qw[:], ALU.mult, ALU.add)
        e.scalar_tensor_tensor(qw[:], qa[:], 0.5, qw[:], ALU.mult, ALU.add)
        e.tensor_copy(qit[:], qw[:])
        e.tensor_tensor(qa[:], qw[:], qit[:], ALU.is_ge)
        e.tensor_scalar(qb[:], qit[:], 1.0, None, ALU.add)
        e.tensor_tensor(qb[:], qw[:], qb[:], ALU.is_ge)
        e.tensor_tensor(qa[:], qa[:], qb[:], ALU.add)
        e.scalar_tensor_tensor(qt[:], qit[:], -65.0, qa[:], ALU.add, ALU.add)
        e.tensor_scalar(qa[:], qt[:], 24.0, -24.0, ALU.min, ALU.max)
        e.tensor_scalar(qb[:], qt[:], 16.0, -16.0, ALU.min, ALU.max)
        e.tensor_scalar(qw[:], qt[:], 4.0, None, ALU.mult)
        e.scalar_tensor_tensor(qw[:], qa[:], -2.0, qw[:], ALU.mult, ALU.add)
        e.scalar_tensor_tensor(qw[:], qb[:], -1.0, qw[:], ALU.mult, ALU.add)
        e.tensor_tensor(g3(qa[:]), g3(qw[:]), ss_b, ALU.mult)

    def f_swf(e):
        for j in range(2):
            src = bass.AP(wffn_h.tensor, j * 128 * C, [[C, 128], [1, C]])
            e.dma_start(out=wfstage[j][:], in_=src).then_inc(S["wf" + str(j)], 16)
    ev("sync", f_swf)
    bump("wf0", 16)
    bump("wf1", 16)

    prev_txd = 0
    for jt in range(8):
        wsem = "wf" + str(jt % 2)
        n_have = val(wsem)

        def f_dg(e, jt=jt, n=n_have, wsem=wsem):
            e.wait_ge(S[wsem], n)
            xg = wfstage[jt % 2][:].rearrange("p (g j) -> p g j", j=32)
            e.tensor_reduce(gmax[:, 0:32].unsqueeze(2), xg, AX.X, ALU.max,
                            apply_absolute_value=True)
            e.drain().then_inc(S["dv"], 1)
        ev("vector", f_dg)
        n_rs = emit_scale_chain()

        def f_dq(e, jt=jt, n=n_rs, g=prev_txd):
            e.wait_ge(S["ac"], n)
            if g:
                e.wait_ge(S["mm"], g)
            emit_quant_dve(e, wfstage[jt % 2])
            e.drain().then_inc(S["dvt"], 1)
        ev("vector", f_dq)
        n_xd = bump("dvt")

        if jt + 2 < 8:
            def f_sn(e, jt=jt, n=n_xd):
                e.wait_ge(S["dvt"], n)
                src = bass.AP(wffn_h.tensor, (jt + 2) * 128 * C, [[C, 128], [1, C]])
                e.dma_start(out=wfstage[jt % 2][:], in_=src).then_inc(
                    S["wf" + str(jt % 2)], 16)
            ev("sync", f_sn)
            bump("wf" + str(jt % 2), 16)

        def f_tt(e, n=n_xd, g=gA):
            e.wait_ge(S["dvt"], n)
            W(e, g)
            for ct in range(8):
                e.transpose(pA[:, ct * 128:(ct + 1) * 128],
                            qa[:, ct * 128:(ct + 1) * 128], eyef[:])
            e.drain().then_inc(S["mm"], 1)
        ev("tensor", f_tt)
        prev_txd = bump("mm")

        def f_ae(e, jt=jt, n=prev_txd):
            e.wait_ge(S["mm"], n)
            for ct in range(8):
                e.activation(slab["wdt"][:, ct * C + jt * 128: ct * C + jt * 128 + 128],
                             pA[:, ct * 128:(ct + 1) * 128], AF.Identity)
            e.drain().then_inc(S["act_t"], 1)
        ev("scalar", f_ae)
        gA = ("act_t", bump("act_t"))

    stages["startup_ffn"] = len(events)
    x_gate = [[], []]
    stage_gate = [None, None]
    out_gate = [None, None]
    xnT_gate = None
    xd_store_gate = None
    q_store_gate = None
    state_ready = None
    prev_txn = 0
    prev_txd_x = prev_txd

    for b in range(nb):
        for tp in range(nt // 2):
            pair = (2 * tp, 2 * tp + 1)
            for tau in pair:
                pb = tau % 2
                xb = tau % 2
                xsem = f"x{xb}"

                def f_sx(e, b=b, tau=tau, xb=xb, gs=tuple(x_gate[xb]),
                         xsem=xsem):
                    for g in gs:
                        W(e, g)
                    src = bass.AP(x_h.tensor, (b * seq + tau * 128) * C,
                                  [[C, 128], [1, C]])
                    e.dma_start(out=xtile[xb][:], in_=src).then_inc(S[xsem], 16)
                ev("sync", f_sx)
                n_x = bump(xsem, 16)

                def f_a1(e, xb=xb, pb=pb, n=n_x, xsem=xsem):
                    e.wait_ge(S[xsem], n)
                    e.activation(sq_dump[:], xtile[xb][:], AF.Square,
                                 accum_out=ln1["sxx"][:, pb:pb + 1])
                    e.activation(sq_dump[:], xtile[xb][:], AF.Identity,
                                 accum_out=ln1["sx"][:, pb:pb + 1])
                    e.drain().then_inc(S["ac"], 1)
                ev("scalar", f_a1)
                n_sxx = bump("ac")

            def f_d2(e, n=n_sxx):
                e.wait_ge(S["ac"], n)
                e.tensor_scalar(ln1["mneg"][:], ln1["sx"][:], -1.0 / C, None,
                                ALU.mult)
                e.drain().then_inc(S["dv"], 1)
            ev("vector", f_d2)
            n_mn = bump("dv")

            def f_a1b(e, n=n_mn):
                e.wait_ge(S["dv"], n)
                e.activation(ln1["m2"][:], ln1["mneg"][:], AF.Square)
                e.activation(ln1["mnegA"][:], ln1["mneg"][:], AF.Identity)
                e.drain().then_inc(S["ac"], 1)
            ev("scalar", f_a1b)
            n_m2 = bump("ac")

            def f_d2b(e, n=n_m2):
                e.wait_ge(S["ac"], n)
                e.scalar_tensor_tensor(ln1["u"][:], ln1["sxx"][:], 1.0 / C,
                                       ln1["m2"][:], ALU.mult, ALU.subtract)
                e.drain().then_inc(S["dv"], 1)
            ev("vector", f_d2b)
            n_u = bump("dv")

            def f_a2(e, n=n_u):
                e.wait_ge(S["dv"], n)
                e.activation(ln1["sig"][:], ln1["u"][:], AF.Sqrt,
                             bias=csts[:, 6:7], scale=1.0)
                e.drain().then_inc(S["ac"], 1)
            ev("scalar", f_a2)
            n_sig = bump("ac")

            def f_d2c(e, n=n_sig):
                e.wait_ge(S["ac"], n)
                e.reciprocal(ln1["rinv"][:], ln1["sig"][:])
                e.drain().then_inc(S["dv"], 1)
            ev("vector", f_d2c)
            n_ri = bump("dv")

            def f_a2b(e, n=n_ri):
                e.wait_ge(S["dv"], n)
                e.activation(ln1["rstd"][:], ln1["rinv"][:], AF.Identity)
                e.drain().then_inc(S["ac"], 1)
            ev("scalar", f_a2b)
            n_rstd = bump("ac")

            for tau in pair:
              pb = tau % 2
              xb = tau % 2
              if True:
                def f_d3(e, pb=pb, xb=xb, n=n_rstd, g=prev_txn):
                    e.wait_ge(S["ac"], n)
                    if g:
                        e.wait_ge(S["mm"], g)
                    mb = ln1["mnegA"][:, pb:pb + 1].broadcast_to([128, C])
                    rb = ln1["rstd"][:, pb:pb + 1].broadcast_to([128, C])
                    e.tensor_tensor(qt[:], xtile[xb][:], mb, ALU.add)
                    e.tensor_tensor(xn[:], qt[:], rb, ALU.mult)
                    xg = xn[:].rearrange("p (g j) -> p g j", j=32)
                    e.tensor_reduce(
                        gmax[:, pb * 32:pb * 32 + 32].unsqueeze(2), xg,
                        AX.X, ALU.max, apply_absolute_value=True)
                    e.drain().then_inc(S["dv"], 1)
                ev("vector", f_d3)
                n_xnr = val("dv") + 1
                if debug_stats:
                    def f_dl(e, b=b, tau=tau, pb=pb, n=n_xnr):
                        e.wait_ge(S["dv"], n)
                        e.dma_start(out=bass.AP(dbgl_h.tensor, (b * nt + tau) * 256,
                                                [[2, 128], [1, 1]]),
                                    in_=ln1["mneg"][:, pb:pb + 1],
                                    allow_slow_non_contiguous=True).then_inc(S["sodd"], 16)
                        e.dma_start(out=bass.AP(dbgl_h.tensor, (b * nt + tau) * 256 + 1,
                                                [[2, 128], [1, 1]]),
                                    in_=ln1["rstd"][:, pb:pb + 1],
                                    allow_slow_non_contiguous=True).then_inc(S["sodd"], 16)
                    ev("sync", f_dl)
                    bump("sodd", 32)
                x_gate[xb] = [("dv", n_xnr)]
                n_rs = emit_scale_chain(off=pb * 32)

                def f_d5(e, n=n_rs, g=prev_txd_x, pb=pb):
                    e.wait_ge(S["ac"], n)
                    if g:
                        e.wait_ge(S["mm"], g)
                    emit_quant_dve(e, xn, off=pb * 32)
                    e.drain().then_inc(S["dvt"], 1)
                ev("vector", f_d5)
                n_xdr = bump("dvt")

                slot = tau % 4

                def f_t1(e, n=n_xnr, g=gA):
                    e.wait_ge(S["dv"], n)
                    W(e, g)
                    for ct in range(8):
                        e.transpose(pA[:, ct * 128:(ct + 1) * 128],
                                    xn[:, ct * 128:(ct + 1) * 128],
                                    eyef[:])
                    e.drain().then_inc(S["mm"], 1)
                ev("tensor", f_t1)
                prev_txn = bump("mm")

                def f_ax(e, n=prev_txn, slot=slot, g=xnT_gate):
                    e.wait_ge(S["mm"], n)
                    W(e, g)
                    for ct in range(8):
                        e.activation(xnT[:, ct * 512 + slot * 128:
                                         ct * 512 + slot * 128 + 128],
                                     pA[:, ct * 128:(ct + 1) * 128], AF.Identity)
                    e.drain().then_inc(S["act_t"], 1)
                ev("scalar", f_ax)
                gA = ("act_t", bump("act_t"))
                n_xnT = gA[1]

                def f_t2(e, n=n_xdr, g=gA):
                    e.wait_ge(S["dvt"], n)
                    W(e, g)
                    for ct in range(8):
                        e.transpose(pA[:, ct * 128:(ct + 1) * 128],
                                    qa[:, ct * 128:(ct + 1) * 128], eyef[:])
                    e.drain().then_inc(S["mm"], 1)
                ev("tensor", f_t2)
                prev_txd_x = bump("mm")

                def f_axd(e, n=prev_txd_x, g=xd_store_gate):
                    e.wait_ge(S["mm"], n)
                    W(e, g)
                    e.activation(xdTout[:], pA[:, :], AF.Identity)
                    e.drain().then_inc(S["act_t"], 1)
                ev("scalar", f_axd)
                gA = ("act_t", bump("act_t"))
                n_xdT = gA[1]

                def f_sxd(e, tau=tau, n=n_xdT):
                    e.wait_ge(S["act_t"], n)
                    e.dma_start(out=xdscr[tau, :, :], in_=xdTout[:]).then_inc(S["sxd"], 16)
                ev("sync", f_sxd)
                xd_store_gate = ("sxd", bump("sxd", 16))

                states_prev = ("mm", val("mm"))
                for rk, wnm, bank in (("k0", "wk0", pB), ("v0", "wv0", pA),
                                      ("k1", "wk1", pB), ("v1", "wv1", pA)):
                    gbank = gB if bank is pB else gA

                    kseg = {"k0": 0, "v0": 1, "k1": 2, "v1": 3}[rk]

                    def f_tr(e, wnm=wnm, bank=bank, slot=slot, nx=n_xnT, g=gbank,
                             kseg=kseg):
                        e.wait_ge(S["act_t"], nx)
                        W(e, g)
                        e.wait_ge(S["gp"], n_rowsbf)
                        for ct in range(8):
                            lhs = xnT[:, ct * 512 + slot * 128:
                                      ct * 512 + slot * 128 + 128]
                            for j in range(2):
                                e.matmul(bank[:, j * 512:(j + 1) * 512], lhs,
                                         slab[wnm][:, ct * C + j * 512:
                                                   ct * C + (j + 1) * 512],
                                         start=(ct == 0), stop=False)
                        for j in range(2):
                            e.matmul(bank[:, j * 512:(j + 1) * 512], onesb[0:1, :],
                                     krow[0:1, kseg * C + j * 512:
                                          kseg * C + (j + 1) * 512],
                                     start=False, stop=True)
                        e.drain().then_inc(S["mme"], 1)
                    ev("tensor", f_tr)
                    n_round = bump("mme")

                    def f_ake(e, rk=rk, bank=bank, n=n_round, g=states_prev):
                        e.wait_ge(S["mme"], n)
                        W(e, g)
                        e.activation(ks_sb[rk][:], bank[:, :], AF.Identity,
                                     bias=0.0, scale=csts[:, 5:6])
                        e.drain().then_inc(S["free"], 1)
                        e.drain().then_inc(S["ac"], 1)
                    ev("scalar", f_ake)
                    nf = bump("free")
                    if bank is pB:
                        gB = ("free", nf)
                    else:
                        gA = ("free", nf)
                    n_ks = bump("ac")

                    if rk in ("v0", "v1"):
                        k = "0" if rk == "v0" else "1"
                        pS = pK0 if k == "0" else pK1

                        def f_ts(e, k=k, pS=pS, tau=tau, n=n_ks):
                            e.wait_ge(S["ac"], n)
                            for h in range(H):
                                e.matmul(pS[0:64, h * 64:(h + 1) * 64],
                                         ks_sb["k" + k][:, h * 64:(h + 1) * 64],
                                         ks_sb["v" + k][:, h * 64:(h + 1) * 64],
                                         start=(tau == 0 and h % 8 == 0),
                                         stop=(tau == nt - 1 and h % 8 == 7))
                            e.drain().then_inc(S["mm"], 1)
                        ev("tensor", f_ts)
                        bump("mm")

                if slot == 3 or tau == nt - 1:
                    span0 = tau - slot
                    nsp = slot + 1
                    for half in range(2):
                        def f_tq(e, half=half, nsp=nsp, ga=gA, gb=gB):
                            W(e, ga)
                            W(e, gb)
                            for dt in range(4):
                                d0 = half * 4 + dt
                                bankq = pA if dt < 2 else pB
                                for ct in range(8):
                                    lhs = slab["wq"][:, ct * C + d0 * 128:
                                                     ct * C + d0 * 128 + 128]
                                    mv = xnT[:, ct * 512: ct * 512 + nsp * 128]
                                    e.matmul(bankq[:, (dt % 2) * 512:
                                                   (dt % 2) * 512 + nsp * 128],
                                             lhs, mv, start=(ct == 0), stop=(ct == 7))
                            e.drain().then_inc(S["mme"], 1)
                        ev("tensor", f_tq)
                        n_q = bump("mme")

                        def f_aq(e, half=half, nsp=nsp, n=n_q, g=q_store_gate):
                            e.wait_ge(S["mme"], n)
                            W(e, g)
                            for dt in range(4):
                                d0 = half * 4 + dt
                                bankq = pA if dt < 2 else pB
                                e.activation(qevac[:, dt * 512:dt * 512 + nsp * 128],
                                             bankq[:, (dt % 2) * 512:
                                                   (dt % 2) * 512 + nsp * 128],
                                             AF.Identity, bias=bq_fm[:, d0:d0 + 1])
                            e.drain().then_inc(S["free"], 2)
                            e.drain().then_inc(S["act_t"], 1)
                        ev("scalar", f_aq)
                        nf = bump("free", 2)
                        gA = ("free", nf)
                        gB = ("free", nf)
                        n_qe = bump("act_t")

                        def f_sq(e, half=half, span0=span0, nsp=nsp, n=n_qe,
                                 b=b, spi=tau // 4):
                            e.wait_ge(S["act_t"], n)
                            if debug_stats and half == 0:
                                e.dma_start(out=bass.AP(dbgx_h.tensor,
                                            (b * (nt // 4 + 1) + spi) * 128 * 4096,
                                            [[4096, 128], [1, 4096]]),
                                            in_=xnT[:]).then_inc(S["sodd"], 16)
                            if debug_stats:
                                e.dma_start(out=bass.AP(dbgv_h.tensor,
                                            ((b * (nt // 4 + 1) + spi) * 2 + half) * 128 * 2048,
                                            [[2048, 128], [1, 2048]]),
                                            in_=qevac[:]).then_inc(S["sodd"], 16)
                            for t in range(nsp):
                                dst = bass.AP(qscr.tensor,
                                              (span0 + t) * 128 * 1024 + half * 4 * 128,
                                              [[1024, 128], [128, 4], [1, 128]])
                                srcap = bass.AP(qevac, t * 128,
                                                [[4 * 512, 128], [512, 4], [1, 128]])
                                e.dma_start(out=dst, in_=srcap).then_inc(S["sq"], 16)
                        ev("sync", f_sq)
                        if debug_stats:
                            bump("sodd", 16 if half else 32)
                        q_store_gate = ("sq", bump("sq", 16 * nsp))
                    xnT_gate = ("mme", val("mme"))

        stages[f"phase1_b{b}"] = len(events)
        n_states_done = val("mm")
        og = (out_gate[0], out_gate[1])

        def f_dst(e, n=n_states_done, og=og):
            e.wait_ge(S["mm"], n)
            for g in og:
                W(e, g)
            e.memset(state_sb[:], 0.0)
            e.tensor_copy(kst[0:64, :], pK0[0:64, :])
            ev_src0 = bass.AP(kst, 0, [[C, 64], [128, 8], [1, 64]])
            ev_src1 = bass.AP(pK1, 0, [[C, 64], [128, 8], [1, 64]])
            ev_dst = bass.AP(state_sb, 0, [[8 * 128, 64], [128, 8], [1, 64]])
            e.tensor_tensor(ev_dst, ev_src0, ev_src1, ALU.mult)
            od_src0 = bass.AP(kst, 64, [[C, 64], [128, 8], [1, 64]])
            od_src1 = bass.AP(pK1, 64, [[C, 64], [128, 8], [1, 64]])
            e.tensor_tensor(oddst[:].rearrange("p (g j) -> p g j", j=64),
                            od_src0, od_src1, ALU.mult)
            e.drain().then_inc(S["dv"], 1)
        ev("vector", f_dst)
        n_odd = bump("dv")

        def f_sodd(e, n=n_odd):
            e.wait_ge(S["dv"], n)
            dst = bass.AP(state_sb, 64 * (8 * 128) + 64,
                          [[8 * 128, 64], [128, 8], [1, 64]])
            e.dma_start(out=dst, in_=oddst[:]).then_inc(S["sodd"], 16)
        ev("sync", f_sodd)
        n_stdma = bump("sodd", 16)

        def f_dsrs(e, n=n_stdma):
            e.wait_ge(S["sodd"], n)
            sg = state_sb[:].rearrange("p (g j) -> p g j", j=128)
            e.tensor_reduce(srs_sb[:].unsqueeze(2), sg, AX.X, ALU.add)
            e.drain().then_inc(S["dvt"], 1)
        ev("vector", f_dsrs)
        n_srs = bump("dvt")

        def f_asrs(e, n=n_srs):
            e.wait_ge(S["dvt"], n)
            e.activation(srs_bf[:], srs_sb[:], AF.Identity)
            e.drain().then_inc(S["ac"], 1)
        ev("scalar", f_asrs)
        state_ready = ("ac", bump("ac"))

        stages[f"states_b{b}"] = len(events)
        prev_norm = None
        sq_all = ("sq", val("sq"))
        sxd_all = ("sxd", val("sxd"))
        for tau in range(nt):
            stages[f"p2_b{b}_t{tau}"] = len(events)
            buf = tau % 2
            psem = "p0" if buf == 0 else "p1"

            def f_s2(e, b=b, tau=tau, buf=buf, psem=psem,
                     gx=tuple(x_gate[buf]), gs=stage_gate[buf],
                     gq=sq_all, gxd=sxd_all):
                for g in gx:
                    W(e, g)
                W(e, gs)
                W(e, gq)
                W(e, gxd)
                src = bass.AP(x_h.tensor, (b * seq + tau * 128) * C,
                              [[C, 128], [1, C]])
                e.dma_start(out=xtile[buf][:], in_=src).then_inc(S[psem], 16)
                e.dma_start(out=qstage[buf][:], in_=qscr[tau, :, :]).then_inc(S[psem], 16)
                e.dma_start(out=xdstage[buf][:], in_=xdscr[tau, :, :]).then_inc(S[psem], 16)
            ev("sync", f_s2)
            n_ld2 = bump(psem, 48)

            def f_tam(e, buf=buf, n=n_ld2, psem=psem, ns=state_ready,
                      pn=prev_norm, ga=gA):
                e.wait_ge(S[psem], n)
                W(e, ns)
                W(e, pn)
                W(e, ga)
                for j in range(8):
                    e.matmul(pA[:, j * 128:(j + 1) * 128],
                             state_sb[:, j * 128:(j + 1) * 128],
                             qstage[buf][:, j * 128:(j + 1) * 128],
                             start=True, stop=True)
                for j in range(8):
                    e.matmul(pK0[0:1, 0:128], srs_bf[:, j:j + 1],
                             qstage[buf][:, j * 128:(j + 1) * 128],
                             start=(j == 0), stop=(j == 7))
                e.drain().then_inc(S["mm"], 1)
            ev("tensor", f_tam)
            n_am = bump("mm")

            def f_asq(e, n=n_am):
                e.wait_ge(S["mm"], n)
                e.activation(sq_sb[:], pA[:, :], AF.Square)
                e.drain().then_inc(S["act_t"], 1)
            ev("scalar", f_asq)
            n_sq2 = bump("act_t")

            def f_tsq(e, n=n_sq2):
                e.wait_ge(S["act_t"], n)
                for j in range(8):
                    e.matmul(pK0[0:1, 128:256], onescol[:, 0:1],
                             sq_sb[:, j * 128:(j + 1) * 128],
                             start=(j == 0), stop=(j == 7))
                e.drain().then_inc(S["mm"], 1)
            ev("tensor", f_tsq)
            n_sums = bump("mm")

            def f_dstat(e, n=n_sums):
                e.wait_ge(S["mm"], n)
                e.tensor_scalar(mrow[:], pK0[0:1, 0:128], 1.0 / C, None, ALU.mult)
                e.tensor_tensor(m2row[:], mrow[:], mrow[:], ALU.mult)
                e.scalar_tensor_tensor(vrow[:], pK0[0:1, 128:256], 1.0 / C,
                                       m2row[:], ALU.mult, ALU.subtract)
                e.tensor_scalar(vrow[:], vrow[:], 1e-5, None, ALU.add)
                e.reciprocal(rqrow[:], vrow[:])
                e.tensor_copy(mrow_bf[:], mrow[:])
                e.drain().then_inc(S["dv"], 1)
            ev("vector", f_dstat)
            n_var = bump("dv")

            def f_ar(e, n=n_var):
                e.wait_ge(S["dv"], n)
                e.activation(rrow[:], rqrow[:], AF.Sqrt)
                e.drain().then_inc(S["ac"], 1)
            ev("scalar", f_ar)
            n_rr = bump("ac")

            def f_tbc(e, n=n_rr):
                e.wait_ge(S["ac"], n)
                e.matmul(pK0[:, 256:384], onesb[0:1, :], mrow_bf[:],
                         start=True, stop=True)
                e.matmul(pK0[:, 384:512], onesb[0:1, :], rrow[:],
                         start=True, stop=True)
                e.drain().then_inc(S["mm"], 1)
            ev("tensor", f_tbc)
            n_bc = bump("mm")

            def f_abc(e, n=n_bc):
                e.wait_ge(S["mm"], n)
                e.activation(mtile2[:], pK0[:, 256:512], AF.Identity)
                e.drain().then_inc(S["act_t"], 1)
            ev("scalar", f_abc)
            n_bc2 = bump("act_t")

            def f_dn(e, n=n_bc2):
                e.wait_ge(S["act_t"], n)
                g3 = lambda ap: ap.rearrange("p (g j) -> p g j", j=128)
                mt = mtile2[:, 0:128].unsqueeze(1).broadcast_to([128, 8, 128])
                rt2 = mtile2[:, 128:256].unsqueeze(1).broadcast_to([128, 8, 128])
                e.tensor_tensor(g3(qt[:]), g3(pA[:, :]), mt, ALU.subtract)
                e.tensor_tensor(g3(ln_sb[:]), g3(qt[:]), rt2, ALU.mult)
                e.drain().then_inc(S["dvt"], 1)
            ev("vector", f_dn)
            prev_norm = ("dvt", bump("dvt"))
            gA = prev_norm
            if debug_stats:
                def f_dbg(e, b=b, tau=tau, buf=buf, n=prev_norm[1], n2=n_sq2):
                    e.wait_ge(S["dvt"], n)
                    e.dma_start(out=bass.AP(dbg_h.tensor,
                                            (b * nt + tau) * 128 * 256,
                                            [[256, 128], [1, 256]]),
                                in_=mtile2[:]).then_inc(S["sodd"], 16)
                    e.dma_start(out=bass.AP(dbgq_h.tensor,
                                            (b * nt + tau) * 128 * 1024,
                                            [[1024, 128], [1, 1024]]),
                                in_=qstage[buf][:]).then_inc(S["sodd"], 16)
                    e.dma_start(out=bass.AP(dbgs_h.tensor,
                                            (b * nt + tau) * 128 * 1024,
                                            [[1024, 128], [1, 1024]]),
                                in_=sq_sb[:]).then_inc(S["sodd"], 16)
                ev("sync", f_dbg)
                bump("sodd", 48)

            def f_to(e, buf=buf, n=prev_norm[1], g=gB):
                e.wait_ge(S["dvt"], n)
                W(e, g)
                for ct in range(8):
                    for j in range(2):
                        e.matmul(pB[:, j * 512:(j + 1) * 512],
                                 ln_sb[:, ct * 128:(ct + 1) * 128],
                                 slab["wo"][:, ct * C + j * 512:
                                           ct * C + (j + 1) * 512],
                                 start=(ct == 0), stop=False)
                for ct in range(8):
                    for j in range(2):
                        e.matmul(pB[:, j * 512:(j + 1) * 512],
                                 xdstage[buf][:, ct * 128:(ct + 1) * 128],
                                 slab["wdt"][:, ct * C + j * 512:
                                            ct * C + (j + 1) * 512],
                                 start=False, stop=False)
                for j in range(2):
                    e.matmul(pB[:, j * 512:(j + 1) * 512], onesb[0:3, :],
                             rows3[0:3, j * 512:(j + 1) * 512],
                             start=False, stop=True)
                e.drain().then_inc(S["mme"], 1)
            ev("tensor", f_to)
            n_out = bump("mme")
            stage_gate[buf] = ("mme", n_out)

            def f_do(e, buf=buf, n=n_out, g=out_gate[buf]):
                e.wait_ge(S["mme"], n)
                W(e, g)
                e.scalar_tensor_tensor(out_sb[buf][:], pB[:, :], 1.0,
                                       xtile[buf][:], ALU.mult, ALU.add)
                e.drain().then_inc(S["free"], 1)
                e.drain().then_inc(S["dv"], 1)
            ev("vector", f_do)
            gB = ("free", bump("free"))
            n_o = bump("dv")
            x_gate[buf] = [("dv", n_o)]

            osem = "so0" if buf == 0 else "so1"

            def f_so(e, b=b, tau=tau, buf=buf, n=n_o, osem=osem):
                e.wait_ge(S["dv"], n)
                dst = bass.AP(y_h.tensor, (b * seq + tau * 128) * C,
                              [[C, 128], [1, C]])
                e.dma_start(out=dst, in_=out_sb[buf][:]).then_inc(S[osem], 16)
            ev("sync", f_so)
            out_gate[buf] = (osem, bump(osem, 16))

    stages["all"] = len(events)
    import os as _os
    if _os.environ.get("PRINT_STAGES"):
        print("STAGES:", stages)
    if stop_stage is not None:
        cut = stop_stage if isinstance(stop_stage, int) else stages[stop_stage]
        del events[cut:]
    for name in ("vector", "scalar", "tensor", "gpsimd"):
        def f_fin(e):
            e.drain().then_inc(S["done"], 1)
        ev(name, f_fin)
    n_done = bump("done", 4)
    final_counts = dict(cnt)

    def f_fins(e):
        if stop_stage is None:
            for nm in ("so0", "so1", "sxd", "sq", "sodd"):
                e.wait_ge(S[nm], final_counts[nm])
        e.wait_ge(S["done"], n_done)
        if debug_tail:
            ntail = 0
            fslots = [xn, qa, out_sb[0], out_sb[1], xtile[0], xtile[1]]
            for i, t in enumerate(fslots):
                e.dma_start(out=bass.AP(tailf_h.tensor, i * 128 * 1024,
                                        [[1024, 128], [1, 1024]]),
                            in_=t[:]).then_inc(S["wld"], 16)
                ntail += 16
            for j, nm in enumerate(("sx", "sxx", "mneg", "m2", "u", "sig", "rstd")):
                e.dma_start(out=bass.AP(tailf_h.tensor, 6 * 128 * 1024 + j,
                                        [[1024, 128], [1, 1]]),
                            in_=ln1[nm][:, 0:1],
                            allow_slow_non_contiguous=True).then_inc(S["wld"], 16)
                ntail += 16
            for j, t in enumerate((gmax, s0, rcp, rs_t, ss_t)):
                e.dma_start(out=bass.AP(tailf_h.tensor,
                                        6 * 128 * 1024 + 32 + j * 32,
                                        [[1024, 128], [1, 32]]),
                            in_=t[:, 0:32],
                            allow_slow_non_contiguous=True).then_inc(S["wld"], 16)
                ntail += 16
            e.dma_start(out=bass.AP(tailf_h.tensor, 7 * 128 * 1024,
                                    [[1024, 128], [1, 256]]),
                        in_=mtile2[:],
                        allow_slow_non_contiguous=True).then_inc(S["wld"], 16)
            ntail += 16
            bslots = [state_sb, ks_sb["k0"], ks_sb["v0"], ks_sb["k1"],
                      ks_sb["v1"], None, None, None, None, qstage[0],
                      xdstage[0], sq_sb, ln_sb, None, None]
            for i, t in enumerate(bslots):
                if t is None:
                    continue
                e.dma_start(out=bass.AP(tailb_h.tensor, i * 128 * 1024,
                                        [[1024, 128], [1, 1024]]),
                            in_=t[:]).then_inc(S["wld"], 16)
                ntail += 16
            for j in range(4):
                e.dma_start(out=bass.AP(tailb_h.tensor, (5 + j) * 128 * 1024,
                                        [[1024, 128], [1, 1024]]),
                            in_=xnT[:, j * 1024:(j + 1) * 1024]).then_inc(S["wld"], 16)
                ntail += 16
            e.dma_start(out=bass.AP(tailb_h.tensor, 13 * 128 * 1024,
                                    [[1024, 128], [1, 1024]]),
                        in_=slab["wdt"][:, 0:1024]).then_inc(S["wld"], 16)
            ntail += 16
            e.dma_start(out=bass.AP(tailb_h.tensor, 14 * 128 * 1024,
                                    [[1024, 64], [1, 512]]),
                        in_=oddst[:],
                        allow_slow_non_contiguous=True).then_inc(S["wld"], 16)
            ntail += 16
            for tt in range(nt):
                e.dma_start(out=bass.AP(tailq_h.tensor, tt * 128 * 1024,
                                        [[1024, 128], [1, 1024]]),
                            in_=qscr[tt, :, :]).then_inc(S["wld"], 16)
                e.dma_start(out=bass.AP(tailq_h.tensor,
                                        (nt + tt) * 128 * 1024,
                                        [[1024, 128], [1, 1024]]),
                            in_=xdscr[tt, :, :]).then_inc(S["wld"], 16)
                ntail += 32
            e.wait_ge(S["wld"], final_counts["wld"] + ntail)
        for s in S.values():
            e.sem_clear(s)
    ev("sync", f_fins)

    with nc.Block() as block:
        @block.sync
        def _(eng):
            for name, fn in events:
                if name == "sync":
                    fn(eng)

        @block.vector
        def _(eng):
            for name, fn in events:
                if name == "vector":
                    fn(eng)

        @block.scalar
        def _(eng):
            for name, fn in events:
                if name == "scalar":
                    fn(eng)

        @block.tensor
        def _(eng):
            for name, fn in events:
                if name == "tensor":
                    fn(eng)

        @block.gpsimd
        def _(eng):
            for name, fn in events:
                if name == "gpsimd":
                    fn(eng)

    stk.close()
    return nc


def make_base_inputs(Wq, bq, Wk, bk, Wv, bv, Wo, bo, ln_g, ln_b, Wffn, bffn):
    brows = np.stack([
        np.asarray(bq, np.float32), np.asarray(bk[0], np.float32),
        np.asarray(bk[1], np.float32), np.asarray(bv[0], np.float32),
        np.asarray(bv[1], np.float32), np.asarray(bo, np.float32),
        np.asarray(bffn, np.float32), np.asarray(ln_b, np.float32),
    ]).astype(np.float32)
    return {
        "wq": np.ascontiguousarray(np.asarray(Wq, np.float32)),
        "wk0": np.ascontiguousarray(np.asarray(Wk[0], np.float32)),
        "wk1": np.ascontiguousarray(np.asarray(Wk[1], np.float32)),
        "wv0": np.ascontiguousarray(np.asarray(Wv[0], np.float32)),
        "wv1": np.ascontiguousarray(np.asarray(Wv[1], np.float32)),
        "wo": np.ascontiguousarray(np.asarray(Wo, np.float32)),
        "wffn": np.ascontiguousarray(np.asarray(Wffn, np.float32)),
        "brows": brows,
        "bq_fm": np.ascontiguousarray(np.asarray(bq, np.float32).reshape(8, 128).T),
        "g_fm": np.ascontiguousarray(np.asarray(ln_g, np.float32).reshape(8, 128).T),
        "b_colb": np.ascontiguousarray(
            np.asarray(ln_b, np.float32).reshape(8, 128).T),
        "eyef": np.eye(128, dtype=np.float32),
        "eyeb": np.eye(128, dtype=BF),
        "onesb": np.ones((3, 128), dtype=BF),
        "onesf": np.ones((1, 128), dtype=np.float32),
        "onescol": np.ones((128, 1), dtype=BF),
        "csts": np.tile(np.array([7.5, 1e-30, 8.0, 0.125, 1.0 / 1024,
                                  1.0 / 64, 1e-5, 0],
                                 dtype=np.float32), (128, 1)),
    }


def kernel(L0_attn_x, L0_attn_Wq, L0_attn_bq, L0_attn_Wk, L0_attn_bk,
           L0_attn_Wv, L0_attn_bv, L0_attn_Wo, L0_attn_bo,
           L0_attn_ln_g, L0_attn_ln_b, L0_ffn_weight, L0_ffn_bias):
    x = np.ascontiguousarray(np.asarray(L0_attn_x, dtype=np.float32))
    B = x.shape[0]
    ncores = 8
    per = B // ncores
    nc = build_program(nb=per, nt=NT)
    base = make_base_inputs(L0_attn_Wq, L0_attn_bq, L0_attn_Wk, L0_attn_bk,
                            L0_attn_Wv, L0_attn_bv, L0_attn_Wo, L0_attn_bo,
                            L0_attn_ln_g, L0_attn_ln_b, L0_ffn_weight,
                            L0_ffn_bias)
    in_maps = []
    for c in range(ncores):
        m = dict(base)
        m["x"] = np.ascontiguousarray(x[c * per:(c + 1) * per])
        in_maps.append(m)
    res = run_bass_kernel_spmd(nc, in_maps, core_ids=list(range(ncores)))
    out = np.concatenate([np.asarray(r["y"]) for r in res.results], axis=0)
    return out.astype(np.float32)


# version 06
# speedup vs baseline: 1.1439x; 1.0986x over previous
import numpy as np
import ml_dtypes
import concourse.bass as bass
import concourse.mybir as mybir
from concourse.bass_utils import run_bass_kernel_spmd
from contextlib import ExitStack

F32 = mybir.dt.float32
BF16 = mybir.dt.bfloat16
I32 = mybir.dt.int32
AF = mybir.ActivationFunctionType
ALU = mybir.AluOpType
AX = mybir.AxisListType
BF = ml_dtypes.bfloat16

C = 1024
H = 16
NT = 16
SCALE = 1.0 / 64.0
BOOT = 64


def build_program(nb=4, nt=NT, stop_stage=None, detect_races=False,
                  debug_stats=False, debug_tail=False):
    nc = bass.Bass("TRN2", target_bir_lowering=False, detect_race_conditions=detect_races)
    seq = nt * 128

    x_h = nc.dram_tensor("x", [nb, seq, C], F32, kind="ExternalInput").ap()
    w_h = {nm: nc.dram_tensor(nm, [C, C], F32, kind="ExternalInput").ap()
           for nm in ("wq", "wk0", "wk1", "wv0", "wv1", "wo")}
    wffn_h = nc.dram_tensor("wffn", [C, C], F32, kind="ExternalInput").ap()
    rows_h = nc.dram_tensor("brows", [8, C], F32, kind="ExternalInput").ap()
    bq_fm_h = nc.dram_tensor("bq_fm", [128, 8], F32, kind="ExternalInput").ap()
    g_fm_h = nc.dram_tensor("g_fm", [128, 8], F32, kind="ExternalInput").ap()
    bcol_h = nc.dram_tensor("b_colb", [128, 8], F32, kind="ExternalInput").ap()
    eyef_h = nc.dram_tensor("eyef", [128, 128], F32, kind="ExternalInput").ap()
    eyeb_h = nc.dram_tensor("eyeb", [128, 128], BF16, kind="ExternalInput").ap()
    onesb_h = nc.dram_tensor("onesb", [3, 128], BF16, kind="ExternalInput").ap()
    onesf_h = nc.dram_tensor("onesf", [1, 128], F32, kind="ExternalInput").ap()
    onescol_h = nc.dram_tensor("onescol", [128, 1], BF16, kind="ExternalInput").ap()
    csts_h = nc.dram_tensor("csts", [128, 8], F32, kind="ExternalInput").ap()
    y_h = nc.dram_tensor("y", [nb, seq, C], F32, kind="ExternalOutput").ap()
    qscr = nc.dram_tensor("qscr", [nt, 128, 8 * 128], BF16).ap()
    dbg_h = (nc.dram_tensor("dbg", [nb, nt, 128, 256], F32, kind="ExternalOutput").ap()
             if debug_stats else None)
    dbgq_h = (nc.dram_tensor("dbgq", [nb, nt, 128, 1024], BF16, kind="ExternalOutput").ap()
              if debug_stats else None)
    dbgs_h = (nc.dram_tensor("dbgs", [nb, nt, 128, 1024], BF16, kind="ExternalOutput").ap()
              if debug_stats else None)
    dbgv_h = (nc.dram_tensor("dbgv", [nb * (nt // 4 + 1), 2, 128, 2048], BF16,
                             kind="ExternalOutput").ap() if debug_stats else None)
    dbgx_h = (nc.dram_tensor("dbgx", [nb * (nt // 4 + 1), 128, 4096], BF16,
                             kind="ExternalOutput").ap() if debug_stats else None)
    dbgl_h = (nc.dram_tensor("dbgl", [nb, nt, 128, 2], F32,
                             kind="ExternalOutput").ap() if debug_stats else None)
    xdscr = nc.dram_tensor("xdscr", [nt, 128, 8 * 128], BF16).ap()
    tailf_h = (nc.dram_tensor("tailf", [8, 128, 1024], F32, kind="ExternalOutput").ap()
               if debug_tail else None)
    tailb_h = (nc.dram_tensor("tailb", [15, 128, 1024], BF16, kind="ExternalOutput").ap()
               if debug_tail else None)
    tailp_h = (nc.dram_tensor("tailp", [3, 128, 1024], F32, kind="ExternalOutput").ap()
               if debug_tail else None)
    tailq_h = (nc.dram_tensor("tailq", [2, nt, 128, 1024], BF16, kind="ExternalOutput").ap()
               if debug_tail else None)

    stk = ExitStack()
    sbt = lambda n, s, d: stk.enter_context(nc.sbuf_tensor(n, s, d))
    psa = lambda n, s: stk.enter_context(nc.psum_tensor(n, s, F32))
    mksem = lambda n: stk.enter_context(nc.semaphore(n))

    slab = {nm: sbt(nm + "_s", [128, 8 * C], BF16) for nm in
            ("wq", "wk0", "wk1", "wv0", "wv1", "wo", "wdt")}
    rows3 = sbt("rows3", [3, C], BF16)
    bq_fm = sbt("bqfm", [128, 8], F32)
    g_fm = sbt("gfm", [128, 8], F32)
    bcol_bf = sbt("bcolb", [128, 8], BF16)
    eyef = sbt("eyefs", [128, 128], F32)
    eyeb = sbt("eyebs", [128, 128], BF16)
    onesb = sbt("onesbs", [3, 128], BF16)
    onesf = sbt("onesfs", [1, 128], F32)
    onescol = sbt("onescols", [128, 1], BF16)
    csts = sbt("cstss", [128, 8], F32)
    krow = sbt("krow", [1, 4 * C], BF16)
    rc75d = sbt("rc75d", [128, 1], F32)
    rc75A = sbt("rc75A", [128, 1], F32)

    xtile = [sbt(f"xt{i}", [128, C], F32) for i in range(2)]
    xn = sbt("xn", [128, C], F32)
    xnT = sbt("xnT", [128, 4 * 8 * 128], BF16)
    ks_sb = {k: sbt("ks_" + k, [128, C], BF16) for k in ("k0", "k1", "v0", "v1")}
    qt = sbt("q_t", [128, C], F32)
    qa = sbt("q_a", [128, C], F32)
    qb = sbt("q_b", [128, C], F32)
    qw = sbt("q_w", [128, C], F32)
    qit = sbt("q_it", [128, C], I32)
    gmax = sbt("gmaxs", [128, 64], F32)
    s0 = sbt("s0s", [128, 64], F32)
    rcp = sbt("rcps", [128, 64], F32)
    rs_t = sbt("rss", [128, 64], F32)
    ss_t = sbt("sss", [128, 64], F32)
    sq_dump = sbt("sqdump", [128, C], BF16)
    ln1 = {n: sbt("l_" + n, [128, 2], F32) for n in
           ("sx", "sxx", "mneg", "m2", "u", "sig", "rinv", "rstd", "mnegA")}
    state_sb = sbt("state", [128, 8 * 128], BF16)
    srs_sb = sbt("srs", [128, 8], F32)
    srs_bf = sbt("srsbf", [128, 8], BF16)
    oddst = sbt("oddst", [64, 8 * 64], BF16)
    qstage = [sbt(f"qstage{i}", [128, 8 * 128], BF16) for i in range(2)]
    xdstage = [sbt(f"xdstage{i}", [128, 8 * 128], BF16) for i in range(2)]
    xdTout = sbt("xdTout", [128, 8 * 128], BF16)
    sq_sb = sbt("sqsb", [128, 8 * 128], BF16)
    ln_sb = sbt("lnsb", [128, 8 * 128], BF16)
    mrow = sbt("mrow", [1, 128], F32)
    mrow_bf = sbt("mrowbf", [1, 128], BF16)
    rrow = sbt("rrow", [1, 128], BF16)
    vrow = sbt("vrow", [1, 128], F32)
    m2row = sbt("m2row", [1, 128], F32)
    rqrow = sbt("rqrow", [1, 128], F32)
    mtile2 = sbt("mtile2", [128, 256], F32)
    out_sb = [sbt(f"outsb{i}", [128, C], F32) for i in range(2)]
    qevac = sbt("qevac", [128, 4 * 512], BF16)
    wfstage = out_sb
    kst = out_sb[0]

    pA = psa("pA", [128, C])
    pB = psa("pB", [128, C])
    pK0 = psa("pK0", [128, C])
    pK1 = psa("pK1", [128, C])

    semnames = ("x0", "x1", "x2", "p0", "p1", "wf0", "wf1", "wld", "sodd",
                "sxd", "sq", "so0", "so1",
                "dv", "ac", "dvt", "act_t", "mm", "mme", "free", "gp", "done")
    S = {n: mksem("s_" + n) for n in semnames}
    cnt = {n: 0 for n in semnames}
    events = []
    stages = {}

    def ev(engine, fn):
        events.append((engine, fn))

    def bump(name, by=1):
        cnt[name] += by
        return cnt[name]

    def val(name):
        return cnt[name]

    def W(e, gate):
        if gate is not None and gate[1] > 0:
            e.wait_ge(S[gate[0]], gate[1])

    def f_sync0(e):
        for s in S.values():
            e.sem_clear(s)
        e.sem_inc(S["done"], BOOT)
        for dst, src in ((bq_fm, bq_fm_h), (g_fm, g_fm_h),
                         (eyef, eyef_h), (eyeb, eyeb_h),
                         (onesb, onesb_h), (onesf, onesf_h),
                         (onescol, onescol_h), (csts, csts_h)):
            e.dma_start(out=dst[:], in_=src[:]).then_inc(S["wld"], 16)
    ev("sync", f_sync0)
    bump("done", BOOT)
    n_consts = bump("wld", 128)

    def f_gp0(e):
        e.wait_ge(S["done"], BOOT)
        for nm in ("wk0", "wv0", "wk1", "wv1", "wq", "wo"):
            src = bass.AP(w_h[nm].tensor, 0, [[C, 128], [128 * C, 8], [1, C]])
            e.dma_start(out=slab[nm][:], in_=src).then_inc(S["gp"], 16)
        e.dma_start(out=rows3[1:2, :],
                    in_=bass.AP(rows_h.tensor, 5 * C, [[C, 1], [1, C]])).then_inc(S["gp"], 16)
        e.dma_start(out=rows3[2:3, :],
                    in_=bass.AP(rows_h.tensor, 6 * C, [[C, 1], [1, C]])).then_inc(S["gp"], 16)
        e.dma_start(out=bcol_bf[:], in_=bcol_h[:]).then_inc(S["gp"], 16)
        for i, r in enumerate((1, 3, 2, 4)):
            e.dma_start(out=krow[0:1, i * C:(i + 1) * C],
                        in_=bass.AP(rows_h.tensor, r * C,
                                    [[C, 1], [1, C]])).then_inc(S["gp"], 16)
    ev("gpsimd", f_gp0)
    n_wslabs = bump("gp", 96)
    n_rowsbf = bump("gp", 112)

    def f_dve0(e):
        e.wait_ge(S["done"], BOOT)
        e.wait_ge(S["wld"], n_consts)
        e.wait_ge(S["gp"], n_rowsbf)
        e.reciprocal(rc75d[:], csts[:, 0:1])
        for nm in ("wq", "wk0", "wk1", "wv0", "wv1"):
            for ct in range(8):
                d0 = ct * C + ct * 128
                e.tensor_tensor(slab[nm][:, d0:d0 + 128],
                                slab[nm][:, d0:d0 + 128], eyeb[:], ALU.add)
        e.drain().then_inc(S["dvt"], 1)
    ev("vector", f_dve0)
    n_wplusI = bump("dvt")

    def f_t0(e):
        e.wait_ge(S["done"], BOOT)
        e.wait_ge(S["gp"], n_rowsbf)
        e.wait_ge(S["dvt"], n_wplusI)
        for j in range(2):
            for ct in range(8):
                e.matmul(pA[0:1, j * 512:(j + 1) * 512], bcol_bf[:, ct:ct + 1],
                         slab["wo"][:, ct * C + j * 512: ct * C + (j + 1) * 512],
                         start=(ct == 0), stop=(ct == 7))
        e.drain().then_inc(S["mm"], 1)
    ev("tensor", f_t0)
    n_bwo_mm = bump("mm")

    def f_a0(e):
        e.wait_ge(S["done"], BOOT)
        e.wait_ge(S["wld"], n_consts)
        e.wait_ge(S["gp"], n_rowsbf)
        e.wait_ge(S["mm"], n_bwo_mm)
        e.activation(rows3[0:1, :], pA[0:1, :], AF.Identity)
        e.activation(rc75A[:], rc75d[:], AF.Identity)
        e.drain().then_inc(S["act_t"], 1)
    ev("scalar", f_a0)
    gA = ("act_t", bump("act_t"))

    def f_awo(e, g=gA):
        e.wait_ge(S[g[0]], g[1])
        for ct in range(8):
            e.activation(slab["wo"][:, ct * C:(ct + 1) * C],
                         slab["wo"][:, ct * C:(ct + 1) * C],
                         AF.Identity, bias=0.0, scale=g_fm[:, ct:ct + 1])
        e.drain().then_inc(S["ac"], 1)
    ev("scalar", f_awo)
    n_wo_ready = bump("ac")

    gB = ("free", 0)

    def emit_scale_chain(off=0, n_gm=None):
        if n_gm is None:
            n_gm = bump("dv")
        sl = slice(off, off + 32)

        def f_ac(e, n=n_gm, sl=sl):
            e.wait_ge(S["dv"], n)
            e.activation(s0[:, sl], gmax[:, sl], AF.Identity,
                         bias=csts[:, 1:2], scale=rc75A[:])
            e.drain().then_inc(S["ac"], 1)
        ev("scalar", f_ac)
        n_s0 = bump("ac")

        def f_dr(e, n=n_s0, sl=sl):
            e.wait_ge(S["ac"], n)
            e.reciprocal(rcp[:, sl], s0[:, sl])
            e.drain().then_inc(S["dv"], 1)
        ev("vector", f_dr)
        n_rc = bump("dv")

        def f_ac2(e, n=n_rc, sl=sl):
            e.wait_ge(S["dv"], n)
            e.activation(rs_t[:, sl], rcp[:, sl], AF.Identity,
                         bias=0.0, scale=csts[:, 2:3])
            e.activation(ss_t[:, sl], s0[:, sl], AF.Identity,
                         bias=0.0, scale=csts[:, 3:4])
            e.drain().then_inc(S["ac"], 1)
        ev("scalar", f_ac2)
        return bump("ac")

    def emit_quant_dve(e, src, off=0):
        g3 = lambda ap: ap.rearrange("p (g j) -> p g j", j=32)
        rs_b = rs_t[:, off:off + 32].unsqueeze(2).broadcast_to([128, 32, 32])
        ss_b = ss_t[:, off:off + 32].unsqueeze(2).broadcast_to([128, 32, 32])
        e.tensor_tensor(g3(qt[:]), g3(src[:]), rs_b, ALU.mult)
        e.tensor_scalar(qa[:], qt[:], 16.0, -16.0, ALU.min, ALU.max)
        e.tensor_scalar(qb[:], qt[:], 32.0, -32.0, ALU.min, ALU.max)
        e.tensor_scalar(qw[:], qt[:], 0.25, 64.5, ALU.mult, ALU.add)
        e.scalar_tensor_tensor(qw[:], qb[:], 0.25, qw[:], ALU.mult, ALU.add)
        e.scalar_tensor_tensor(qw[:], qa[:], 0.5, qw[:], ALU.mult, ALU.add)
        e.tensor_copy(qit[:], qw[:])
        e.tensor_tensor(qa[:], qw[:], qit[:], ALU.is_ge)
        e.tensor_scalar(qb[:], qit[:], 1.0, None, ALU.add)
        e.tensor_tensor(qb[:], qw[:], qb[:], ALU.is_ge)
        e.tensor_tensor(qa[:], qa[:], qb[:], ALU.add)
        e.scalar_tensor_tensor(qt[:], qit[:], -65.0, qa[:], ALU.add, ALU.add)
        e.tensor_scalar(qa[:], qt[:], 24.0, -24.0, ALU.min, ALU.max)
        e.tensor_scalar(qb[:], qt[:], 16.0, -16.0, ALU.min, ALU.max)
        e.tensor_scalar(qw[:], qt[:], 4.0, None, ALU.mult)
        e.scalar_tensor_tensor(qw[:], qa[:], -2.0, qw[:], ALU.mult, ALU.add)
        e.scalar_tensor_tensor(qw[:], qb[:], -1.0, qw[:], ALU.mult, ALU.add)
        e.tensor_tensor(g3(qa[:]), g3(qw[:]), ss_b, ALU.mult)

    def f_swf(e):
        for j in range(2):
            src = bass.AP(wffn_h.tensor, j * 128 * C, [[C, 128], [1, C]])
            e.dma_start(out=wfstage[j][:], in_=src).then_inc(S["wf" + str(j)], 16)
    ev("sync", f_swf)
    bump("wf0", 16)
    bump("wf1", 16)

    prev_txd = 0
    for jt in range(8):
        wsem = "wf" + str(jt % 2)
        n_have = val(wsem)

        def f_dg(e, jt=jt, n=n_have, wsem=wsem):
            e.wait_ge(S[wsem], n)
            xg = wfstage[jt % 2][:].rearrange("p (g j) -> p g j", j=32)
            e.tensor_reduce(gmax[:, 0:32].unsqueeze(2), xg, AX.X, ALU.max,
                            apply_absolute_value=True)
            e.drain().then_inc(S["dv"], 1)
        ev("vector", f_dg)
        n_rs = emit_scale_chain()

        def f_dq(e, jt=jt, n=n_rs, g=prev_txd):
            e.wait_ge(S["ac"], n)
            if g:
                e.wait_ge(S["mm"], g)
            emit_quant_dve(e, wfstage[jt % 2])
            e.drain().then_inc(S["dvt"], 1)
        ev("vector", f_dq)
        n_xd = bump("dvt")

        if jt + 2 < 8:
            def f_sn(e, jt=jt, n=n_xd):
                e.wait_ge(S["dvt"], n)
                src = bass.AP(wffn_h.tensor, (jt + 2) * 128 * C, [[C, 128], [1, C]])
                e.dma_start(out=wfstage[jt % 2][:], in_=src).then_inc(
                    S["wf" + str(jt % 2)], 16)
            ev("sync", f_sn)
            bump("wf" + str(jt % 2), 16)

        def f_tt(e, n=n_xd, g=gA):
            e.wait_ge(S["dvt"], n)
            W(e, g)
            for ct in range(8):
                e.transpose(pA[:, ct * 128:(ct + 1) * 128],
                            qa[:, ct * 128:(ct + 1) * 128], eyef[:])
            e.drain().then_inc(S["mm"], 1)
        ev("tensor", f_tt)
        prev_txd = bump("mm")

        def f_ae(e, jt=jt, n=prev_txd):
            e.wait_ge(S["mm"], n)
            for ct in range(8):
                e.activation(slab["wdt"][:, ct * C + jt * 128: ct * C + jt * 128 + 128],
                             pA[:, ct * 128:(ct + 1) * 128], AF.Identity)
            e.drain().then_inc(S["act_t"], 1)
        ev("scalar", f_ae)
        gA = ("act_t", bump("act_t"))

    stages["startup_ffn"] = len(events)
    x_gate = [[], []]
    stage_gate = [None, None]
    out_gate = [None, None]
    xnT_gate = None
    xd_store_gate = None
    q_store_gate = None
    state_ready = None
    prev_txn = 0
    prev_txd_x = prev_txd

    for b in range(nb):
        for tp in range(nt // 2):
            pair = (2 * tp, 2 * tp + 1)
            for tau in pair:
                pb = tau % 2
                xb = tau % 2
                xsem = f"x{xb}"

                def f_sx(e, b=b, tau=tau, xb=xb, gs=tuple(x_gate[xb]),
                         xsem=xsem):
                    for g in gs:
                        W(e, g)
                    src = bass.AP(x_h.tensor, (b * seq + tau * 128) * C,
                                  [[C, 128], [1, C]])
                    e.dma_start(out=xtile[xb][:], in_=src).then_inc(S[xsem], 16)
                ev("sync", f_sx)
                n_x = bump(xsem, 16)

                def f_a1(e, xb=xb, pb=pb, n=n_x, xsem=xsem):
                    e.wait_ge(S[xsem], n)
                    e.activation(sq_dump[:], xtile[xb][:], AF.Square,
                                 accum_out=ln1["sxx"][:, pb:pb + 1])
                    e.activation(sq_dump[:], xtile[xb][:], AF.Identity,
                                 accum_out=ln1["sx"][:, pb:pb + 1])
                    e.drain().then_inc(S["ac"], 1)
                ev("scalar", f_a1)
                n_sxx = bump("ac")

            def f_d2(e, n=n_sxx):
                e.wait_ge(S["ac"], n)
                e.tensor_scalar(ln1["mneg"][:], ln1["sx"][:], -1.0 / C, None,
                                ALU.mult)
                e.drain().then_inc(S["dv"], 1)
            ev("vector", f_d2)
            n_mn = bump("dv")

            def f_a1b(e, n=n_mn):
                e.wait_ge(S["dv"], n)
                e.activation(ln1["m2"][:], ln1["mneg"][:], AF.Square)
                e.activation(ln1["mnegA"][:], ln1["mneg"][:], AF.Identity)
                e.drain().then_inc(S["ac"], 1)
            ev("scalar", f_a1b)
            n_m2 = bump("ac")

            def f_d2b(e, n=n_m2):
                e.wait_ge(S["ac"], n)
                e.scalar_tensor_tensor(ln1["u"][:], ln1["sxx"][:], 1.0 / C,
                                       ln1["m2"][:], ALU.mult, ALU.subtract)
                e.drain().then_inc(S["dv"], 1)
            ev("vector", f_d2b)
            n_u = bump("dv")

            def f_a2(e, n=n_u):
                e.wait_ge(S["dv"], n)
                e.activation(ln1["sig"][:], ln1["u"][:], AF.Sqrt,
                             bias=csts[:, 6:7], scale=1.0)
                e.drain().then_inc(S["ac"], 1)
            ev("scalar", f_a2)
            n_sig = bump("ac")

            def f_d2c(e, n=n_sig):
                e.wait_ge(S["ac"], n)
                e.reciprocal(ln1["rinv"][:], ln1["sig"][:])
                e.drain().then_inc(S["dv"], 1)
            ev("vector", f_d2c)
            n_ri = bump("dv")

            def f_a2b(e, n=n_ri):
                e.wait_ge(S["dv"], n)
                e.activation(ln1["rstd"][:], ln1["rinv"][:], AF.Identity)
                e.drain().then_inc(S["ac"], 1)
            ev("scalar", f_a2b)
            n_rstd = bump("ac")

            for tau in pair:
              pb = tau % 2
              xb = tau % 2
              if True:
                def f_d3(e, pb=pb, xb=xb, n=n_rstd, g=prev_txn):
                    e.wait_ge(S["ac"], n)
                    if g:
                        e.wait_ge(S["mm"], g)
                    mb = ln1["mnegA"][:, pb:pb + 1].broadcast_to([128, C])
                    rb = ln1["rstd"][:, pb:pb + 1].broadcast_to([128, C])
                    e.tensor_tensor(qt[:], xtile[xb][:], mb, ALU.add)
                    e.tensor_tensor(xn[:], qt[:], rb, ALU.mult)
                    xg = xn[:].rearrange("p (g j) -> p g j", j=32)
                    e.tensor_reduce(
                        gmax[:, pb * 32:pb * 32 + 32].unsqueeze(2), xg,
                        AX.X, ALU.max, apply_absolute_value=True)
                    e.drain().then_inc(S["dv"], 1)
                ev("vector", f_d3)
                n_xnr = val("dv") + 1
                if debug_stats:
                    def f_dl(e, b=b, tau=tau, pb=pb, n=n_xnr):
                        e.wait_ge(S["dv"], n)
                        e.dma_start(out=bass.AP(dbgl_h.tensor, (b * nt + tau) * 256,
                                                [[2, 128], [1, 1]]),
                                    in_=ln1["mneg"][:, pb:pb + 1],
                                    allow_slow_non_contiguous=True).then_inc(S["sodd"], 16)
                        e.dma_start(out=bass.AP(dbgl_h.tensor, (b * nt + tau) * 256 + 1,
                                                [[2, 128], [1, 1]]),
                                    in_=ln1["rstd"][:, pb:pb + 1],
                                    allow_slow_non_contiguous=True).then_inc(S["sodd"], 16)
                    ev("sync", f_dl)
                    bump("sodd", 32)
                x_gate[xb] = [("dv", n_xnr)]
                n_rs = emit_scale_chain(off=pb * 32)

                def f_d5(e, n=n_rs, g=prev_txd_x, pb=pb):
                    e.wait_ge(S["ac"], n)
                    if g:
                        e.wait_ge(S["mm"], g)
                    emit_quant_dve(e, xn, off=pb * 32)
                    e.drain().then_inc(S["dvt"], 1)
                ev("vector", f_d5)
                n_xdr = bump("dvt")

                slot = tau % 4

                def f_t1(e, n=n_xnr, g=gA):
                    e.wait_ge(S["dv"], n)
                    W(e, g)
                    for ct in range(8):
                        e.transpose(pA[:, ct * 128:(ct + 1) * 128],
                                    xn[:, ct * 128:(ct + 1) * 128],
                                    eyef[:])
                    e.drain().then_inc(S["mm"], 1)
                ev("tensor", f_t1)
                prev_txn = bump("mm")

                def f_ax(e, n=prev_txn, slot=slot, g=xnT_gate):
                    e.wait_ge(S["mm"], n)
                    W(e, g)
                    for ct in range(8):
                        e.activation(xnT[:, ct * 512 + slot * 128:
                                         ct * 512 + slot * 128 + 128],
                                     pA[:, ct * 128:(ct + 1) * 128], AF.Identity)
                    e.drain().then_inc(S["act_t"], 1)
                ev("scalar", f_ax)
                gA = ("act_t", bump("act_t"))
                n_xnT = gA[1]

                def f_t2(e, n=n_xdr, g=gA):
                    e.wait_ge(S["dvt"], n)
                    W(e, g)
                    for ct in range(8):
                        e.transpose(pA[:, ct * 128:(ct + 1) * 128],
                                    qa[:, ct * 128:(ct + 1) * 128], eyef[:])
                    e.drain().then_inc(S["mm"], 1)
                ev("tensor", f_t2)
                prev_txd_x = bump("mm")

                def f_axd(e, n=prev_txd_x, g=xd_store_gate):
                    e.wait_ge(S["mm"], n)
                    W(e, g)
                    e.activation(xdTout[:], pA[:, :], AF.Identity)
                    e.drain().then_inc(S["act_t"], 1)
                ev("scalar", f_axd)
                gA = ("act_t", bump("act_t"))
                n_xdT = gA[1]

                def f_sxd(e, tau=tau, n=n_xdT):
                    e.wait_ge(S["act_t"], n)
                    e.dma_start(out=xdscr[tau, :, :], in_=xdTout[:]).then_inc(S["sxd"], 16)
                ev("sync", f_sxd)
                xd_store_gate = ("sxd", bump("sxd", 16))

                states_prev = ("mm", val("mm"))
                for rk, wnm, bank in (("k0", "wk0", pB), ("v0", "wv0", pA),
                                      ("k1", "wk1", pB), ("v1", "wv1", pA)):
                    gbank = gB if bank is pB else gA

                    kseg = {"k0": 0, "v0": 1, "k1": 2, "v1": 3}[rk]

                    def f_tr(e, wnm=wnm, bank=bank, slot=slot, nx=n_xnT, g=gbank,
                             kseg=kseg):
                        e.wait_ge(S["act_t"], nx)
                        W(e, g)
                        e.wait_ge(S["gp"], n_rowsbf)
                        for ct in range(8):
                            lhs = xnT[:, ct * 512 + slot * 128:
                                      ct * 512 + slot * 128 + 128]
                            for j in range(2):
                                e.matmul(bank[:, j * 512:(j + 1) * 512], lhs,
                                         slab[wnm][:, ct * C + j * 512:
                                                   ct * C + (j + 1) * 512],
                                         start=(ct == 0), stop=False)
                        for j in range(2):
                            e.matmul(bank[:, j * 512:(j + 1) * 512], onesb[0:1, :],
                                     krow[0:1, kseg * C + j * 512:
                                          kseg * C + (j + 1) * 512],
                                     start=False, stop=True)
                        e.drain().then_inc(S["mme"], 1)
                    ev("tensor", f_tr)
                    n_round = bump("mme")

                    def f_ake(e, rk=rk, bank=bank, n=n_round, g=states_prev):
                        e.wait_ge(S["mme"], n)
                        W(e, g)
                        e.activation(ks_sb[rk][:], bank[:, :], AF.Identity,
                                     bias=0.0, scale=csts[:, 5:6])
                        e.drain().then_inc(S["free"], 1)
                        e.drain().then_inc(S["ac"], 1)
                    ev("scalar", f_ake)
                    nf = bump("free")
                    if bank is pB:
                        gB = ("free", nf)
                    else:
                        gA = ("free", nf)
                    n_ks = bump("ac")

                    if rk in ("v0", "v1"):
                        k = "0" if rk == "v0" else "1"
                        pS = pK0 if k == "0" else pK1

                        def f_ts(e, k=k, pS=pS, tau=tau, n=n_ks):
                            e.wait_ge(S["ac"], n)
                            for h in range(H):
                                e.matmul(pS[0:64, h * 64:(h + 1) * 64],
                                         ks_sb["k" + k][:, h * 64:(h + 1) * 64],
                                         ks_sb["v" + k][:, h * 64:(h + 1) * 64],
                                         start=(tau == 0 and h % 8 == 0),
                                         stop=(tau == nt - 1 and h % 8 == 7))
                            e.drain().then_inc(S["mm"], 1)
                        ev("tensor", f_ts)
                        bump("mm")

                if slot == 3 or tau == nt - 1:
                    span0 = tau - slot
                    nsp = slot + 1
                    for half in range(2):
                        def f_tq(e, half=half, nsp=nsp, ga=gA, gb=gB):
                            W(e, ga)
                            W(e, gb)
                            for dt in range(4):
                                d0 = half * 4 + dt
                                bankq = pA if dt < 2 else pB
                                for ct in range(8):
                                    lhs = slab["wq"][:, ct * C + d0 * 128:
                                                     ct * C + d0 * 128 + 128]
                                    mv = xnT[:, ct * 512: ct * 512 + nsp * 128]
                                    e.matmul(bankq[:, (dt % 2) * 512:
                                                   (dt % 2) * 512 + nsp * 128],
                                             lhs, mv, start=(ct == 0), stop=(ct == 7))
                            e.drain().then_inc(S["mme"], 1)
                        ev("tensor", f_tq)
                        n_q = bump("mme")

                        def f_aq(e, half=half, nsp=nsp, n=n_q, g=q_store_gate):
                            e.wait_ge(S["mme"], n)
                            W(e, g)
                            for dt in range(4):
                                d0 = half * 4 + dt
                                bankq = pA if dt < 2 else pB
                                e.activation(qevac[:, dt * 512:dt * 512 + nsp * 128],
                                             bankq[:, (dt % 2) * 512:
                                                   (dt % 2) * 512 + nsp * 128],
                                             AF.Identity, bias=bq_fm[:, d0:d0 + 1])
                            e.drain().then_inc(S["free"], 2)
                            e.drain().then_inc(S["act_t"], 1)
                        ev("scalar", f_aq)
                        nf = bump("free", 2)
                        gA = ("free", nf)
                        gB = ("free", nf)
                        n_qe = bump("act_t")

                        def f_sq(e, half=half, span0=span0, nsp=nsp, n=n_qe,
                                 b=b, spi=tau // 4):
                            e.wait_ge(S["act_t"], n)
                            if debug_stats and half == 0:
                                e.dma_start(out=bass.AP(dbgx_h.tensor,
                                            (b * (nt // 4 + 1) + spi) * 128 * 4096,
                                            [[4096, 128], [1, 4096]]),
                                            in_=xnT[:]).then_inc(S["sodd"], 16)
                            if debug_stats:
                                e.dma_start(out=bass.AP(dbgv_h.tensor,
                                            ((b * (nt // 4 + 1) + spi) * 2 + half) * 128 * 2048,
                                            [[2048, 128], [1, 2048]]),
                                            in_=qevac[:]).then_inc(S["sodd"], 16)
                            for t in range(nsp):
                                dst = bass.AP(qscr.tensor,
                                              (span0 + t) * 128 * 1024 + half * 4 * 128,
                                              [[1024, 128], [128, 4], [1, 128]])
                                srcap = bass.AP(qevac, t * 128,
                                                [[4 * 512, 128], [512, 4], [1, 128]])
                                e.dma_start(out=dst, in_=srcap).then_inc(S["sq"], 16)
                        ev("sync", f_sq)
                        if debug_stats:
                            bump("sodd", 16 if half else 32)
                        q_store_gate = ("sq", bump("sq", 16 * nsp))
                    xnT_gate = ("mme", val("mme"))

        stages[f"phase1_b{b}"] = len(events)
        n_states_done = val("mm")
        og = (out_gate[0], out_gate[1])

        def f_dst(e, n=n_states_done, og=og):
            e.wait_ge(S["mm"], n)
            for g in og:
                W(e, g)
            e.memset(state_sb[:], 0.0)
            e.tensor_copy(kst[0:64, :], pK0[0:64, :])
            ev_src0 = bass.AP(kst, 0, [[C, 64], [128, 8], [1, 64]])
            ev_src1 = bass.AP(pK1, 0, [[C, 64], [128, 8], [1, 64]])
            ev_dst = bass.AP(state_sb, 0, [[8 * 128, 64], [128, 8], [1, 64]])
            e.tensor_tensor(ev_dst, ev_src0, ev_src1, ALU.mult)
            od_src0 = bass.AP(kst, 64, [[C, 64], [128, 8], [1, 64]])
            od_src1 = bass.AP(pK1, 64, [[C, 64], [128, 8], [1, 64]])
            e.tensor_tensor(oddst[:].rearrange("p (g j) -> p g j", j=64),
                            od_src0, od_src1, ALU.mult)
            e.drain().then_inc(S["dv"], 1)
        ev("vector", f_dst)
        n_odd = bump("dv")

        def f_sodd(e, n=n_odd):
            e.wait_ge(S["dv"], n)
            dst = bass.AP(state_sb, 64 * (8 * 128) + 64,
                          [[8 * 128, 64], [128, 8], [1, 64]])
            e.dma_start(out=dst, in_=oddst[:]).then_inc(S["sodd"], 16)
        ev("sync", f_sodd)
        n_stdma = bump("sodd", 16)

        def f_dsrs(e, n=n_stdma):
            e.wait_ge(S["sodd"], n)
            sg = state_sb[:].rearrange("p (g j) -> p g j", j=128)
            e.tensor_reduce(srs_sb[:].unsqueeze(2), sg, AX.X, ALU.add)
            e.drain().then_inc(S["dvt"], 1)
        ev("vector", f_dsrs)
        n_srs = bump("dvt")

        def f_asrs(e, n=n_srs):
            e.wait_ge(S["dvt"], n)
            e.activation(srs_bf[:], srs_sb[:], AF.Identity)
            e.drain().then_inc(S["ac"], 1)
        ev("scalar", f_asrs)
        state_ready = ("ac", bump("ac"))

        stages[f"states_b{b}"] = len(events)
        prev_norm = None
        sq_all = ("sq", val("sq"))
        sxd_all = ("sxd", val("sxd"))
        for tau in range(nt):
            stages[f"p2_b{b}_t{tau}"] = len(events)
            buf = tau % 2
            psem = "p0" if buf == 0 else "p1"

            def f_s2(e, b=b, tau=tau, buf=buf, psem=psem,
                     gx=tuple(x_gate[buf]), gs=stage_gate[buf],
                     gq=sq_all, gxd=sxd_all):
                for g in gx:
                    W(e, g)
                W(e, gs)
                W(e, gq)
                W(e, gxd)
                src = bass.AP(x_h.tensor, (b * seq + tau * 128) * C,
                              [[C, 128], [1, C]])
                e.dma_start(out=xtile[buf][:], in_=src).then_inc(S[psem], 16)
                e.dma_start(out=qstage[buf][:], in_=qscr[tau, :, :]).then_inc(S[psem], 16)
                e.dma_start(out=xdstage[buf][:], in_=xdscr[tau, :, :]).then_inc(S[psem], 16)
            ev("gpsimd", f_s2)
            n_ld2 = bump(psem, 48)

            def f_tam(e, buf=buf, n=n_ld2, psem=psem, ns=state_ready,
                      pn=prev_norm, ga=gA):
                e.wait_ge(S[psem], n)
                W(e, ns)
                W(e, pn)
                W(e, ga)
                for j in range(8):
                    e.matmul(pA[:, j * 128:(j + 1) * 128],
                             state_sb[:, j * 128:(j + 1) * 128],
                             qstage[buf][:, j * 128:(j + 1) * 128],
                             start=True, stop=True)
                for j in range(8):
                    e.matmul(pK0[0:1, 0:128], srs_bf[:, j:j + 1],
                             qstage[buf][:, j * 128:(j + 1) * 128],
                             start=(j == 0), stop=(j == 7))
                e.drain().then_inc(S["mm"], 1)
            ev("tensor", f_tam)
            n_am = bump("mm")

            def f_asq(e, n=n_am):
                e.wait_ge(S["mm"], n)
                e.activation(sq_sb[:], pA[:, :], AF.Square)
                e.drain().then_inc(S["act_t"], 1)
            ev("scalar", f_asq)
            n_sq2 = bump("act_t")

            def f_tsq(e, n=n_sq2):
                e.wait_ge(S["act_t"], n)
                for j in range(8):
                    e.matmul(pK0[0:1, 128:256], onescol[:, 0:1],
                             sq_sb[:, j * 128:(j + 1) * 128],
                             start=(j == 0), stop=(j == 7))
                e.drain().then_inc(S["mm"], 1)
            ev("tensor", f_tsq)
            n_sums = bump("mm")

            def f_dstat(e, n=n_sums):
                e.wait_ge(S["mm"], n)
                e.tensor_scalar(mrow[:], pK0[0:1, 0:128], 1.0 / C, None, ALU.mult)
                e.tensor_tensor(m2row[:], mrow[:], mrow[:], ALU.mult)
                e.scalar_tensor_tensor(vrow[:], pK0[0:1, 128:256], 1.0 / C,
                                       m2row[:], ALU.mult, ALU.subtract)
                e.tensor_scalar(vrow[:], vrow[:], 1e-5, None, ALU.add)
                e.reciprocal(rqrow[:], vrow[:])
                e.tensor_copy(mrow_bf[:], mrow[:])
                e.drain().then_inc(S["dv"], 1)
            ev("vector", f_dstat)
            n_var = bump("dv")

            def f_ar(e, n=n_var):
                e.wait_ge(S["dv"], n)
                e.activation(rrow[:], rqrow[:], AF.Sqrt)
                e.drain().then_inc(S["ac"], 1)
            ev("scalar", f_ar)
            n_rr = bump("ac")

            def f_tbc(e, n=n_rr):
                e.wait_ge(S["ac"], n)
                e.matmul(pK0[:, 256:384], onesb[0:1, :], mrow_bf[:],
                         start=True, stop=True)
                e.matmul(pK0[:, 384:512], onesb[0:1, :], rrow[:],
                         start=True, stop=True)
                e.drain().then_inc(S["mm"], 1)
            ev("tensor", f_tbc)
            n_bc = bump("mm")

            def f_abc(e, n=n_bc):
                e.wait_ge(S["mm"], n)
                e.activation(mtile2[:], pK0[:, 256:512], AF.Identity)
                e.drain().then_inc(S["act_t"], 1)
            ev("scalar", f_abc)
            n_bc2 = bump("act_t")

            def f_dn(e, n=n_bc2):
                e.wait_ge(S["act_t"], n)
                g3 = lambda ap: ap.rearrange("p (g j) -> p g j", j=128)
                mt = mtile2[:, 0:128].unsqueeze(1).broadcast_to([128, 8, 128])
                rt2 = mtile2[:, 128:256].unsqueeze(1).broadcast_to([128, 8, 128])
                e.tensor_tensor(g3(qt[:]), g3(pA[:, :]), mt, ALU.subtract)
                e.tensor_tensor(g3(ln_sb[:]), g3(qt[:]), rt2, ALU.mult)
                e.drain().then_inc(S["dvt"], 1)
            ev("vector", f_dn)
            prev_norm = ("dvt", bump("dvt"))
            gA = prev_norm
            if debug_stats:
                def f_dbg(e, b=b, tau=tau, buf=buf, n=prev_norm[1], n2=n_sq2):
                    e.wait_ge(S["dvt"], n)
                    e.dma_start(out=bass.AP(dbg_h.tensor,
                                            (b * nt + tau) * 128 * 256,
                                            [[256, 128], [1, 256]]),
                                in_=mtile2[:]).then_inc(S["sodd"], 16)
                    e.dma_start(out=bass.AP(dbgq_h.tensor,
                                            (b * nt + tau) * 128 * 1024,
                                            [[1024, 128], [1, 1024]]),
                                in_=qstage[buf][:]).then_inc(S["sodd"], 16)
                    e.dma_start(out=bass.AP(dbgs_h.tensor,
                                            (b * nt + tau) * 128 * 1024,
                                            [[1024, 128], [1, 1024]]),
                                in_=sq_sb[:]).then_inc(S["sodd"], 16)
                ev("sync", f_dbg)
                bump("sodd", 48)

            def f_to(e, buf=buf, n=prev_norm[1], g=gB):
                e.wait_ge(S["dvt"], n)
                W(e, g)
                for ct in range(8):
                    for j in range(2):
                        e.matmul(pB[:, j * 512:(j + 1) * 512],
                                 ln_sb[:, ct * 128:(ct + 1) * 128],
                                 slab["wo"][:, ct * C + j * 512:
                                           ct * C + (j + 1) * 512],
                                 start=(ct == 0), stop=False)
                for ct in range(8):
                    for j in range(2):
                        e.matmul(pB[:, j * 512:(j + 1) * 512],
                                 xdstage[buf][:, ct * 128:(ct + 1) * 128],
                                 slab["wdt"][:, ct * C + j * 512:
                                            ct * C + (j + 1) * 512],
                                 start=False, stop=False)
                for j in range(2):
                    e.matmul(pB[:, j * 512:(j + 1) * 512], onesb[0:3, :],
                             rows3[0:3, j * 512:(j + 1) * 512],
                             start=False, stop=True)
                e.drain().then_inc(S["mme"], 1)
            ev("tensor", f_to)
            n_out = bump("mme")
            stage_gate[buf] = ("mme", n_out)

            def f_do(e, buf=buf, n=n_out, g=out_gate[buf]):
                e.wait_ge(S["mme"], n)
                W(e, g)
                e.scalar_tensor_tensor(out_sb[buf][:], pB[:, :], 1.0,
                                       xtile[buf][:], ALU.mult, ALU.add)
                e.drain().then_inc(S["free"], 1)
                e.drain().then_inc(S["dv"], 1)
            ev("vector", f_do)
            gB = ("free", bump("free"))
            n_o = bump("dv")
            x_gate[buf] = [("dv", n_o)]

            osem = "so0" if buf == 0 else "so1"

            def f_so(e, b=b, tau=tau, buf=buf, n=n_o, osem=osem):
                e.wait_ge(S["dv"], n)
                dst = bass.AP(y_h.tensor, (b * seq + tau * 128) * C,
                              [[C, 128], [1, C]])
                e.dma_start(out=dst, in_=out_sb[buf][:]).then_inc(S[osem], 16)
            ev("sync", f_so)
            out_gate[buf] = (osem, bump(osem, 16))

    stages["all"] = len(events)
    import os as _os
    if _os.environ.get("PRINT_STAGES"):
        print("STAGES:", stages)
    if stop_stage is not None:
        cut = stop_stage if isinstance(stop_stage, int) else stages[stop_stage]
        del events[cut:]
    for name in ("vector", "scalar", "tensor", "gpsimd"):
        def f_fin(e):
            e.drain().then_inc(S["done"], 1)
        ev(name, f_fin)
    n_done = bump("done", 4)
    final_counts = dict(cnt)

    def f_fins(e):
        if stop_stage is None:
            for nm in ("so0", "so1", "sxd", "sq", "sodd"):
                e.wait_ge(S[nm], final_counts[nm])
        e.wait_ge(S["done"], n_done)
        if debug_tail:
            ntail = 0
            fslots = [xn, qa, out_sb[0], out_sb[1], xtile[0], xtile[1]]
            for i, t in enumerate(fslots):
                e.dma_start(out=bass.AP(tailf_h.tensor, i * 128 * 1024,
                                        [[1024, 128], [1, 1024]]),
                            in_=t[:]).then_inc(S["wld"], 16)
                ntail += 16
            for j, nm in enumerate(("sx", "sxx", "mneg", "m2", "u", "sig", "rstd")):
                e.dma_start(out=bass.AP(tailf_h.tensor, 6 * 128 * 1024 + j,
                                        [[1024, 128], [1, 1]]),
                            in_=ln1[nm][:, 0:1],
                            allow_slow_non_contiguous=True).then_inc(S["wld"], 16)
                ntail += 16
            for j, t in enumerate((gmax, s0, rcp, rs_t, ss_t)):
                e.dma_start(out=bass.AP(tailf_h.tensor,
                                        6 * 128 * 1024 + 32 + j * 32,
                                        [[1024, 128], [1, 32]]),
                            in_=t[:, 0:32],
                            allow_slow_non_contiguous=True).then_inc(S["wld"], 16)
                ntail += 16
            e.dma_start(out=bass.AP(tailf_h.tensor, 7 * 128 * 1024,
                                    [[1024, 128], [1, 256]]),
                        in_=mtile2[:],
                        allow_slow_non_contiguous=True).then_inc(S["wld"], 16)
            ntail += 16
            bslots = [state_sb, ks_sb["k0"], ks_sb["v0"], ks_sb["k1"],
                      ks_sb["v1"], None, None, None, None, qstage[0],
                      xdstage[0], sq_sb, ln_sb, None, None]
            for i, t in enumerate(bslots):
                if t is None:
                    continue
                e.dma_start(out=bass.AP(tailb_h.tensor, i * 128 * 1024,
                                        [[1024, 128], [1, 1024]]),
                            in_=t[:]).then_inc(S["wld"], 16)
                ntail += 16
            for j in range(4):
                e.dma_start(out=bass.AP(tailb_h.tensor, (5 + j) * 128 * 1024,
                                        [[1024, 128], [1, 1024]]),
                            in_=xnT[:, j * 1024:(j + 1) * 1024]).then_inc(S["wld"], 16)
                ntail += 16
            e.dma_start(out=bass.AP(tailb_h.tensor, 13 * 128 * 1024,
                                    [[1024, 128], [1, 1024]]),
                        in_=slab["wdt"][:, 0:1024]).then_inc(S["wld"], 16)
            ntail += 16
            e.dma_start(out=bass.AP(tailb_h.tensor, 14 * 128 * 1024,
                                    [[1024, 64], [1, 512]]),
                        in_=oddst[:],
                        allow_slow_non_contiguous=True).then_inc(S["wld"], 16)
            ntail += 16
            for tt in range(nt):
                e.dma_start(out=bass.AP(tailq_h.tensor, tt * 128 * 1024,
                                        [[1024, 128], [1, 1024]]),
                            in_=qscr[tt, :, :]).then_inc(S["wld"], 16)
                e.dma_start(out=bass.AP(tailq_h.tensor,
                                        (nt + tt) * 128 * 1024,
                                        [[1024, 128], [1, 1024]]),
                            in_=xdscr[tt, :, :]).then_inc(S["wld"], 16)
                ntail += 32
            e.wait_ge(S["wld"], final_counts["wld"] + ntail)
        for s in S.values():
            e.sem_clear(s)
    ev("sync", f_fins)

    with nc.Block() as block:
        @block.sync
        def _(eng):
            for name, fn in events:
                if name == "sync":
                    fn(eng)

        @block.vector
        def _(eng):
            for name, fn in events:
                if name == "vector":
                    fn(eng)

        @block.scalar
        def _(eng):
            for name, fn in events:
                if name == "scalar":
                    fn(eng)

        @block.tensor
        def _(eng):
            for name, fn in events:
                if name == "tensor":
                    fn(eng)

        @block.gpsimd
        def _(eng):
            for name, fn in events:
                if name == "gpsimd":
                    fn(eng)

    stk.close()
    return nc


def make_base_inputs(Wq, bq, Wk, bk, Wv, bv, Wo, bo, ln_g, ln_b, Wffn, bffn):
    brows = np.stack([
        np.asarray(bq, np.float32), np.asarray(bk[0], np.float32),
        np.asarray(bk[1], np.float32), np.asarray(bv[0], np.float32),
        np.asarray(bv[1], np.float32), np.asarray(bo, np.float32),
        np.asarray(bffn, np.float32), np.asarray(ln_b, np.float32),
    ]).astype(np.float32)
    return {
        "wq": np.ascontiguousarray(np.asarray(Wq, np.float32)),
        "wk0": np.ascontiguousarray(np.asarray(Wk[0], np.float32)),
        "wk1": np.ascontiguousarray(np.asarray(Wk[1], np.float32)),
        "wv0": np.ascontiguousarray(np.asarray(Wv[0], np.float32)),
        "wv1": np.ascontiguousarray(np.asarray(Wv[1], np.float32)),
        "wo": np.ascontiguousarray(np.asarray(Wo, np.float32)),
        "wffn": np.ascontiguousarray(np.asarray(Wffn, np.float32)),
        "brows": brows,
        "bq_fm": np.ascontiguousarray(np.asarray(bq, np.float32).reshape(8, 128).T),
        "g_fm": np.ascontiguousarray(np.asarray(ln_g, np.float32).reshape(8, 128).T),
        "b_colb": np.ascontiguousarray(
            np.asarray(ln_b, np.float32).reshape(8, 128).T),
        "eyef": np.eye(128, dtype=np.float32),
        "eyeb": np.eye(128, dtype=BF),
        "onesb": np.ones((3, 128), dtype=BF),
        "onesf": np.ones((1, 128), dtype=np.float32),
        "onescol": np.ones((128, 1), dtype=BF),
        "csts": np.tile(np.array([7.5, 1e-30, 8.0, 0.125, 1.0 / 1024,
                                  1.0 / 64, 1e-5, 0],
                                 dtype=np.float32), (128, 1)),
    }


def kernel(L0_attn_x, L0_attn_Wq, L0_attn_bq, L0_attn_Wk, L0_attn_bk,
           L0_attn_Wv, L0_attn_bv, L0_attn_Wo, L0_attn_bo,
           L0_attn_ln_g, L0_attn_ln_b, L0_ffn_weight, L0_ffn_bias):
    x = np.ascontiguousarray(np.asarray(L0_attn_x, dtype=np.float32))
    B = x.shape[0]
    ncores = 8
    per = B // ncores
    nc = build_program(nb=per, nt=NT)
    base = make_base_inputs(L0_attn_Wq, L0_attn_bq, L0_attn_Wk, L0_attn_bk,
                            L0_attn_Wv, L0_attn_bv, L0_attn_Wo, L0_attn_bo,
                            L0_attn_ln_g, L0_attn_ln_b, L0_ffn_weight,
                            L0_ffn_bias)
    in_maps = []
    for c in range(ncores):
        m = dict(base)
        m["x"] = np.ascontiguousarray(x[c * per:(c + 1) * per])
        in_maps.append(m)
    res = run_bass_kernel_spmd(nc, in_maps, core_ids=list(range(ncores)))
    out = np.concatenate([np.asarray(r["y"]) for r in res.results], axis=0)
    return out.astype(np.float32)


# version 07 (final)
# speedup vs baseline: 1.1650x; 1.0184x over previous
import numpy as np
import ml_dtypes
import concourse.bass as bass
import concourse.mybir as mybir
from concourse.bass_utils import run_bass_kernel_spmd
from contextlib import ExitStack

F32 = mybir.dt.float32
BF16 = mybir.dt.bfloat16
I32 = mybir.dt.int32
AF = mybir.ActivationFunctionType
ALU = mybir.AluOpType
AX = mybir.AxisListType
BF = ml_dtypes.bfloat16

C = 1024
H = 16
NT = 16
SCALE = 1.0 / 64.0
BOOT = 64


def build_program(nb=4, nt=NT, stop_stage=None, detect_races=False,
                  debug_stats=False, debug_tail=False):
    nc = bass.Bass("TRN2", target_bir_lowering=False, detect_race_conditions=detect_races)
    seq = nt * 128

    x_h = nc.dram_tensor("x", [nb, seq, C], F32, kind="ExternalInput").ap()
    w_h = {nm: nc.dram_tensor(nm, [C, C], F32, kind="ExternalInput").ap()
           for nm in ("wq", "wk0", "wk1", "wv0", "wv1", "wo")}
    wffn_h = nc.dram_tensor("wffn", [C, C], F32, kind="ExternalInput").ap()
    rows_h = nc.dram_tensor("brows", [8, C], F32, kind="ExternalInput").ap()
    bq_fm_h = nc.dram_tensor("bq_fm", [128, 8], F32, kind="ExternalInput").ap()
    g_fm_h = nc.dram_tensor("g_fm", [128, 8], F32, kind="ExternalInput").ap()
    bcol_h = nc.dram_tensor("b_colb", [128, 8], F32, kind="ExternalInput").ap()
    eyef_h = nc.dram_tensor("eyef", [128, 128], F32, kind="ExternalInput").ap()
    eyeb_h = nc.dram_tensor("eyeb", [128, 128], BF16, kind="ExternalInput").ap()
    onesb_h = nc.dram_tensor("onesb", [3, 128], BF16, kind="ExternalInput").ap()
    onesf_h = nc.dram_tensor("onesf", [1, 128], F32, kind="ExternalInput").ap()
    onescol_h = nc.dram_tensor("onescol", [128, 1], BF16, kind="ExternalInput").ap()
    csts_h = nc.dram_tensor("csts", [128, 8], F32, kind="ExternalInput").ap()
    y_h = nc.dram_tensor("y", [nb, seq, C], F32, kind="ExternalOutput").ap()
    qscr = nc.dram_tensor("qscr", [nt, 128, 8 * 128], BF16).ap()
    dbg_h = (nc.dram_tensor("dbg", [nb, nt, 128, 256], F32, kind="ExternalOutput").ap()
             if debug_stats else None)
    dbgq_h = (nc.dram_tensor("dbgq", [nb, nt, 128, 1024], BF16, kind="ExternalOutput").ap()
              if debug_stats else None)
    dbgs_h = (nc.dram_tensor("dbgs", [nb, nt, 128, 1024], BF16, kind="ExternalOutput").ap()
              if debug_stats else None)
    dbgv_h = (nc.dram_tensor("dbgv", [nb * (nt // 4 + 1), 2, 128, 2048], BF16,
                             kind="ExternalOutput").ap() if debug_stats else None)
    dbgx_h = (nc.dram_tensor("dbgx", [nb * (nt // 4 + 1), 128, 4096], BF16,
                             kind="ExternalOutput").ap() if debug_stats else None)
    dbgl_h = (nc.dram_tensor("dbgl", [nb, nt, 128, 2], F32,
                             kind="ExternalOutput").ap() if debug_stats else None)
    xdscr = nc.dram_tensor("xdscr", [nt, 128, 8 * 128], BF16).ap()
    tailf_h = (nc.dram_tensor("tailf", [8, 128, 1024], F32, kind="ExternalOutput").ap()
               if debug_tail else None)
    tailb_h = (nc.dram_tensor("tailb", [15, 128, 1024], BF16, kind="ExternalOutput").ap()
               if debug_tail else None)
    tailp_h = (nc.dram_tensor("tailp", [3, 128, 1024], F32, kind="ExternalOutput").ap()
               if debug_tail else None)
    tailq_h = (nc.dram_tensor("tailq", [2, nt, 128, 1024], BF16, kind="ExternalOutput").ap()
               if debug_tail else None)

    stk = ExitStack()
    sbt = lambda n, s, d: stk.enter_context(nc.sbuf_tensor(n, s, d))
    psa = lambda n, s: stk.enter_context(nc.psum_tensor(n, s, F32))
    mksem = lambda n: stk.enter_context(nc.semaphore(n))

    slab = {nm: sbt(nm + "_s", [128, 8 * C], BF16) for nm in
            ("wq", "wk0", "wk1", "wv0", "wv1", "wo", "wdt")}
    rows3 = sbt("rows3", [3, C], BF16)
    bq_fm = sbt("bqfm", [128, 8], F32)
    g_fm = sbt("gfm", [128, 8], F32)
    bcol_bf = sbt("bcolb", [128, 8], BF16)
    eyef = sbt("eyefs", [128, 128], F32)
    eyeb = sbt("eyebs", [128, 128], BF16)
    onesb = sbt("onesbs", [3, 128], BF16)
    onesf = sbt("onesfs", [1, 128], F32)
    onescol = sbt("onescols", [128, 1], BF16)
    csts = sbt("cstss", [128, 8], F32)
    krow = sbt("krow", [1, 4 * C], BF16)
    rc75d = sbt("rc75d", [128, 1], F32)
    rc75A = sbt("rc75A", [128, 1], F32)

    xtile = [sbt(f"xt{i}", [128, C], F32) for i in range(2)]
    xn = sbt("xn", [128, C], F32)
    xnT = sbt("xnT", [128, 4 * 8 * 128], BF16)
    ks_sb = {k: sbt("ks_" + k, [128, C], BF16) for k in ("k0", "k1", "v0", "v1")}
    qt = sbt("q_t", [128, C], F32)
    qa = sbt("q_a", [128, C], F32)
    qb = sbt("q_b", [128, C], F32)
    qw = sbt("q_w", [128, C], F32)
    qit = sbt("q_it", [128, C], I32)
    gmax = sbt("gmaxs", [128, 64], F32)
    s0 = sbt("s0s", [128, 64], F32)
    rcp = sbt("rcps", [128, 64], F32)
    rs_t = sbt("rss", [128, 64], F32)
    ss_t = sbt("sss", [128, 64], F32)
    sq_dump = sbt("sqdump", [128, C], BF16)
    ln1 = {n: sbt("l_" + n, [128, 2], F32) for n in
           ("sx", "sxx", "mneg", "m2", "u", "sig", "rinv", "rstd", "mnegA")}
    state_sb = sbt("state", [128, 8 * 128], BF16)
    srs_sb = sbt("srs", [128, 8], F32)
    srs_bf = sbt("srsbf", [128, 8], BF16)
    oddst = sbt("oddst", [64, 8 * 64], BF16)
    qstage = [sbt(f"qstage{i}", [128, 8 * 128], BF16) for i in range(2)]
    xdstage = [sbt(f"xdstage{i}", [128, 8 * 128], BF16) for i in range(2)]
    xdTout = sbt("xdTout", [128, 8 * 128], BF16)
    sq_sb = sbt("sqsb", [128, 8 * 128], BF16)
    ln_sb = sbt("lnsb", [128, 8 * 128], BF16)
    mrow = sbt("mrow", [1, 128], F32)
    mrow_bf = sbt("mrowbf", [1, 128], BF16)
    rrow = sbt("rrow", [1, 128], BF16)
    vrow = sbt("vrow", [1, 128], F32)
    m2row = sbt("m2row", [1, 128], F32)
    rqrow = sbt("rqrow", [1, 128], F32)
    mtile2 = sbt("mtile2", [128, 256], F32)
    out_sb = [sbt(f"outsb{i}", [128, C], F32) for i in range(2)]
    qevac = sbt("qevac", [128, 4 * 512], BF16)
    wfstage = out_sb
    kst = out_sb[0]

    pA = psa("pA", [128, C])
    pB = psa("pB", [128, C])
    pK0 = psa("pK0", [128, C])
    pK1 = psa("pK1", [128, C])

    semnames = ("x0", "x1", "x2", "p0", "p1", "wf0", "wf1", "wld", "sodd",
                "sxd", "sq", "so0", "so1",
                "dv", "ac", "dvt", "act_t", "mm", "mme", "free", "gp", "done")
    S = {n: mksem("s_" + n) for n in semnames}
    cnt = {n: 0 for n in semnames}
    events = []
    stages = {}

    def ev(engine, fn):
        events.append((engine, fn))

    def bump(name, by=1):
        cnt[name] += by
        return cnt[name]

    def val(name):
        return cnt[name]

    def W(e, gate):
        if gate is not None and gate[1] > 0:
            e.wait_ge(S[gate[0]], gate[1])

    def f_sync0(e):
        for s in S.values():
            e.sem_clear(s)
        e.sem_inc(S["done"], BOOT)
        for dst, src in ((bq_fm, bq_fm_h), (g_fm, g_fm_h),
                         (eyef, eyef_h), (eyeb, eyeb_h),
                         (onesb, onesb_h), (onesf, onesf_h),
                         (onescol, onescol_h), (csts, csts_h)):
            e.dma_start(out=dst[:], in_=src[:]).then_inc(S["wld"], 16)
    ev("sync", f_sync0)
    bump("done", BOOT)
    n_consts = bump("wld", 128)

    def f_gp0(e):
        e.wait_ge(S["done"], BOOT)
        for nm in ("wk0", "wv0", "wk1", "wv1", "wq", "wo"):
            src = bass.AP(w_h[nm].tensor, 0, [[C, 128], [128 * C, 8], [1, C]])
            e.dma_start(out=slab[nm][:], in_=src).then_inc(S["gp"], 16)
        e.dma_start(out=rows3[1:2, :],
                    in_=bass.AP(rows_h.tensor, 5 * C, [[C, 1], [1, C]])).then_inc(S["gp"], 16)
        e.dma_start(out=rows3[2:3, :],
                    in_=bass.AP(rows_h.tensor, 6 * C, [[C, 1], [1, C]])).then_inc(S["gp"], 16)
        e.dma_start(out=bcol_bf[:], in_=bcol_h[:]).then_inc(S["gp"], 16)
        for i, r in enumerate((1, 3, 2, 4)):
            e.dma_start(out=krow[0:1, i * C:(i + 1) * C],
                        in_=bass.AP(rows_h.tensor, r * C,
                                    [[C, 1], [1, C]])).then_inc(S["gp"], 16)
    ev("gpsimd", f_gp0)
    n_wslabs = bump("gp", 96)
    n_rowsbf = bump("gp", 112)

    def f_dve0(e):
        e.wait_ge(S["done"], BOOT)
        e.wait_ge(S["wld"], n_consts)
        e.wait_ge(S["gp"], n_rowsbf)
        e.reciprocal(rc75d[:], csts[:, 0:1])
        for nm in ("wq", "wk0", "wk1", "wv0", "wv1"):
            for ct in range(8):
                d0 = ct * C + ct * 128
                e.tensor_tensor(slab[nm][:, d0:d0 + 128],
                                slab[nm][:, d0:d0 + 128], eyeb[:], ALU.add)
        e.drain().then_inc(S["dvt"], 1)
    ev("vector", f_dve0)
    n_wplusI = bump("dvt")

    def f_t0(e):
        e.wait_ge(S["done"], BOOT)
        e.wait_ge(S["gp"], n_rowsbf)
        e.wait_ge(S["dvt"], n_wplusI)
        for j in range(2):
            for ct in range(8):
                e.matmul(pA[0:1, j * 512:(j + 1) * 512], bcol_bf[:, ct:ct + 1],
                         slab["wo"][:, ct * C + j * 512: ct * C + (j + 1) * 512],
                         start=(ct == 0), stop=(ct == 7))
        e.drain().then_inc(S["mm"], 1)
    ev("tensor", f_t0)
    n_bwo_mm = bump("mm")

    def f_a0(e):
        e.wait_ge(S["done"], BOOT)
        e.wait_ge(S["wld"], n_consts)
        e.wait_ge(S["gp"], n_rowsbf)
        e.wait_ge(S["mm"], n_bwo_mm)
        e.activation(rows3[0:1, :], pA[0:1, :], AF.Identity)
        e.activation(rc75A[:], rc75d[:], AF.Identity)
        e.drain().then_inc(S["act_t"], 1)
    ev("scalar", f_a0)
    gA = ("act_t", bump("act_t"))

    def f_awo(e, g=gA):
        e.wait_ge(S[g[0]], g[1])
        for ct in range(8):
            e.activation(slab["wo"][:, ct * C:(ct + 1) * C],
                         slab["wo"][:, ct * C:(ct + 1) * C],
                         AF.Identity, bias=0.0, scale=g_fm[:, ct:ct + 1])
        e.drain().then_inc(S["ac"], 1)
    ev("scalar", f_awo)
    n_wo_ready = bump("ac")

    gB = ("free", 0)

    def emit_scale_chain(off=0, n_gm=None):
        if n_gm is None:
            n_gm = bump("dv")
        sl = slice(off, off + 32)

        def f_ac(e, n=n_gm, sl=sl):
            e.wait_ge(S["dv"], n)
            e.activation(s0[:, sl], gmax[:, sl], AF.Identity,
                         bias=csts[:, 1:2], scale=rc75A[:])
            e.drain().then_inc(S["ac"], 1)
        ev("scalar", f_ac)
        n_s0 = bump("ac")

        def f_dr(e, n=n_s0, sl=sl):
            e.wait_ge(S["ac"], n)
            e.reciprocal(rcp[:, sl], s0[:, sl])
            e.drain().then_inc(S["dv"], 1)
        ev("vector", f_dr)
        n_rc = bump("dv")

        def f_ac2(e, n=n_rc, sl=sl):
            e.wait_ge(S["dv"], n)
            e.activation(rs_t[:, sl], rcp[:, sl], AF.Identity,
                         bias=0.0, scale=csts[:, 2:3])
            e.activation(ss_t[:, sl], s0[:, sl], AF.Identity,
                         bias=0.0, scale=csts[:, 3:4])
            e.drain().then_inc(S["ac"], 1)
        ev("scalar", f_ac2)
        return bump("ac")

    def emit_quant_dve(e, src, off=0):
        g3 = lambda ap: ap.rearrange("p (g j) -> p g j", j=32)
        rs_b = rs_t[:, off:off + 32].unsqueeze(2).broadcast_to([128, 32, 32])
        ss_b = ss_t[:, off:off + 32].unsqueeze(2).broadcast_to([128, 32, 32])
        e.tensor_tensor(g3(qt[:]), g3(src[:]), rs_b, ALU.mult)
        e.tensor_scalar(qa[:], qt[:], 16.0, -16.0, ALU.min, ALU.max)
        e.tensor_scalar(qb[:], qt[:], 32.0, -32.0, ALU.min, ALU.max)
        e.tensor_scalar(qw[:], qt[:], 0.25, 64.5, ALU.mult, ALU.add)
        e.scalar_tensor_tensor(qw[:], qb[:], 0.25, qw[:], ALU.mult, ALU.add)
        e.scalar_tensor_tensor(qw[:], qa[:], 0.5, qw[:], ALU.mult, ALU.add)
        e.tensor_copy(qit[:], qw[:])
        e.tensor_tensor(qa[:], qw[:], qit[:], ALU.is_ge)
        e.tensor_scalar(qb[:], qit[:], 1.0, None, ALU.add)
        e.tensor_tensor(qb[:], qw[:], qb[:], ALU.is_ge)
        e.tensor_tensor(qa[:], qa[:], qb[:], ALU.add)
        e.scalar_tensor_tensor(qt[:], qit[:], -65.0, qa[:], ALU.add, ALU.add)
        e.tensor_scalar(qa[:], qt[:], 24.0, -24.0, ALU.min, ALU.max)
        e.tensor_scalar(qb[:], qt[:], 16.0, -16.0, ALU.min, ALU.max)
        e.tensor_scalar(qw[:], qt[:], 4.0, None, ALU.mult)
        e.scalar_tensor_tensor(qw[:], qa[:], -2.0, qw[:], ALU.mult, ALU.add)
        e.scalar_tensor_tensor(qw[:], qb[:], -1.0, qw[:], ALU.mult, ALU.add)
        e.tensor_tensor(g3(qa[:]), g3(qw[:]), ss_b, ALU.mult)

    def f_swf(e):
        for j in range(2):
            src = bass.AP(wffn_h.tensor, j * 128 * C, [[C, 128], [1, C]])
            e.dma_start(out=wfstage[j][:], in_=src).then_inc(S["wf" + str(j)], 16)
    ev("sync", f_swf)
    bump("wf0", 16)
    bump("wf1", 16)

    prev_txd = 0
    for jt in range(8):
        wsem = "wf" + str(jt % 2)
        n_have = val(wsem)

        def f_dg(e, jt=jt, n=n_have, wsem=wsem):
            e.wait_ge(S[wsem], n)
            xg = wfstage[jt % 2][:].rearrange("p (g j) -> p g j", j=32)
            e.tensor_reduce(gmax[:, 0:32].unsqueeze(2), xg, AX.X, ALU.max,
                            apply_absolute_value=True)
            e.drain().then_inc(S["dv"], 1)
        ev("vector", f_dg)
        n_rs = emit_scale_chain()

        def f_dq(e, jt=jt, n=n_rs, g=prev_txd):
            e.wait_ge(S["ac"], n)
            if g:
                e.wait_ge(S["mm"], g)
            emit_quant_dve(e, wfstage[jt % 2])
            e.drain().then_inc(S["dvt"], 1)
        ev("vector", f_dq)
        n_xd = bump("dvt")

        if jt + 2 < 8:
            def f_sn(e, jt=jt, n=n_xd):
                e.wait_ge(S["dvt"], n)
                src = bass.AP(wffn_h.tensor, (jt + 2) * 128 * C, [[C, 128], [1, C]])
                e.dma_start(out=wfstage[jt % 2][:], in_=src).then_inc(
                    S["wf" + str(jt % 2)], 16)
            ev("sync", f_sn)
            bump("wf" + str(jt % 2), 16)

        def f_tt(e, n=n_xd, g=gA):
            e.wait_ge(S["dvt"], n)
            W(e, g)
            for ct in range(8):
                e.transpose(pA[:, ct * 128:(ct + 1) * 128],
                            qa[:, ct * 128:(ct + 1) * 128], eyef[:])
            e.drain().then_inc(S["mm"], 1)
        ev("tensor", f_tt)
        prev_txd = bump("mm")

        def f_ae(e, jt=jt, n=prev_txd):
            e.wait_ge(S["mm"], n)
            for ct in range(8):
                e.activation(slab["wdt"][:, ct * C + jt * 128: ct * C + jt * 128 + 128],
                             pA[:, ct * 128:(ct + 1) * 128], AF.Identity)
            e.drain().then_inc(S["act_t"], 1)
        ev("scalar", f_ae)
        gA = ("act_t", bump("act_t"))

    stages["startup_ffn"] = len(events)
    x_gate = [[], []]
    stage_gate = [None, None]
    out_gate = [None, None]
    xnT_gate = None
    xd_store_gate = None
    q_store_gate = None
    state_ready = None
    prev_txn = 0
    prev_txd_x = prev_txd

    for b in range(nb):
        for tp in range(nt // 2):
            pair = (2 * tp, 2 * tp + 1)
            for tau in pair:
                pb = tau % 2
                xb = tau % 2
                xsem = f"x{xb}"

                def f_sx(e, b=b, tau=tau, xb=xb, gs=tuple(x_gate[xb]),
                         xsem=xsem):
                    for g in gs:
                        W(e, g)
                    src = bass.AP(x_h.tensor, (b * seq + tau * 128) * C,
                                  [[C, 128], [1, C]])
                    e.dma_start(out=xtile[xb][:], in_=src).then_inc(S[xsem], 16)
                ev("gpsimd", f_sx)
                n_x = bump(xsem, 16)

                def f_a1(e, xb=xb, pb=pb, n=n_x, xsem=xsem):
                    e.wait_ge(S[xsem], n)
                    e.activation(sq_dump[:], xtile[xb][:], AF.Square,
                                 accum_out=ln1["sxx"][:, pb:pb + 1])
                    e.activation(sq_dump[:], xtile[xb][:], AF.Identity,
                                 accum_out=ln1["sx"][:, pb:pb + 1])
                    e.drain().then_inc(S["ac"], 1)
                ev("scalar", f_a1)
                n_sxx = bump("ac")

            def f_d2(e, n=n_sxx):
                e.wait_ge(S["ac"], n)
                e.tensor_scalar(ln1["mneg"][:], ln1["sx"][:], -1.0 / C, None,
                                ALU.mult)
                e.drain().then_inc(S["dv"], 1)
            ev("vector", f_d2)
            n_mn = bump("dv")

            def f_a1b(e, n=n_mn):
                e.wait_ge(S["dv"], n)
                e.activation(ln1["m2"][:], ln1["mneg"][:], AF.Square)
                e.activation(ln1["mnegA"][:], ln1["mneg"][:], AF.Identity)
                e.drain().then_inc(S["ac"], 1)
            ev("scalar", f_a1b)
            n_m2 = bump("ac")

            def f_d2b(e, n=n_m2):
                e.wait_ge(S["ac"], n)
                e.scalar_tensor_tensor(ln1["u"][:], ln1["sxx"][:], 1.0 / C,
                                       ln1["m2"][:], ALU.mult, ALU.subtract)
                e.drain().then_inc(S["dv"], 1)
            ev("vector", f_d2b)
            n_u = bump("dv")

            def f_a2(e, n=n_u):
                e.wait_ge(S["dv"], n)
                e.activation(ln1["sig"][:], ln1["u"][:], AF.Sqrt,
                             bias=csts[:, 6:7], scale=1.0)
                e.drain().then_inc(S["ac"], 1)
            ev("scalar", f_a2)
            n_sig = bump("ac")

            def f_d2c(e, n=n_sig):
                e.wait_ge(S["ac"], n)
                e.reciprocal(ln1["rinv"][:], ln1["sig"][:])
                e.drain().then_inc(S["dv"], 1)
            ev("vector", f_d2c)
            n_ri = bump("dv")

            def f_a2b(e, n=n_ri):
                e.wait_ge(S["dv"], n)
                e.activation(ln1["rstd"][:], ln1["rinv"][:], AF.Identity)
                e.drain().then_inc(S["ac"], 1)
            ev("scalar", f_a2b)
            n_rstd = bump("ac")

            for tau in pair:
              pb = tau % 2
              xb = tau % 2
              if True:
                def f_d3(e, pb=pb, xb=xb, n=n_rstd, g=prev_txn):
                    e.wait_ge(S["ac"], n)
                    if g:
                        e.wait_ge(S["mm"], g)
                    mb = ln1["mnegA"][:, pb:pb + 1].broadcast_to([128, C])
                    rb = ln1["rstd"][:, pb:pb + 1].broadcast_to([128, C])
                    e.tensor_tensor(qt[:], xtile[xb][:], mb, ALU.add)
                    e.tensor_tensor(xn[:], qt[:], rb, ALU.mult)
                    xg = xn[:].rearrange("p (g j) -> p g j", j=32)
                    e.tensor_reduce(
                        gmax[:, pb * 32:pb * 32 + 32].unsqueeze(2), xg,
                        AX.X, ALU.max, apply_absolute_value=True)
                    e.drain().then_inc(S["dv"], 1)
                ev("vector", f_d3)
                n_xnr = val("dv") + 1
                if debug_stats:
                    def f_dl(e, b=b, tau=tau, pb=pb, n=n_xnr):
                        e.wait_ge(S["dv"], n)
                        e.dma_start(out=bass.AP(dbgl_h.tensor, (b * nt + tau) * 256,
                                                [[2, 128], [1, 1]]),
                                    in_=ln1["mneg"][:, pb:pb + 1],
                                    allow_slow_non_contiguous=True).then_inc(S["sodd"], 16)
                        e.dma_start(out=bass.AP(dbgl_h.tensor, (b * nt + tau) * 256 + 1,
                                                [[2, 128], [1, 1]]),
                                    in_=ln1["rstd"][:, pb:pb + 1],
                                    allow_slow_non_contiguous=True).then_inc(S["sodd"], 16)
                    ev("sync", f_dl)
                    bump("sodd", 32)
                x_gate[xb] = [("dv", n_xnr)]
                n_rs = emit_scale_chain(off=pb * 32)

                def f_d5(e, n=n_rs, g=prev_txd_x, pb=pb):
                    e.wait_ge(S["ac"], n)
                    if g:
                        e.wait_ge(S["mm"], g)
                    emit_quant_dve(e, xn, off=pb * 32)
                    e.drain().then_inc(S["dvt"], 1)
                ev("vector", f_d5)
                n_xdr = bump("dvt")

                slot = tau % 4

                def f_t1(e, n=n_xnr, g=gA):
                    e.wait_ge(S["dv"], n)
                    W(e, g)
                    for ct in range(8):
                        e.transpose(pA[:, ct * 128:(ct + 1) * 128],
                                    xn[:, ct * 128:(ct + 1) * 128],
                                    eyef[:])
                    e.drain().then_inc(S["mm"], 1)
                ev("tensor", f_t1)
                prev_txn = bump("mm")

                def f_ax(e, n=prev_txn, slot=slot, g=xnT_gate):
                    e.wait_ge(S["mm"], n)
                    W(e, g)
                    for ct in range(8):
                        e.activation(xnT[:, ct * 512 + slot * 128:
                                         ct * 512 + slot * 128 + 128],
                                     pA[:, ct * 128:(ct + 1) * 128], AF.Identity)
                    e.drain().then_inc(S["act_t"], 1)
                ev("scalar", f_ax)
                gA = ("act_t", bump("act_t"))
                n_xnT = gA[1]

                def f_t2(e, n=n_xdr, g=gA):
                    e.wait_ge(S["dvt"], n)
                    W(e, g)
                    for ct in range(8):
                        e.transpose(pA[:, ct * 128:(ct + 1) * 128],
                                    qa[:, ct * 128:(ct + 1) * 128], eyef[:])
                    e.drain().then_inc(S["mm"], 1)
                ev("tensor", f_t2)
                prev_txd_x = bump("mm")

                def f_axd(e, n=prev_txd_x, g=xd_store_gate):
                    e.wait_ge(S["mm"], n)
                    W(e, g)
                    e.activation(xdTout[:], pA[:, :], AF.Identity)
                    e.drain().then_inc(S["act_t"], 1)
                ev("scalar", f_axd)
                gA = ("act_t", bump("act_t"))
                n_xdT = gA[1]

                def f_sxd(e, tau=tau, n=n_xdT):
                    e.wait_ge(S["act_t"], n)
                    e.dma_start(out=xdscr[tau, :, :], in_=xdTout[:]).then_inc(S["sxd"], 16)
                ev("sync", f_sxd)
                xd_store_gate = ("sxd", bump("sxd", 16))

                states_prev = ("mm", val("mm"))
                for rk, wnm, bank in (("k0", "wk0", pB), ("v0", "wv0", pA),
                                      ("k1", "wk1", pB), ("v1", "wv1", pA)):
                    gbank = gB if bank is pB else gA

                    kseg = {"k0": 0, "v0": 1, "k1": 2, "v1": 3}[rk]

                    def f_tr(e, wnm=wnm, bank=bank, slot=slot, nx=n_xnT, g=gbank,
                             kseg=kseg):
                        e.wait_ge(S["act_t"], nx)
                        W(e, g)
                        e.wait_ge(S["gp"], n_rowsbf)
                        for ct in range(8):
                            lhs = xnT[:, ct * 512 + slot * 128:
                                      ct * 512 + slot * 128 + 128]
                            for j in range(2):
                                e.matmul(bank[:, j * 512:(j + 1) * 512], lhs,
                                         slab[wnm][:, ct * C + j * 512:
                                                   ct * C + (j + 1) * 512],
                                         start=(ct == 0), stop=False)
                        for j in range(2):
                            e.matmul(bank[:, j * 512:(j + 1) * 512], onesb[0:1, :],
                                     krow[0:1, kseg * C + j * 512:
                                          kseg * C + (j + 1) * 512],
                                     start=False, stop=True)
                        e.drain().then_inc(S["mme"], 1)
                    ev("tensor", f_tr)
                    n_round = bump("mme")

                    def f_ake(e, rk=rk, bank=bank, n=n_round, g=states_prev):
                        e.wait_ge(S["mme"], n)
                        W(e, g)
                        e.activation(ks_sb[rk][:], bank[:, :], AF.Identity,
                                     bias=0.0, scale=csts[:, 5:6])
                        e.drain().then_inc(S["free"], 1)
                        e.drain().then_inc(S["ac"], 1)
                    ev("scalar", f_ake)
                    nf = bump("free")
                    if bank is pB:
                        gB = ("free", nf)
                    else:
                        gA = ("free", nf)
                    n_ks = bump("ac")

                    if rk in ("v0", "v1"):
                        k = "0" if rk == "v0" else "1"
                        pS = pK0 if k == "0" else pK1

                        def f_ts(e, k=k, pS=pS, tau=tau, n=n_ks):
                            e.wait_ge(S["ac"], n)
                            for h in range(H):
                                e.matmul(pS[0:64, h * 64:(h + 1) * 64],
                                         ks_sb["k" + k][:, h * 64:(h + 1) * 64],
                                         ks_sb["v" + k][:, h * 64:(h + 1) * 64],
                                         start=(tau == 0 and h % 8 == 0),
                                         stop=(tau == nt - 1 and h % 8 == 7))
                            e.drain().then_inc(S["mm"], 1)
                        ev("tensor", f_ts)
                        bump("mm")

                if slot == 3 or tau == nt - 1:
                    span0 = tau - slot
                    nsp = slot + 1
                    for half in range(2):
                        def f_tq(e, half=half, nsp=nsp, ga=gA, gb=gB):
                            W(e, ga)
                            W(e, gb)
                            for dt in range(4):
                                d0 = half * 4 + dt
                                bankq = pA if dt < 2 else pB
                                for ct in range(8):
                                    lhs = slab["wq"][:, ct * C + d0 * 128:
                                                     ct * C + d0 * 128 + 128]
                                    mv = xnT[:, ct * 512: ct * 512 + nsp * 128]
                                    e.matmul(bankq[:, (dt % 2) * 512:
                                                   (dt % 2) * 512 + nsp * 128],
                                             lhs, mv, start=(ct == 0), stop=(ct == 7))
                            e.drain().then_inc(S["mme"], 1)
                        ev("tensor", f_tq)
                        n_q = bump("mme")

                        def f_aq(e, half=half, nsp=nsp, n=n_q, g=q_store_gate):
                            e.wait_ge(S["mme"], n)
                            W(e, g)
                            for dt in range(4):
                                d0 = half * 4 + dt
                                bankq = pA if dt < 2 else pB
                                e.activation(qevac[:, dt * 512:dt * 512 + nsp * 128],
                                             bankq[:, (dt % 2) * 512:
                                                   (dt % 2) * 512 + nsp * 128],
                                             AF.Identity, bias=bq_fm[:, d0:d0 + 1])
                            e.drain().then_inc(S["free"], 2)
                            e.drain().then_inc(S["act_t"], 1)
                        ev("scalar", f_aq)
                        nf = bump("free", 2)
                        gA = ("free", nf)
                        gB = ("free", nf)
                        n_qe = bump("act_t")

                        def f_sq(e, half=half, span0=span0, nsp=nsp, n=n_qe,
                                 b=b, spi=tau // 4):
                            e.wait_ge(S["act_t"], n)
                            if debug_stats and half == 0:
                                e.dma_start(out=bass.AP(dbgx_h.tensor,
                                            (b * (nt // 4 + 1) + spi) * 128 * 4096,
                                            [[4096, 128], [1, 4096]]),
                                            in_=xnT[:]).then_inc(S["sodd"], 16)
                            if debug_stats:
                                e.dma_start(out=bass.AP(dbgv_h.tensor,
                                            ((b * (nt // 4 + 1) + spi) * 2 + half) * 128 * 2048,
                                            [[2048, 128], [1, 2048]]),
                                            in_=qevac[:]).then_inc(S["sodd"], 16)
                            for t in range(nsp):
                                dst = bass.AP(qscr.tensor,
                                              (span0 + t) * 128 * 1024 + half * 4 * 128,
                                              [[1024, 128], [128, 4], [1, 128]])
                                srcap = bass.AP(qevac, t * 128,
                                                [[4 * 512, 128], [512, 4], [1, 128]])
                                e.dma_start(out=dst, in_=srcap).then_inc(S["sq"], 16)
                        ev("sync", f_sq)
                        if debug_stats:
                            bump("sodd", 16 if half else 32)
                        q_store_gate = ("sq", bump("sq", 16 * nsp))
                    xnT_gate = ("mme", val("mme"))

        stages[f"phase1_b{b}"] = len(events)
        n_states_done = val("mm")
        og = (out_gate[0], out_gate[1])

        def f_dst(e, n=n_states_done, og=og):
            e.wait_ge(S["mm"], n)
            for g in og:
                W(e, g)
            e.memset(state_sb[:], 0.0)
            e.tensor_copy(kst[0:64, :], pK0[0:64, :])
            ev_src0 = bass.AP(kst, 0, [[C, 64], [128, 8], [1, 64]])
            ev_src1 = bass.AP(pK1, 0, [[C, 64], [128, 8], [1, 64]])
            ev_dst = bass.AP(state_sb, 0, [[8 * 128, 64], [128, 8], [1, 64]])
            e.tensor_tensor(ev_dst, ev_src0, ev_src1, ALU.mult)
            od_src0 = bass.AP(kst, 64, [[C, 64], [128, 8], [1, 64]])
            od_src1 = bass.AP(pK1, 64, [[C, 64], [128, 8], [1, 64]])
            e.tensor_tensor(oddst[:].rearrange("p (g j) -> p g j", j=64),
                            od_src0, od_src1, ALU.mult)
            e.drain().then_inc(S["dv"], 1)
        ev("vector", f_dst)
        n_odd = bump("dv")

        def f_sodd(e, n=n_odd):
            e.wait_ge(S["dv"], n)
            dst = bass.AP(state_sb, 64 * (8 * 128) + 64,
                          [[8 * 128, 64], [128, 8], [1, 64]])
            e.dma_start(out=dst, in_=oddst[:]).then_inc(S["sodd"], 16)
        ev("sync", f_sodd)
        n_stdma = bump("sodd", 16)

        def f_dsrs(e, n=n_stdma):
            e.wait_ge(S["sodd"], n)
            sg = state_sb[:].rearrange("p (g j) -> p g j", j=128)
            e.tensor_reduce(srs_sb[:].unsqueeze(2), sg, AX.X, ALU.add)
            e.drain().then_inc(S["dvt"], 1)
        ev("vector", f_dsrs)
        n_srs = bump("dvt")

        def f_asrs(e, n=n_srs):
            e.wait_ge(S["dvt"], n)
            e.activation(srs_bf[:], srs_sb[:], AF.Identity)
            e.drain().then_inc(S["ac"], 1)
        ev("scalar", f_asrs)
        state_ready = ("ac", bump("ac"))

        stages[f"states_b{b}"] = len(events)
        prev_norm = None
        sq_all = ("sq", val("sq"))
        sxd_all = ("sxd", val("sxd"))
        for tau in range(nt):
            stages[f"p2_b{b}_t{tau}"] = len(events)
            buf = tau % 2
            psem = "p0" if buf == 0 else "p1"

            def f_s2(e, b=b, tau=tau, buf=buf, psem=psem,
                     gx=tuple(x_gate[buf]), gs=stage_gate[buf],
                     gq=sq_all, gxd=sxd_all):
                for g in gx:
                    W(e, g)
                W(e, gs)
                W(e, gq)
                W(e, gxd)
                src = bass.AP(x_h.tensor, (b * seq + tau * 128) * C,
                              [[C, 128], [1, C]])
                e.dma_start(out=xtile[buf][:], in_=src).then_inc(S[psem], 16)
                e.dma_start(out=qstage[buf][:], in_=qscr[tau, :, :]).then_inc(S[psem], 16)
                e.dma_start(out=xdstage[buf][:], in_=xdscr[tau, :, :]).then_inc(S[psem], 16)
            ev("gpsimd", f_s2)
            n_ld2 = bump(psem, 48)

            def f_tam(e, buf=buf, n=n_ld2, psem=psem, ns=state_ready,
                      pn=prev_norm, ga=gA):
                e.wait_ge(S[psem], n)
                W(e, ns)
                W(e, pn)
                W(e, ga)
                for j in range(8):
                    e.matmul(pA[:, j * 128:(j + 1) * 128],
                             state_sb[:, j * 128:(j + 1) * 128],
                             qstage[buf][:, j * 128:(j + 1) * 128],
                             start=True, stop=True)
                for j in range(8):
                    e.matmul(pK0[0:1, 0:128], srs_bf[:, j:j + 1],
                             qstage[buf][:, j * 128:(j + 1) * 128],
                             start=(j == 0), stop=(j == 7))
                e.drain().then_inc(S["mm"], 1)
            ev("tensor", f_tam)
            n_am = bump("mm")

            def f_asq(e, n=n_am):
                e.wait_ge(S["mm"], n)
                e.activation(sq_sb[:], pA[:, :], AF.Square)
                e.drain().then_inc(S["act_t"], 1)
            ev("scalar", f_asq)
            n_sq2 = bump("act_t")

            def f_tsq(e, n=n_sq2):
                e.wait_ge(S["act_t"], n)
                for j in range(8):
                    e.matmul(pK0[0:1, 128:256], onescol[:, 0:1],
                             sq_sb[:, j * 128:(j + 1) * 128],
                             start=(j == 0), stop=(j == 7))
                e.drain().then_inc(S["mm"], 1)
            ev("tensor", f_tsq)
            n_sums = bump("mm")

            def f_dstat(e, n=n_sums):
                e.wait_ge(S["mm"], n)
                e.tensor_scalar(mrow[:], pK0[0:1, 0:128], 1.0 / C, None, ALU.mult)
                e.tensor_tensor(m2row[:], mrow[:], mrow[:], ALU.mult)
                e.scalar_tensor_tensor(vrow[:], pK0[0:1, 128:256], 1.0 / C,
                                       m2row[:], ALU.mult, ALU.subtract)
                e.tensor_scalar(vrow[:], vrow[:], 1e-5, None, ALU.add)
                e.reciprocal(rqrow[:], vrow[:])
                e.tensor_copy(mrow_bf[:], mrow[:])
                e.drain().then_inc(S["dv"], 1)
            ev("vector", f_dstat)
            n_var = bump("dv")

            def f_ar(e, n=n_var):
                e.wait_ge(S["dv"], n)
                e.activation(rrow[:], rqrow[:], AF.Sqrt)
                e.drain().then_inc(S["ac"], 1)
            ev("scalar", f_ar)
            n_rr = bump("ac")

            def f_tbc(e, n=n_rr):
                e.wait_ge(S["ac"], n)
                e.matmul(pK0[:, 256:384], onesb[0:1, :], mrow_bf[:],
                         start=True, stop=True)
                e.matmul(pK0[:, 384:512], onesb[0:1, :], rrow[:],
                         start=True, stop=True)
                e.drain().then_inc(S["mm"], 1)
            ev("tensor", f_tbc)
            n_bc = bump("mm")

            def f_abc(e, n=n_bc):
                e.wait_ge(S["mm"], n)
                e.activation(mtile2[:], pK0[:, 256:512], AF.Identity)
                e.drain().then_inc(S["act_t"], 1)
            ev("scalar", f_abc)
            n_bc2 = bump("act_t")

            def f_dn(e, n=n_bc2):
                e.wait_ge(S["act_t"], n)
                g3 = lambda ap: ap.rearrange("p (g j) -> p g j", j=128)
                mt = mtile2[:, 0:128].unsqueeze(1).broadcast_to([128, 8, 128])
                rt2 = mtile2[:, 128:256].unsqueeze(1).broadcast_to([128, 8, 128])
                e.tensor_tensor(g3(qt[:]), g3(pA[:, :]), mt, ALU.subtract)
                e.tensor_tensor(g3(ln_sb[:]), g3(qt[:]), rt2, ALU.mult)
                e.drain().then_inc(S["dvt"], 1)
            ev("vector", f_dn)
            prev_norm = ("dvt", bump("dvt"))
            gA = prev_norm
            if debug_stats:
                def f_dbg(e, b=b, tau=tau, buf=buf, n=prev_norm[1], n2=n_sq2):
                    e.wait_ge(S["dvt"], n)
                    e.dma_start(out=bass.AP(dbg_h.tensor,
                                            (b * nt + tau) * 128 * 256,
                                            [[256, 128], [1, 256]]),
                                in_=mtile2[:]).then_inc(S["sodd"], 16)
                    e.dma_start(out=bass.AP(dbgq_h.tensor,
                                            (b * nt + tau) * 128 * 1024,
                                            [[1024, 128], [1, 1024]]),
                                in_=qstage[buf][:]).then_inc(S["sodd"], 16)
                    e.dma_start(out=bass.AP(dbgs_h.tensor,
                                            (b * nt + tau) * 128 * 1024,
                                            [[1024, 128], [1, 1024]]),
                                in_=sq_sb[:]).then_inc(S["sodd"], 16)
                ev("sync", f_dbg)
                bump("sodd", 48)

            def f_to(e, buf=buf, n=prev_norm[1], g=gB):
                e.wait_ge(S["dvt"], n)
                W(e, g)
                for ct in range(8):
                    for j in range(2):
                        e.matmul(pB[:, j * 512:(j + 1) * 512],
                                 ln_sb[:, ct * 128:(ct + 1) * 128],
                                 slab["wo"][:, ct * C + j * 512:
                                           ct * C + (j + 1) * 512],
                                 start=(ct == 0), stop=False)
                for ct in range(8):
                    for j in range(2):
                        e.matmul(pB[:, j * 512:(j + 1) * 512],
                                 xdstage[buf][:, ct * 128:(ct + 1) * 128],
                                 slab["wdt"][:, ct * C + j * 512:
                                            ct * C + (j + 1) * 512],
                                 start=False, stop=False)
                for j in range(2):
                    e.matmul(pB[:, j * 512:(j + 1) * 512], onesb[0:3, :],
                             rows3[0:3, j * 512:(j + 1) * 512],
                             start=False, stop=True)
                e.drain().then_inc(S["mme"], 1)
            ev("tensor", f_to)
            n_out = bump("mme")
            stage_gate[buf] = ("mme", n_out)

            def f_do(e, buf=buf, n=n_out, g=out_gate[buf]):
                e.wait_ge(S["mme"], n)
                W(e, g)
                e.scalar_tensor_tensor(out_sb[buf][:], pB[:, :], 1.0,
                                       xtile[buf][:], ALU.mult, ALU.add)
                e.drain().then_inc(S["free"], 1)
                e.drain().then_inc(S["dv"], 1)
            ev("vector", f_do)
            gB = ("free", bump("free"))
            n_o = bump("dv")
            x_gate[buf] = [("dv", n_o)]

            osem = "so0" if buf == 0 else "so1"

            def f_so(e, b=b, tau=tau, buf=buf, n=n_o, osem=osem):
                e.wait_ge(S["dv"], n)
                dst = bass.AP(y_h.tensor, (b * seq + tau * 128) * C,
                              [[C, 128], [1, C]])
                e.dma_start(out=dst, in_=out_sb[buf][:]).then_inc(S[osem], 16)
            ev("sync", f_so)
            out_gate[buf] = (osem, bump(osem, 16))

    stages["all"] = len(events)
    import os as _os
    if _os.environ.get("PRINT_STAGES"):
        print("STAGES:", stages)
    if stop_stage is not None:
        cut = stop_stage if isinstance(stop_stage, int) else stages[stop_stage]
        del events[cut:]
    for name in ("vector", "scalar", "tensor", "gpsimd"):
        def f_fin(e):
            e.drain().then_inc(S["done"], 1)
        ev(name, f_fin)
    n_done = bump("done", 4)
    final_counts = dict(cnt)

    def f_fins(e):
        if stop_stage is None:
            for nm in ("so0", "so1", "sxd", "sq", "sodd"):
                e.wait_ge(S[nm], final_counts[nm])
        e.wait_ge(S["done"], n_done)
        if debug_tail:
            ntail = 0
            fslots = [xn, qa, out_sb[0], out_sb[1], xtile[0], xtile[1]]
            for i, t in enumerate(fslots):
                e.dma_start(out=bass.AP(tailf_h.tensor, i * 128 * 1024,
                                        [[1024, 128], [1, 1024]]),
                            in_=t[:]).then_inc(S["wld"], 16)
                ntail += 16
            for j, nm in enumerate(("sx", "sxx", "mneg", "m2", "u", "sig", "rstd")):
                e.dma_start(out=bass.AP(tailf_h.tensor, 6 * 128 * 1024 + j,
                                        [[1024, 128], [1, 1]]),
                            in_=ln1[nm][:, 0:1],
                            allow_slow_non_contiguous=True).then_inc(S["wld"], 16)
                ntail += 16
            for j, t in enumerate((gmax, s0, rcp, rs_t, ss_t)):
                e.dma_start(out=bass.AP(tailf_h.tensor,
                                        6 * 128 * 1024 + 32 + j * 32,
                                        [[1024, 128], [1, 32]]),
                            in_=t[:, 0:32],
                            allow_slow_non_contiguous=True).then_inc(S["wld"], 16)
                ntail += 16
            e.dma_start(out=bass.AP(tailf_h.tensor, 7 * 128 * 1024,
                                    [[1024, 128], [1, 256]]),
                        in_=mtile2[:],
                        allow_slow_non_contiguous=True).then_inc(S["wld"], 16)
            ntail += 16
            bslots = [state_sb, ks_sb["k0"], ks_sb["v0"], ks_sb["k1"],
                      ks_sb["v1"], None, None, None, None, qstage[0],
                      xdstage[0], sq_sb, ln_sb, None, None]
            for i, t in enumerate(bslots):
                if t is None:
                    continue
                e.dma_start(out=bass.AP(tailb_h.tensor, i * 128 * 1024,
                                        [[1024, 128], [1, 1024]]),
                            in_=t[:]).then_inc(S["wld"], 16)
                ntail += 16
            for j in range(4):
                e.dma_start(out=bass.AP(tailb_h.tensor, (5 + j) * 128 * 1024,
                                        [[1024, 128], [1, 1024]]),
                            in_=xnT[:, j * 1024:(j + 1) * 1024]).then_inc(S["wld"], 16)
                ntail += 16
            e.dma_start(out=bass.AP(tailb_h.tensor, 13 * 128 * 1024,
                                    [[1024, 128], [1, 1024]]),
                        in_=slab["wdt"][:, 0:1024]).then_inc(S["wld"], 16)
            ntail += 16
            e.dma_start(out=bass.AP(tailb_h.tensor, 14 * 128 * 1024,
                                    [[1024, 64], [1, 512]]),
                        in_=oddst[:],
                        allow_slow_non_contiguous=True).then_inc(S["wld"], 16)
            ntail += 16
            for tt in range(nt):
                e.dma_start(out=bass.AP(tailq_h.tensor, tt * 128 * 1024,
                                        [[1024, 128], [1, 1024]]),
                            in_=qscr[tt, :, :]).then_inc(S["wld"], 16)
                e.dma_start(out=bass.AP(tailq_h.tensor,
                                        (nt + tt) * 128 * 1024,
                                        [[1024, 128], [1, 1024]]),
                            in_=xdscr[tt, :, :]).then_inc(S["wld"], 16)
                ntail += 32
            e.wait_ge(S["wld"], final_counts["wld"] + ntail)
        for s in S.values():
            e.sem_clear(s)
    ev("sync", f_fins)

    with nc.Block() as block:
        @block.sync
        def _(eng):
            for name, fn in events:
                if name == "sync":
                    fn(eng)

        @block.vector
        def _(eng):
            for name, fn in events:
                if name == "vector":
                    fn(eng)

        @block.scalar
        def _(eng):
            for name, fn in events:
                if name == "scalar":
                    fn(eng)

        @block.tensor
        def _(eng):
            for name, fn in events:
                if name == "tensor":
                    fn(eng)

        @block.gpsimd
        def _(eng):
            for name, fn in events:
                if name == "gpsimd":
                    fn(eng)

    stk.close()
    return nc


def make_base_inputs(Wq, bq, Wk, bk, Wv, bv, Wo, bo, ln_g, ln_b, Wffn, bffn):
    brows = np.stack([
        np.asarray(bq, np.float32), np.asarray(bk[0], np.float32),
        np.asarray(bk[1], np.float32), np.asarray(bv[0], np.float32),
        np.asarray(bv[1], np.float32), np.asarray(bo, np.float32),
        np.asarray(bffn, np.float32), np.asarray(ln_b, np.float32),
    ]).astype(np.float32)
    return {
        "wq": np.ascontiguousarray(np.asarray(Wq, np.float32)),
        "wk0": np.ascontiguousarray(np.asarray(Wk[0], np.float32)),
        "wk1": np.ascontiguousarray(np.asarray(Wk[1], np.float32)),
        "wv0": np.ascontiguousarray(np.asarray(Wv[0], np.float32)),
        "wv1": np.ascontiguousarray(np.asarray(Wv[1], np.float32)),
        "wo": np.ascontiguousarray(np.asarray(Wo, np.float32)),
        "wffn": np.ascontiguousarray(np.asarray(Wffn, np.float32)),
        "brows": brows,
        "bq_fm": np.ascontiguousarray(np.asarray(bq, np.float32).reshape(8, 128).T),
        "g_fm": np.ascontiguousarray(np.asarray(ln_g, np.float32).reshape(8, 128).T),
        "b_colb": np.ascontiguousarray(
            np.asarray(ln_b, np.float32).reshape(8, 128).T),
        "eyef": np.eye(128, dtype=np.float32),
        "eyeb": np.eye(128, dtype=BF),
        "onesb": np.ones((3, 128), dtype=BF),
        "onesf": np.ones((1, 128), dtype=np.float32),
        "onescol": np.ones((128, 1), dtype=BF),
        "csts": np.tile(np.array([7.5, 1e-30, 8.0, 0.125, 1.0 / 1024,
                                  1.0 / 64, 1e-5, 0],
                                 dtype=np.float32), (128, 1)),
    }


def kernel(L0_attn_x, L0_attn_Wq, L0_attn_bq, L0_attn_Wk, L0_attn_bk,
           L0_attn_Wv, L0_attn_bv, L0_attn_Wo, L0_attn_bo,
           L0_attn_ln_g, L0_attn_ln_b, L0_ffn_weight, L0_ffn_bias):
    x = np.ascontiguousarray(np.asarray(L0_attn_x, dtype=np.float32))
    B = x.shape[0]
    ncores = 8
    per = B // ncores
    nc = build_program(nb=per, nt=NT)
    base = make_base_inputs(L0_attn_Wq, L0_attn_bq, L0_attn_Wk, L0_attn_bk,
                            L0_attn_Wv, L0_attn_bv, L0_attn_Wo, L0_attn_bo,
                            L0_attn_ln_g, L0_attn_ln_b, L0_ffn_weight,
                            L0_ffn_bias)
    in_maps = []
    for c in range(ncores):
        m = dict(base)
        m["x"] = np.ascontiguousarray(x[c * per:(c + 1) * per])
        in_maps.append(m)
    res = run_bass_kernel_spmd(nc, in_maps, core_ids=list(range(ncores)))
    out = np.concatenate([np.asarray(r["y"]) for r in res.results], axis=0)
    return out.astype(np.float32)
